# version 06
# baseline (speedup 1.0000x reference)
import os
import numpy as np
from contextlib import ExitStack

import concourse.bass as bass
import concourse.mybir as mybir
from concourse.bass_utils import run_bass_kernel_spmd
from concourse.bass_core_rust import IndirectOffsetOnAxis

F32 = mybir.dt.float32
I32 = mybir.dt.int32
ALU = mybir.AluOpType
AX = mybir.AxisListType
ACT = mybir.ActivationFunctionType

NCORES = 8
D = 1024
DC = 8
L = 2048
NSEQ = 4
TPC = NSEQ * L
NT = TPC // 128
STOK = NCORES * TPC
NGRP = STOK // 512
GPR = NGRP // NCORES
E = 8
CL = 2560
CCAP = 20480
TRASH = E * CL
NHEAD = 4

ENGS = ("sync", "tensor", "vector", "scalar", "gpsimd")


class Sem:
    def __init__(self, handle):
        self.h = handle
        self.n = 0


class Walk:
    def __init__(self):
        self.ops = {k: [] for k in ENGS}
        self.self_sems = {}

    def do(self, eng, fn, inc=None):
        ss = self.self_sems.get(eng)
        if ss is not None:
            self.ops[eng].append([lambda e, h=ss.h, v=ss.n: e.wait_ge(h, v),
                                  None, True])
            if inc is None:
                inc = (ss, 1)
        if inc is not None:
            inc[0].n += inc[1]
        self.ops[eng].append([fn, inc, False])

    def wait(self, eng, sem, val=None):
        v = sem.n if val is None else val
        self.ops[eng].append([lambda e, h=sem.h, v=v: e.wait_ge(h, v), None, True])

    def mark(self, eng, sem, k=1):
        for op in reversed(self.ops[eng]):
            if not op[2]:
                if op[1] is None:
                    op[1] = (sem, k)
                    sem.n += k
                    return True
                if op[1][0] is sem:
                    return False
                raise AssertionError(f"mark({eng}): last op has other-sem inc")
        raise AssertionError(f"mark({eng}): no real op to mark")

    def emit(self, block):
        for eng in ENGS:
            ops = self.ops[eng]

            def mk(ops):
                def f(e):
                    for fn, inc, _ in ops:
                        r = fn(e)
                        if inc is not None:
                            r.then_inc(inc[0].h, inc[1])
                return f
            getattr(block, eng)(mk(ops))


class Arena:
    def __init__(self, t, words):
        self.t = t
        self.words = words
        self.off = 0

    def alloc(self, w_, align=16):
        o = (self.off + align - 1) // align * align
        assert o + w_ <= self.words, f"arena overflow {o}+{w_}>{self.words}"
        self.off = o + w_
        return self.t[:, o:o + w_]

    def mark(self):
        return self.off

    def reset(self, m):
        self.off = m


def build_nc():
    PH = int(os.environ.get("K_PHASES", "9"))
    P1S = int(os.environ.get("K_P1SUB", "9"))
    P4S = int(os.environ.get("K_P4SUB", "9"))
    P4K = int(os.environ.get("K_P4K", "9"))
    KSSQ = int(os.environ.get("K_SSQ", "1"))
    KEVAC = os.environ.get("K_EVAC", "mix")
    nc = bass.Bass("TRN2", target_bir_lowering=False, debug=False,
                   detect_race_conditions=False)

    def di(name, shape, dt=F32, kind="ExternalInput"):
        return nc.dram_tensor(name, list(shape), dt, kind=kind).ap()

    x_full = di("x_full", [STOK, D])
    norm_w = di("norm_w", [1, D])
    wq_d = di("wq", [D, D])
    wk_d = di("wk", [D, 512])
    wv_d = di("wv", [D, 512])
    wo_d = di("wo", [D, D])
    gate_in = di("gate_in", [1, 1])
    ltemp_in = di("ltemp_in", [1, 1])
    wg_ext = di("wg_ext", [D, 9])
    w_in_d = di("w_in", [E * D, 4096])
    w_out_d = di("w_out", [E * 2048, D])
    c_ident = di("c_ident", [128, 128])
    c_tri = di("c_tri", [128, 128])
    c_ones = di("c_ones", [128, 128])
    c_iotar = di("c_iotar", [128, 32])
    c_maskd = di("c_maskd", [128, 2048])
    c_coremask = di("c_coremask", [1, 128])
    c_mymask = di("c_mymask", [1, 128])
    out_d = di("out", [TPC, D], kind="ExternalOutput")
    DBG = "ExternalOutput" if os.environ.get("K_DEBUG_OUT") else "Internal"
    expbuf = di("expbuf", [TRASH + 128, D], kind=DBG)
    outbuf = di("outbuf", [TRASH, D], kind=DBG)
    s_stage = di("s_stage", [TPC, D], kind=DBG)
    dbg_arena = di("dbg_arena", [128, 7424], kind=DBG) if DBG == "ExternalOutput" else None
    dbg_i = di("dbg_i", [128, 512], I32, kind=DBG) if DBG == "ExternalOutput" else None
    h_stage = di("h_stage", [16 * D, 512], kind="Internal")
    bounce = di("bounce", [64, 16], kind="Internal")

    es = ExitStack()
    w = Walk()
    with es:
        AP_ = es.enter_context(nc.sbuf_tensor("arena_p", [128, 7424], F32))
        AF_ = es.enter_context(nc.sbuf_tensor("arena_f", [128, 45000], F32))
        AI_ = es.enter_context(nc.sbuf_tensor("arena_i", [128, 512], I32))
        P = Arena(AP_, 7424)
        A = Arena(AF_, 45000)
        AI = Arena(AI_, 512)
        pb = [es.enter_context(nc.psum_tensor(f"pb{i}", [128, 512], F32))
              for i in range(8)]
        s_ld = Sem(es.enter_context(nc.semaphore("s_ld")))
        s_gp = Sem(es.enter_context(nc.semaphore("s_gp")))
        s_pe = Sem(es.enter_context(nc.semaphore("s_pe")))
        s_ve = Sem(es.enter_context(nc.semaphore("s_ve")))
        s_ac = Sem(es.enter_context(nc.semaphore("s_ac")))
        s_gb = Sem(es.enter_context(nc.semaphore("s_gb")))
        ALLS = (s_ld, s_gp, s_pe, s_ve, s_ac, s_gb)
        w.self_sems = {"vector": s_ve, "scalar": s_ac}
        block = es.enter_context(nc.Block())

        ident = P.alloc(128)
        tri = P.alloc(128)
        ones = P.alloc(128)
        iotar = P.alloc(32)
        maskd = P.alloc(2048)
        wgext = P.alloc(80)
        nwrep = P.alloc(D)
        cwrep = P.alloc(32)
        grep = P.alloc(16)
        cm_row = P.alloc(128)
        mm_row = P.alloc(128)
        cnts8 = P.alloc(128)
        base_row = P.alloc(16)
        k2_row = P.alloc(16)
        negbuf = P.alloc(512)
        ohbuf = P.alloc(1024)
        ecat = P.alloc(128)
        e12 = P.alloc(128)
        mbuf = P.alloc(64)
        ssqbuf = P.alloc(64)
        rbuf = P.alloc(64)
        gbuf = P.alloc(128)
        trow = P.alloc(32)
        physS = AI.alloc(128)
        physG = AI.alloc(128)
        assert P.off <= 7424

        def W8(v):
            return v.rearrange("p (t e) -> p t e", e=8)

        def W16(v):
            return v.rearrange("p (t e) -> p t e", e=16)

        def bc(v, shape):
            return v[:, :, None].to_broadcast(shape)

        def V(fn):
            w.do("vector", fn, (s_ve, 1))

        def S(fn):
            w.do("scalar", fn, (s_ac, 1))

        def barrier():
            w.mark("tensor", s_pe, 1)
            w.mark("vector", s_ve, 1)
            w.mark("scalar", s_ac, 1)
            w.mark("gpsimd", s_gp, 16)
            tgt = [s.n for s in ALLS]
            for eng in ENGS:
                for s, v in zip(ALLS, tgt):
                    w.wait(eng, s, v)

        tr_evac = {"k": 0}

        def transp8(src, dstv, sub, banks, evac_engines=("vector", "scalar")):
            for st_ in range(2):
                bank = banks[st_]
                for j in range(4):
                    fc = st_ * 4 + j
                    w.do("tensor", lambda e, bank=bank, j=j, fc=fc, src=src:
                         e.transpose(pb[bank][:, j * 128:(j + 1) * 128],
                                     src[:, fc * 128:(fc + 1) * 128], ident))
                w.mark("tensor", s_pe, 1)
                eng = evac_engines[st_ % len(evac_engines)]
                sem = s_ve if eng == "vector" else s_ac
                w.wait(eng, s_pe)
                if eng == "vector":
                    w.do(eng, lambda e, bank=bank, st_=st_, sub=sub, dstv=dstv:
                         e.tensor_copy(
                             dstv[:, st_ * 4:st_ * 4 + 4, sub * 128:(sub + 1) * 128],
                             pb[bank].rearrange("p (j n) -> p j n", n=128)), (sem, 1))
                else:
                    w.do(eng, lambda e, bank=bank, st_=st_, sub=sub, dstv=dstv:
                         e.copy(
                             dstv[:, st_ * 4:st_ * 4 + 4, sub * 128:(sub + 1) * 128],
                             pb[bank].rearrange("p (j n) -> p j n", n=128)), (sem, 1))
                yield (eng, (s_ve.n if eng == "vector" else s_ac.n))

        for dst, src in ((ident, c_ident), (tri, c_tri), (ones, c_ones),
                         (iotar, c_iotar), (maskd, c_maskd)):
            w.do("sync", lambda e, d=dst, s=src: e.dma_start(out=d, in_=s), (s_ld, 16))
        w.do("sync", lambda e: e.dma_start(out=cm_row[0:1, :], in_=c_coremask), (s_ld, 16))
        w.do("sync", lambda e: e.dma_start(out=mm_row[0:1, :], in_=c_mymask), (s_ld, 16))
        w.do("sync", lambda e: e.dma_start(
            out=wgext[:, 0:72].rearrange("p (c n) -> p c n", n=9),
            in_=wg_ext.rearrange("(c p) n -> p c n", p=128)), (s_ld, 16))
        w.do("sync", lambda e: e.dma_start(out=grep[0:1, 0:1], in_=gate_in), (s_ld, 16))
        w.do("sync", lambda e: e.dma_start(out=grep[0:1, 1:2], in_=ltemp_in), (s_ld, 16))
        zt = P.alloc(D)
        p0m = A.mark()
        nw_row = A.alloc(D)
        cw_row = A.alloc(32)
        w.do("sync", lambda e: e.dma_start(out=nw_row[0:1, :], in_=norm_w), (s_ld, 16))
        ld0 = s_ld.n
        w.wait("scalar", s_ld, ld0)
        S(lambda e: e.activation(grep[0:1, 1:2], grep[0:1, 1:2], ACT.Exp))
        w.wait("tensor", s_ld, ld0)
        w.wait("tensor", s_ac)
        w.do("tensor", lambda e: e.matmul(pb[0][:, 0:2], ones[0:1, :], grep[0:1, 0:2],
                                          start=True, stop=True))
        w.do("tensor", lambda e: e.matmul(pb[1][:, 0:512], ones[0:1, :],
                                          nw_row[0:1, 0:512], start=True, stop=True))
        w.do("tensor", lambda e: e.matmul(pb[2][:, 0:512], ones[0:1, :],
                                          nw_row[0:1, 512:1024], start=True, stop=True))
        for fc in range(DC):
            w.do("tensor", lambda e, fc=fc: e.matmul(
                pb[3][0:1, 0:8], ones[:, 0:1], wgext[:, fc * 9:fc * 9 + 8],
                start=(fc == 0), stop=(fc == DC - 1)),
                (s_pe, 1) if fc == DC - 1 else None)
        w.wait("vector", s_pe)
        V(lambda e: e.tensor_copy(grep[:, 2:3], pb[0][:, 0:1]))
        V(lambda e: e.tensor_copy(grep[:, 3:4], pb[0][:, 1:2]))
        V(lambda e: e.tensor_copy(nwrep[:, 0:512], pb[1][:, 0:512]))
        V(lambda e: e.tensor_copy(nwrep[:, 512:1024], pb[2][:, 0:512]))
        V(lambda e: e.tensor_copy(
            cw_row[0:1, :].rearrange("p (t n) -> p t n", n=8),
            pb[3][0:1, 0:8].rearrange("p (o n) -> p o n", o=1).to_broadcast([1, 4, 8])))
        V(lambda e: e.memset(zt[:], 0.0))
        ceps = P.alloc(16)
        V(lambda e: e.memset(ceps[:, 0:1], 1e-5))
        V(lambda e: e.memset(ceps[:, 1:2], 1e-6))
        V(lambda e: e.memset(ceps[:, 2:3], 1.0 / 1024))
        nc.const_aps.register(F32, 1e-5, ceps[:, 0:1])
        nc.const_aps.register(F32, 1e-6, ceps[:, 1:2])
        nc.const_aps.register(F32, 1.0 / 1024, ceps[:, 2:3])
        ve_cw = s_ve.n
        w.wait("tensor", s_ve, ve_cw)
        w.do("tensor", lambda e: e.matmul(pb[4][:, 0:32], ones[0:1, :], cw_row[0:1, :],
                                          start=True, stop=True), (s_pe, 1))
        w.wait("vector", s_pe)
        V(lambda e: e.tensor_copy(cwrep[:], pb[4][:, 0:32]))
        ve_consts = s_ve.n
        gate_ap = grep[:, 2:3]
        temp_ap = grep[:, 3:4]
        w.wait("gpsimd", s_ve, ve_consts)
        for i in range((TRASH + 128) // 128):
            w.do("gpsimd", lambda e, i=i: e.dma_start(
                out=expbuf[i * 128:(i + 1) * 128, :], in_=zt[:]), (s_gp, 16))
        zf_done = s_gp.n

        if PH < 1:
            w.wait("gpsimd", s_gp, s_gp.n)
            w.emit(block)
            return nc
        A.reset(p0m)
        p1m = A.mark()
        xg = [A.alloc(4096), A.alloc(4096)]
        xfm = A.alloc(4096)
        rawg = A.alloc(512)
        rawtm = A.alloc(48)
        sqd = A.alloc(1024)
        tA = A.alloc(32)
        tB = A.alloc(32)
        tC = A.alloc(32)
        mg = A.alloc(16)
        fng = A.alloc(32)
        fog = A.alloc(64)
        feg = A.alloc(16)
        CNT = 6
        ld_grp = [None] * NGRP
        grp_free = [None] * NGRP
        evac_hist = []

        def p1_load(g):
            if grp_free[g - 2] is not None if g >= 2 else False:
                sem, val = grp_free[g - 2]
                w.wait("sync", sem, val)
            if g >= 2 and g - 2 < GPR:
                w.wait("sync", s_ac, ac_own[g - 2])
            w.do("sync", lambda e, g=g: e.dma_start(
                out=xg[g % 2].rearrange("p (s d) -> p s d", d=D),
                in_=x_full[g * 512:(g + 1) * 512, :].rearrange("(s p) d -> p s d", p=128)),
                (s_ld, 16))
            ld_grp[g] = s_ld.n

        ac_own = [0] * NGRP
        xfv = xfm.rearrange("p (c n) -> p c n", n=512)
        w.wait("sync", s_pe, s_pe.n)
        w.wait("sync", s_ve, s_ve.n)
        for g in range(NGRP):
            own = g < GPR
            if g < 2:
                p1_load(g)
            xgv = xg[g % 2].rearrange("p (s d) -> p s d", d=D)
            w.wait("tensor", s_ld, ld_grp[g])
            for h in range(8 if P1S >= 2 else 0):
                sub, st_ = h // 2, h % 2
                bank = h % 4
                k = len(evac_hist)
                if k >= 4:
                    sem, val = evac_hist[k - 4]
                    w.wait("tensor", sem, val)
                for j in range(4):
                    fc = st_ * 4 + j
                    w.do("tensor", lambda e, bank=bank, j=j, sub=sub, fc=fc, xgv=xgv:
                         e.transpose(pb[bank][:, j * 128:(j + 1) * 128],
                                     xgv[:, sub, fc * 128:(fc + 1) * 128], ident))
                w.mark("tensor", s_pe, 1)
                if h % 2 == 0 or KEVAC == "v":
                    w.wait("vector", s_pe)
                    V(lambda e, bank=bank, st_=st_, sub=sub:
                      e.tensor_copy(xfv[:, st_ * 4:st_ * 4 + 4, sub * 128:(sub + 1) * 128],
                                    pb[bank].rearrange("p (j n) -> p j n", n=128)))
                    evac_hist.append((s_ve, s_ve.n))
                else:
                    w.wait("scalar", s_pe)
                    S(lambda e, bank=bank, st_=st_, sub=sub:
                      e.copy(xfv[:, st_ * 4:st_ * 4 + 4, sub * 128:(sub + 1) * 128],
                             pb[bank].rearrange("p (j n) -> p j n", n=128)))
                    evac_hist.append((s_ac, s_ac.n))
            pe_tr = s_pe.n
            if own and P1S >= 2 and KSSQ:
                w.wait("scalar", s_ld, ld_grp[g])
                for sub in range(4):
                    t = g * 4 + sub
                    S(lambda e, sub=sub, t=t, xgv=xgv:
                      e.activation(sqd[:], xgv[:, sub, :], ACT.Square,
                                   accum_out=ssqbuf[:, t:t + 1]))
            ac_own[g] = s_ac.n
            grp_free[g] = (s_pe, pe_tr) if P1S >= 2 else None
            if g + 2 < NGRP:
                p1_load(g + 2)
            if P1S < 3:
                continue
            for kk in range(8):
                sem, val = evac_hist[len(evac_hist) - 8 + kk]
                w.wait("tensor", sem, val)
            for fc in range(DC):
                w.do("tensor", lambda e, fc=fc: e.matmul(
                    pb[4][0:9, :], wgext[:, fc * 9:(fc + 1) * 9], xfv[:, fc, :],
                    start=(fc == 0), stop=(fc == DC - 1)))
            w.mark("tensor", s_pe, 1)
            w.wait("vector", s_pe)
            V(lambda e: e.tensor_copy(rawg[0:9, :], pb[4][0:9, :]))
            if P1S < 4:
                continue
            w.wait("tensor", s_ve)
            for j in range(4):
                w.do("tensor", lambda e, j=j: e.transpose(
                    pb[5][:, j * 9:(j + 1) * 9], rawg[0:9, j * 128:(j + 1) * 128],
                    ident[0:9, 0:9]))
            w.mark("tensor", s_pe, 1)
            w.wait("vector", s_pe)
            V(lambda e: e.tensor_copy(rawtm[:, 0:36], pb[5][:, 0:36]))
            r9 = rawtm[:, 0:36].rearrange("p (t n) -> p t n", n=9)
            raw8 = r9[:, :, 0:8]
            mcol = r9[:, :, 8:9]
            negs = negbuf[:, g * 32:(g + 1) * 32] if own else fng
            ohg = ohbuf[:, g * 64:(g + 1) * 64] if own else fog
            if own:
                e1d = e12[:, g * 4:(g + 1) * 4]
                e2d = e12[:, 64 + g * 4:64 + (g + 1) * 4]
            else:
                e1d = feg[:, 0:4]
                e2d = feg[:, 4:8]
            if own:
                V(lambda e, g=g, r9=r9: e.tensor_copy(
                    mbuf[:, g * 4:(g + 1) * 4],
                    r9[:, :, 8:9].rearrange("p t o -> p (t o)")))
            V(lambda e, mcol=mcol: e.tensor_tensor(
                W8(tA), W8(cwrep), mcol.to_broadcast([128, 4, 8]), op=ALU.mult))
            V(lambda e, raw8=raw8, negs=negs: e.tensor_tensor(
                W8(negs), W8(tA), raw8, op=ALU.subtract))
            V(lambda e, negs=negs: e.tensor_reduce(
                mg[:, 0:4], W8(negs), axis=AX.X, op=ALU.min))
            V(lambda e, negs=negs: e.tensor_tensor(
                W8(tB), W8(negs), bc(mg[:, 0:4], [128, 4, 8]), op=ALU.is_equal))
            V(lambda e: e.tensor_tensor(W8(tC), W8(tB), W8(iotar), op=ALU.mult))
            V(lambda e, e1d=e1d: e.tensor_reduce(e1d, W8(tC), axis=AX.X, op=ALU.max))
            oh1 = W16(ohg)[:, :, 0:8]
            oh2 = W16(ohg)[:, :, 8:16]
            V(lambda e, oh1=oh1, e1d=e1d: e.tensor_tensor(
                oh1, W8(iotar), e1d[:, :, None].to_broadcast([128, 4, 8]),
                op=ALU.is_equal))
            V(lambda e, oh1=oh1, negs=negs: e.scalar_tensor_tensor(
                W8(tA), oh1, 1e30, W8(negs), op0=ALU.mult, op1=ALU.add))
            V(lambda e: e.tensor_reduce(mg[:, 4:8], W8(tA), axis=AX.X, op=ALU.min))
            V(lambda e: e.tensor_tensor(
                W8(tB), W8(tA), bc(mg[:, 4:8], [128, 4, 8]), op=ALU.is_equal))
            V(lambda e: e.tensor_tensor(W8(tC), W8(tB), W8(iotar), op=ALU.mult))
            V(lambda e, e2d=e2d: e.tensor_reduce(e2d, W8(tC), axis=AX.X, op=ALU.max))
            V(lambda e, oh2=oh2, e2d=e2d: e.tensor_tensor(
                oh2, W8(iotar), e2d[:, :, None].to_broadcast([128, 4, 8]),
                op=ALU.is_equal))
            ve_oh = s_ve.n
            if P1S < 6:
                continue
            r = g // GPR
            w.wait("tensor", s_ve, ve_oh)
            w.do("tensor", lambda e, r=r, g=g, ohg=ohg: e.matmul(
                pb[CNT][0:1, r * 64:(r + 1) * 64], ones[:, 0:1], ohg,
                start=(g % GPR == 0), stop=(g % GPR == GPR - 1)), (s_pe, 1))
            if not own:
                w.wait("vector", s_pe)
        p1_pe = s_pe.n
        w.wait("vector", s_pe, p1_pe)
        if P1S >= 6:
            V(lambda e: e.tensor_reduce(
                cnts8[0:1, :].rearrange("p (r ke) -> p r ke", ke=16),
                pb[CNT][0:1, :].rearrange("p (r t ke) -> p r t ke", t=4, ke=16)
                .transpose([0, 1, 3, 2]),
                axis=AX.X, op=ALU.add))
        barrier()

        if PH < 2:
            if dbg_arena is not None:
                w.do("gpsimd", lambda e: e.dma_start(out=dbg_arena, in_=AP_[:]), (s_gp, 16))
                w.do("gpsimd", lambda e: e.dma_start(out=dbg_i, in_=AI_[:]), (s_gp, 16))
            w.wait("gpsimd", s_gp, s_gp.n)
            w.emit(block)
            return nc
        A.reset(p1m)
        p2m = A.mark()
        lbuf = A.alloc(512)
        pbuf = A.alloc(512)
        qbuf = A.alloc(512)
        s64a = A.alloc(64)
        s64b = A.alloc(64)
        s64c = A.alloc(64)
        locG = A.alloc(1024)
        locB = A.alloc(1024)
        selG = A.alloc(128)
        selB = A.alloc(128)
        kbuf = A.alloc(128)
        kBbuf = A.alloc(128)
        t128a = A.alloc(128)
        t128b = A.alloc(128)
        trash_c = A.alloc(128)
        tot64 = A.alloc(16)
        tprow = A.alloc(1024)

        V(lambda e: e.tensor_tensor(t128a[0:1, :], cnts8[0:1, :], cm_row[0:1, :], op=ALU.mult))
        V(lambda e: e.tensor_reduce(
            base_row[0:1, :].rearrange("p (o ke) -> p o ke", o=1),
            t128a[0:1, :].rearrange("p (r ke) -> p r ke", ke=16).transpose([0, 2, 1]),
            axis=AX.X, op=ALU.add))
        V(lambda e: e.tensor_reduce(
            trow[0:1, 0:16].rearrange("p (o ke) -> p o ke", o=1),
            cnts8[0:1, :].rearrange("p (r ke) -> p r ke", ke=16).transpose([0, 2, 1]),
            axis=AX.X, op=ALU.add))
        V(lambda e: e.tensor_tensor(base_row[0:1, 8:16], base_row[0:1, 8:16],
                                    trow[0:1, 0:8], op=ALU.add))
        V(lambda e: e.tensor_tensor(t128a[0:1, :], cnts8[0:1, :], mm_row[0:1, :], op=ALU.mult))
        V(lambda e: e.tensor_reduce(
            trow[0:1, 16:32].rearrange("p (o ke) -> p o ke", o=1),
            t128a[0:1, :].rearrange("p (r ke) -> p r ke", ke=16).transpose([0, 2, 1]),
            axis=AX.X, op=ALU.add))
        V(lambda e: e.memset(k2_row[0:1, 0:8], 0.0))
        V(lambda e: e.tensor_copy(k2_row[0:1, 8:16], trow[0:1, 16:24]))
        ve_bases = s_ve.n
        w.wait("tensor", s_ve, ve_bases)
        for t in range(NT):
            b = t // 32
            w.do("tensor", lambda e, b=b, t=t: e.matmul(
                pb[b][0:1, (t % 32) * 16:(t % 32) * 16 + 16], ones[:, 0:1],
                ohbuf[:, t * 16:(t + 1) * 16],
                start=(t % 32 == 0), stop=(t % 32 == 31)),
                (s_pe, 1) if t % 32 == 31 else None)
        w.wait("vector", s_pe)
        V(lambda e: e.tensor_copy(tprow[0:1, 0:512], pb[0][0:1, :]))
        V(lambda e: e.tensor_copy(tprow[0:1, 512:1024], pb[1][0:1, :]))
        ve_tot = s_ve.n
        w.wait("gpsimd", s_ve, ve_tot)
        w.do("gpsimd", lambda e: e.dma_start(out=bounce, in_=tprow[0:1, :]), (s_gp, 16))
        w.wait("gpsimd", s_gp)
        w.do("gpsimd", lambda e: e.dma_start(out=tot64[0:64, :], in_=bounce), (s_gp, 16))
        gp_t64 = s_gp.n
        w.wait("tensor", s_gp, gp_t64)
        w.do("tensor", lambda e: e.matmul(pb[2][0:64, 0:16], tri[0:64, 0:64],
                                          tot64[0:64, :], start=True, stop=True), (s_pe, 1))
        w.wait("vector", s_pe)
        V(lambda e: e.tensor_copy(tot64[0:64, :], pb[2][0:64, 0:16]))
        ve_tp = s_ve.n
        w.wait("gpsimd", s_ve, ve_tp)
        w.do("gpsimd", lambda e: e.dma_start(out=bounce, in_=tot64[0:64, :]), (s_gp, 16))
        w.wait("gpsimd", s_gp)
        w.do("gpsimd", lambda e: e.dma_start(out=tprow[0:1, :], in_=bounce), (s_gp, 16))
        gp_tp = s_gp.n
        w.wait("tensor", s_gp, gp_tp)
        ve_loc = {}
        for t in range(NT):
            ba = (t % 2) * 2
            if t >= 2:
                w.wait("tensor", s_ve, ve_loc[t - 2])
            for bx, krow in ((ba, base_row), (ba + 1, k2_row)):
                w.do("tensor", lambda e, bx=bx, t=t: e.matmul(
                    pb[bx][:, 0:16], tri, ohbuf[:, t * 16:(t + 1) * 16],
                    start=True, stop=False))
                w.do("tensor", lambda e, bx=bx, t=t: e.matmul(
                    pb[bx][:, 0:16], ones[0:1, :], tprow[0:1, t * 16:(t + 1) * 16],
                    start=False, stop=False))
                w.do("tensor", lambda e, bx=bx, krow=krow: e.matmul(
                    pb[bx][:, 0:16], ones[0:1, :], krow[0:1, :],
                    start=False, stop=True),
                    (s_pe, 1) if bx == ba + 1 else None)
            w.wait("vector", s_pe)
            V(lambda e, ba=ba, t=t: e.tensor_copy(locG[:, t * 16:(t + 1) * 16],
                                                  pb[ba][:, 0:16]))
            V(lambda e, ba=ba, t=t: e.tensor_copy(locB[:, t * 16:(t + 1) * 16],
                                                  pb[ba + 1][:, 0:16]))
            ve_loc[t] = s_ve.n
        V(lambda e: e.tensor_tensor(s64a[:, :], mbuf[:, :], mbuf[:, :], op=ALU.mult))
        V(lambda e: e.tensor_scalar(s64b[:, :], ssqbuf[:, :], 1.0 / 1024, scalar2=None,
                                    op0=ALU.mult))
        V(lambda e: e.tensor_tensor(s64a[:, :], s64b[:, :], s64a[:, :], op=ALU.subtract))
        ve_var = s_ve.n
        w.wait("scalar", s_ve, ve_var)
        S(lambda e: e.activation(s64b[:, :], s64a[:, :], ACT.Sqrt, bias=1e-5))
        w.wait("vector", s_ac)
        V(lambda e: e.reciprocal(rbuf[:, :], s64b[:, :]))
        V(lambda e: e.tensor_scalar(s64c[:, :], rbuf[:, :], -1.0, scalar2=None,
                                    op0=ALU.mult))
        V(lambda e: e.tensor_tensor(W8(lbuf), W8(negbuf), bc(s64c, [128, 64, 8]),
                                    op=ALU.mult))
        V(lambda e: e.tensor_reduce(s64a[:, :], W8(lbuf), axis=AX.X, op=ALU.max))
        V(lambda e: e.tensor_tensor(W8(lbuf), W8(lbuf), bc(s64a, [128, 64, 8]),
                                    op=ALU.subtract))
        ve_l = s_ve.n
        w.wait("scalar", s_ve, ve_l)
        S(lambda e: e.activation(pbuf[:, :], lbuf[:, :], ACT.Exp))
        w.wait("vector", s_ac)
        V(lambda e: e.tensor_reduce(s64a[:, :], W8(pbuf), axis=AX.X, op=ALU.add))
        V(lambda e: e.reciprocal(s64b[:, :], s64a[:, :]))
        V(lambda e: e.tensor_tensor(W8(pbuf), W8(pbuf), bc(s64b, [128, 64, 8]),
                                    op=ALU.mult))
        V(lambda e: e.tensor_tensor(W8(qbuf), W8(pbuf), W16(ohbuf)[:, :, 0:8],
                                    op=ALU.mult))
        V(lambda e: e.tensor_reduce(gbuf[:, 0:64], W8(qbuf), axis=AX.X, op=ALU.add))
        V(lambda e: e.tensor_tensor(W8(qbuf), W8(pbuf), W16(ohbuf)[:, :, 8:16],
                                    op=ALU.mult))
        V(lambda e: e.tensor_reduce(gbuf[:, 64:128], W8(qbuf), axis=AX.X, op=ALU.add))
        V(lambda e: e.tensor_tensor(locG[:, :], locG[:, :], ohbuf[:, :], op=ALU.mult))
        V(lambda e: e.tensor_reduce(selG[:, 0:64], W16(locG)[:, :, 0:8],
                                    axis=AX.X, op=ALU.add))
        V(lambda e: e.tensor_reduce(selG[:, 64:128], W16(locG)[:, :, 8:16],
                                    axis=AX.X, op=ALU.add))
        V(lambda e: e.tensor_tensor(locB[:, :], locB[:, :], ohbuf[:, :], op=ALU.mult))
        V(lambda e: e.tensor_reduce(selB[:, 0:64], W16(locB)[:, :, 0:8],
                                    axis=AX.X, op=ALU.add))
        V(lambda e: e.tensor_reduce(selB[:, 64:128], W16(locB)[:, :, 8:16],
                                    axis=AX.X, op=ALU.add))
        V(lambda e: e.tensor_scalar(kbuf[:, :], selG[:, :], float(CCAP), scalar2=None,
                                    op0=ALU.is_lt))
        V(lambda e: e.tensor_scalar(kBbuf[:, :], selB[:, :], float(CL), scalar2=None,
                                    op0=ALU.is_lt))
        V(lambda e: e.tensor_tensor(gbuf[:, :], gbuf[:, :], kbuf[:, :], op=ALU.mult))
        V(lambda e: e.tensor_tensor(s64a[:, :], gbuf[:, 0:64], gbuf[:, 64:128],
                                    op=ALU.add))
        V(lambda e: e.tensor_scalar(s64a[:, :], s64a[:, :], 1e-9, scalar2=None,
                                    op0=ALU.max))
        V(lambda e: e.reciprocal(s64b[:, :], s64a[:, :]))
        V(lambda e: e.tensor_tensor(gbuf[:, 0:64], gbuf[:, 0:64], s64b[:, :],
                                    op=ALU.mult))
        V(lambda e: e.tensor_tensor(gbuf[:, 64:128], gbuf[:, 64:128], s64b[:, :],
                                    op=ALU.mult))
        V(lambda e: e.tensor_tensor(gbuf[:, :], gbuf[:, :], kBbuf[:, :], op=ALU.mult))
        V(lambda e: e.scalar_tensor_tensor(t128a[:, :], e12[:, :], -2560.0, selB[:, :],
                                           op0=ALU.mult, op1=ALU.add))
        V(lambda e: e.tensor_scalar(t128a[:, :], t128a[:, :], 17920.0, scalar2=None,
                                    op0=ALU.add))
        V(lambda e: e.tensor_tensor(t128a[:, :], t128a[:, :], kBbuf[:, :], op=ALU.mult))
        V(lambda e: e.tensor_copy(physG[:, :], t128a[:, :]))
        V(lambda e: e.tensor_scalar(t128b[:, :], kBbuf[:, :], -1.0, scalar2=1.0,
                                    op0=ALU.mult, op1=ALU.add))
        V(lambda e: e.scalar_tensor_tensor(t128b[:, :], t128b[:, :], float(TRASH),
                                           t128a[:, :], op0=ALU.mult, op1=ALU.add))
        V(lambda e: e.tensor_copy(physS[:, :], t128b[:, :]))
        barrier()

        if PH < 3:
            w.wait("gpsimd", s_gp, s_gp.n)
            w.emit(block)
            return nc
        A.reset(p2m)
        p3m = A.mark()
        x3 = [A.alloc(1024), A.alloc(1024)]
        xn3 = [A.alloc(1024), A.alloc(1024)]
        ld3 = [None] * NT
        ve3 = [None] * NT
        gp3 = [None] * NT

        def p3_load(t):
            if t >= 2:
                w.wait("sync", s_ve, ve3[t - 2])
            w.do("sync", lambda e, t=t: e.dma_start(
                out=x3[t % 2], in_=x_full[t * 128:(t + 1) * 128, :]), (s_ld, 16))
            ld3[t] = s_ld.n

        p3_load(0)
        p3_load(1)
        for t in range(NT):
            w.wait("vector", s_ld, ld3[t])
            if t >= 2:
                w.wait("vector", s_gp, gp3[t - 2])
            V(lambda e, t=t: e.tensor_scalar(
                xn3[t % 2], x3[t % 2], mbuf[:, t:t + 1], scalar2=rbuf[:, t:t + 1],
                op0=ALU.subtract, op1=ALU.mult))
            ve3[t] = s_ve.n
            if t + 2 < NT:
                p3_load(t + 2)
            w.wait("gpsimd", s_ve, ve3[t])
            if t == 0:
                w.wait("gpsimd", s_gp, zf_done)
            for k in range(2):
                w.do("gpsimd", lambda e, t=t, k=k: e.indirect_dma_start(
                    out=expbuf,
                    out_offset=IndirectOffsetOnAxis(
                        ap=physS[:, k * 64 + t:k * 64 + t + 1], axis=0),
                    in_=xn3[t % 2], in_offset=None), (s_gp, 16))
            gp3[t] = s_gp.n
        barrier()

        if PH < 4:
            w.wait("gpsimd", s_gp, s_gp.n)
            w.emit(block)
            return nc
        A.reset(p3m)
        p4m = A.mark()
        wkv_sb = A.alloc(4096)
        wo_sb = A.alloc(4096)
        wq_sb = A.alloc(2048)
        hqb = A.alloc(4096)
        khat = A.alloc(8192)
        vext = A.alloc(8224)
        qhat = A.alloc(1024)
        nsc = A.alloc(512)
        nrm = A.alloc(512)
        attn_tm = A.alloc(4096)
        attnfm = A.alloc(4096)
        x4 = A.alloc(1024)
        s4 = A.alloc(1024)
        sc1 = A.alloc(16)
        vxv = vext.rearrange("p (t n) -> p t n", n=514)
        V(lambda e: e.memset(vxv[:, :, 256:257], 1.0))
        V(lambda e: e.memset(vxv[:, :, 513:514], 1.0))
        ve_vone = s_ve.n
        hv = hqb.rearrange("p (c n) -> p c n", n=512)
        wkv = wkv_sb.rearrange("p (c n) -> p c n", n=512)
        wvv = wkv
        wqv = wq_sb.rearrange("p (c n) -> p c n", n=256)
        wov = wo_sb.rearrange("p (c n) -> p c n", n=512)
        afv = attnfm.rearrange("p (c n) -> p c n", n=512)
        sfv = attn_tm.rearrange("p (c n) -> p c n", n=512)
        ve_x4 = 0
        pe_wq = 0
        pe_h = 0
        pe_wo = 0
        pe_wv = 0
        gp_h = 0
        for sq in range(NSEQ):
            for qb in range(NSEQ):
                tb = sq * 4 + qb
                w.wait("vector", s_gp, gp_h)
                for sub in range(4):
                    t = tb * 4 + sub
                    w.wait("sync", s_ve, ve_x4)
                    w.do("sync", lambda e, t=t: e.dma_start(
                        out=x4, in_=x_full[t * 128:(t + 1) * 128, :]), (s_ld, 16))
                    ldx = s_ld.n
                    w.wait("vector", s_ld, ldx)
                    V(lambda e, t=t: e.tensor_scalar(
                        s4, x4, mbuf[:, t:t + 1], scalar2=rbuf[:, t:t + 1],
                        op0=ALU.subtract, op1=ALU.mult))
                    ve_xn = s_ve.n
                    ve_x4 = ve_xn
                    w.wait("scalar", s_ve, ve_xn)
                    S(lambda e: e.activation(nsc[:, 0:512], s4[:, 0:512], ACT.Square,
                                             accum_out=sc1[:, 0:1]))
                    S(lambda e: e.activation(nsc[:, 0:512], s4[:, 512:1024], ACT.Square,
                                             accum_out=sc1[:, 1:2]))
                    w.wait("vector", s_ac)
                    V(lambda e: e.tensor_tensor(sc1[:, 2:3], sc1[:, 0:1], sc1[:, 1:2],
                                                op=ALU.add))
                    ve_s = s_ve.n
                    w.wait("scalar", s_ve, ve_s)
                    S(lambda e: e.activation(sc1[:, 3:4], sc1[:, 2:3], ACT.Sqrt,
                                             scale=1.0 / 1024, bias=1e-6))
                    w.wait("vector", s_ac)
                    V(lambda e: e.reciprocal(sc1[:, 4:5], sc1[:, 3:4]))
                    V(lambda e: e.tensor_scalar(s4, s4, sc1[:, 4:5], scalar2=None,
                                                op0=ALU.mult))
                    V(lambda e: e.tensor_tensor(s4, s4, nwrep, op=ALU.mult))
                    ve_h2 = s_ve.n
                    w.wait("tensor", s_ve, ve_h2)
                    for st_ in range(2):
                        bank = 6 + st_
                        for j in range(4):
                            fc = st_ * 4 + j
                            w.do("tensor", lambda e, bank=bank, j=j, fc=fc:
                                 e.transpose(pb[bank][:, j * 128:(j + 1) * 128],
                                             s4[:, fc * 128:(fc + 1) * 128], ident))
                        w.mark("tensor", s_pe, 1)
                        w.wait("vector", s_pe)
                        V(lambda e, bank=bank, st_=st_, sub=sub: e.tensor_copy(
                            hv[:, st_ * 4:st_ * 4 + 4, sub * 128:(sub + 1) * 128],
                            pb[bank].rearrange("p (j n) -> p j n", n=128)))
                        w.wait("tensor", s_ve)
                    pe_h = s_pe.n
                    w.wait("vector", s_pe, pe_h)
                ve_hd = s_ve.n
                w.wait("gpsimd", s_ve, ve_hd)
                w.do("gpsimd", lambda e, tb=tb: e.dma_start(
                    out=h_stage[tb * 1024:(tb + 1) * 1024, :]
                    .rearrange("(c p) n -> p c n", p=128),
                    in_=hv), (s_gp, 16))
                gp_h = s_gp.n
                if P4S < 2:
                    continue
                w.wait("sync", s_pe, pe_wv)
                w.do("sync", lambda e: e.dma_start(
                    out=wkv, in_=wk_d.rearrange("(c p) n -> p c n", p=128)), (s_ld, 16))
                ld_wk = s_ld.n
                w.wait("tensor", s_ld, ld_wk)
                for kvh in range(2):
                    for c in range(2):
                        kd = kvh * 2 + c
                        for fc in range(DC):
                            w.do("tensor", lambda e, kd=kd, fc=fc, c=c: e.matmul(
                                pb[c][:, :], wkv[:, fc, kd * 128:(kd + 1) * 128],
                                hv[:, fc, :], start=(fc == 0), stop=(fc == DC - 1)))
                        w.mark("tensor", s_pe, 1)
                        w.wait("vector", s_pe)
                        V(lambda e, kd=kd, qb=qb, c=c: e.tensor_copy(
                            khat[:, kd * 2048 + qb * 512: kd * 2048 + (qb + 1) * 512],
                            pb[c][:, :]))
                        if P4K >= 2:
                            w.wait("scalar", s_ve)
                            S(lambda e, kd=kd, qb=qb: e.activation(
                                nsc[:, 0:512],
                                khat[:, kd * 2048 + qb * 512: kd * 2048 + (qb + 1) * 512],
                                ACT.Square))
                            w.wait("tensor", s_ac)
                            w.do("tensor", lambda e, c=c: e.matmul(
                                pb[2][:, :], ones, nsc[:, 0:512],
                                start=(c == 0), stop=(c == 1)),
                                (s_pe, 1) if c == 1 else None)
                    if P4K >= 2:
                        w.wait("scalar", s_pe)
                        S(lambda e: e.activation(nrm[:, 0:512], pb[2][:, :], ACT.Sqrt))
                        w.wait("vector", s_ac)
                        V(lambda e: e.tensor_scalar(nrm[:, 0:512], nrm[:, 0:512], 1e-12,
                                                    scalar2=None, op0=ALU.max))
                        V(lambda e: e.reciprocal(nrm[:, 0:512], nrm[:, 0:512]))
                        for c in range(2):
                            kd = kvh * 2 + c
                            V(lambda e, kd=kd, qb=qb: e.tensor_tensor(
                                khat[:, kd * 2048 + qb * 512: kd * 2048 + (qb + 1) * 512],
                                khat[:, kd * 2048 + qb * 512: kd * 2048 + (qb + 1) * 512],
                                nrm[:, 0:512], op=ALU.mult))
                    ve_k = s_ve.n
                    w.wait("tensor", s_ve, ve_k)
                if P4K < 3:
                    continue
                w.wait("sync", s_pe, s_pe.n)
                w.do("sync", lambda e: e.dma_start(
                    out=wkv, in_=wv_d.rearrange("(c p) n -> p c n", p=128)), (s_ld, 16))
                ld_wv = s_ld.n
                w.wait("tensor", s_ld, ld_wv)
                if sq == 0 and qb == 0:
                    w.wait("tensor", s_ve, ve_vone)
                for sub in range(4):
                    ti = qb * 4 + sub
                    for fc in range(DC):
                        w.do("tensor", lambda e, sub=sub, fc=fc: e.matmul(
                            pb[3][:, :], hv[:, fc, sub * 128:(sub + 1) * 128],
                            wvv[:, fc, :], start=(fc == 0), stop=(fc == DC - 1)))
                    w.mark("tensor", s_pe, 1)
                    w.wait("vector", s_pe)
                    V(lambda e, ti=ti: e.tensor_copy(
                        vext[:, ti * 514:ti * 514 + 256], pb[3][:, 0:256]))
                    V(lambda e, ti=ti: e.tensor_copy(
                        vext[:, ti * 514 + 257:ti * 514 + 513], pb[3][:, 256:512]))
                    ve_v = s_ve.n
                    w.wait("tensor", s_ve, ve_v)
                pe_wv = s_pe.n
                w.wait("vector", s_pe, s_pe.n)
            if P4S < 3:
                continue
            for qb in range(NSEQ):
                tb = sq * 4 + qb
                w.wait("sync", s_pe, pe_wq)
                w.wait("sync", s_gp, gp_h)
                w.do("sync", lambda e, tb=tb: e.dma_start(
                    out=hqb.rearrange("p (c n) -> p c n", n=512),
                    in_=h_stage[tb * 1024:(tb + 1) * 1024, :]
                    .rearrange("(c p) n -> p c n", p=128)), (s_ld, 16))
                ld_h = s_ld.n
                for head in range(NHEAD):
                    kvh = head // 2
                    w.wait("sync", s_pe, pe_wq)
                    w.do("sync", lambda e, head=head: e.dma_start(
                        out=wqv,
                        in_=wq_d.rearrange("(c p) n -> p c n", p=128)
                        [:, :, head * 256:(head + 1) * 256]), (s_ld, 16))
                    ld_q = s_ld.n
                    w.wait("tensor", s_ld, ld_q)
                    if head == 0:
                        w.wait("tensor", s_ld, ld_h)
                    for c in range(2):
                        for fc in range(DC):
                            w.do("tensor", lambda e, c=c, fc=fc: e.matmul(
                                pb[6 + c][:, :], wqv[:, fc, c * 128:(c + 1) * 128],
                                hv[:, fc, :], start=(fc == 0), stop=(fc == DC - 1)))
                        w.mark("tensor", s_pe, 1)
                        w.wait("vector", s_pe)
                        V(lambda e, c=c: e.tensor_copy(qhat[:, c * 512:(c + 1) * 512],
                                                       pb[6 + c][:, :]))
                        w.wait("scalar", s_ve)
                        S(lambda e, c=c: e.activation(nsc[:, 0:512],
                                                      qhat[:, c * 512:(c + 1) * 512],
                                                      ACT.Square))
                        w.wait("tensor", s_ac)
                        w.do("tensor", lambda e, c=c: e.matmul(
                            pb[0][:, :], ones, nsc[:, 0:512],
                            start=(c == 0), stop=(c == 1)),
                            (s_pe, 1) if c == 1 else None)
                    pe_wq = s_pe.n
                    w.wait("scalar", s_pe, pe_wq)
                    S(lambda e: e.activation(nrm[:, 0:512], pb[0][:, :], ACT.Sqrt))
                    w.wait("vector", s_ac)
                    V(lambda e: e.tensor_scalar(nrm[:, 0:512], nrm[:, 0:512], 1e-12,
                                                scalar2=None, op0=ALU.max))
                    V(lambda e: e.reciprocal(nrm[:, 0:512], nrm[:, 0:512]))
                    for c in range(2):
                        V(lambda e, c=c: e.tensor_tensor(
                            qhat[:, c * 512:(c + 1) * 512],
                            qhat[:, c * 512:(c + 1) * 512],
                            nrm[:, 0:512], op=ALU.mult))
                    ve_q = s_ve.n
                    w.wait("tensor", s_ve, ve_q)
                    if P4S < 4:
                        continue
                    nkt = 4 * qb + 4
                    ac_exp = [0] * nkt
                    for kt in range(nkt):
                        sb_ = kt % 2
                        if kt >= 2:
                            w.wait("tensor", s_ac, ac_exp[kt - 2])
                        for c in range(2):
                            kd = kvh * 2 + c
                            w.do("tensor", lambda e, sb_=sb_, kd=kd, kt=kt, c=c: e.matmul(
                                pb[sb_][:, :],
                                khat[:, kd * 2048 + kt * 128: kd * 2048 + (kt + 1) * 128],
                                qhat[:, c * 512:(c + 1) * 512],
                                start=(c == 0), stop=(c == 1)),
                                (s_pe, 1) if c == 1 else None)
                        w.wait("scalar", s_pe)
                        S(lambda e, sb_=sb_: e.activation(
                            nrm[:, 0:512], pb[sb_][:, :], ACT.Exp, scale=temp_ap))
                        ac_exp[kt] = s_ac.n
                        if kt >= 4 * qb:
                            j = kt - 4 * qb
                            w.wait("vector", s_ac, ac_exp[kt])
                            V(lambda e, j=j: e.tensor_tensor(
                                nrm[:, 0:512], nrm[:, 0:512],
                                maskd[:, j * 512:(j + 1) * 512], op=ALU.mult))
                            w.wait("tensor", s_ve)
                        else:
                            w.wait("tensor", s_ac, ac_exp[kt])
                        for qs in range(4):
                            w.do("tensor", lambda e, qs=qs, kt=kt, kvh=kvh, nkt=nkt:
                                 e.matmul(
                                     pb[2 + qs][:, 0:257],
                                     nrm[:, qs * 128:(qs + 1) * 128],
                                     vxv[:, kt, kvh * 257:(kvh + 1) * 257],
                                     start=(kt == 0), stop=(kt == nkt - 1)),
                                 (s_pe, 1) if (kt == nkt - 1 and qs == 3) else None)
                        w.wait("scalar", s_pe, s_pe.n)
                    w.wait("vector", s_pe)
                    for qs in range(4):
                        V(lambda e, qs=qs: e.reciprocal(sc1[:, 5:6],
                                                        pb[2 + qs][:, 256:257]))
                        V(lambda e, qs=qs, head=head: e.tensor_scalar(
                            attn_tm[:, qs * 1024 + head * 256:
                                    qs * 1024 + (head + 1) * 256],
                            pb[2 + qs][:, 0:256], sc1[:, 5:6], scalar2=None,
                            op0=ALU.mult))
                    ve_pv = s_ve.n
                    w.wait("tensor", s_ve, ve_pv)
                if P4S < 5:
                    continue
                for qs in range(4):
                    for st_ in range(2):
                        bank = 6 + st_
                        for j in range(4):
                            adc = st_ * 4 + j
                            w.do("tensor", lambda e, bank=bank, j=j, qs=qs, adc=adc:
                                 e.transpose(pb[bank][:, j * 128:(j + 1) * 128],
                                             attn_tm[:, qs * 1024 + adc * 128:
                                                     qs * 1024 + (adc + 1) * 128],
                                             ident))
                        w.mark("tensor", s_pe, 1)
                        w.wait("vector", s_pe)
                        V(lambda e, bank=bank, st_=st_, qs=qs: e.tensor_copy(
                            afv[:, st_ * 4:st_ * 4 + 4, qs * 128:(qs + 1) * 128],
                            pb[bank].rearrange("p (j n) -> p j n", n=128)))
                        w.wait("tensor", s_ve)
                ve_wo = {}
                for half in range(2):
                    w.wait("sync", s_pe, pe_wo)
                    w.do("sync", lambda e, half=half: e.dma_start(
                        out=wov,
                        in_=wo_d.rearrange("(c p) n -> p c n", p=128)
                        [:, :, half * 512:(half + 1) * 512]), (s_ld, 16))
                    ld_wo = s_ld.n
                    w.wait("tensor", s_ld, ld_wo)
                    for oc in range(4):
                        b2 = 2 + (oc % 2)
                        if oc >= 2:
                            w.wait("tensor", s_ve, ve_wo[oc - 2])
                        for adc in range(DC):
                            w.do("tensor", lambda e, b2=b2, adc=adc, oc=oc: e.matmul(
                                pb[b2][:, :], wov[:, adc, oc * 128:(oc + 1) * 128],
                                afv[:, adc, :], start=(adc == 0), stop=(adc == DC - 1)))
                        w.mark("tensor", s_pe, 1)
                        w.wait("vector", s_pe)
                        V(lambda e, b2=b2, half=half, oc=oc: e.tensor_copy(
                            sfv[:, half * 4 + oc, :], pb[b2][:, :]))
                        ve_wo[oc] = s_ve.n
                    pe_wo = s_pe.n
                w.wait("tensor", s_ve, s_ve.n)
                gp_s = 0
                for sub in range(4):
                    t = tb * 4 + sub
                    w.wait("sync", s_ve, ve_x4)
                    w.do("sync", lambda e, t=t: e.dma_start(
                        out=x4, in_=x_full[t * 128:(t + 1) * 128, :]), (s_ld, 16))
                    ld_x = s_ld.n
                    if sub > 0:
                        w.wait("vector", s_gp, gp_s)
                    for st_ in range(2):
                        bank = 6 + st_
                        for j in range(4):
                            oc2 = st_ * 4 + j
                            w.do("tensor", lambda e, bank=bank, j=j, oc2=oc2, sub=sub:
                                 e.transpose(pb[bank][:, j * 128:(j + 1) * 128],
                                             sfv[:, oc2, sub * 128:(sub + 1) * 128],
                                             ident))
                        w.mark("tensor", s_pe, 1)
                        w.wait("vector", s_pe)
                        V(lambda e, bank=bank, st_=st_: e.tensor_copy(
                            s4[:, st_ * 512:(st_ + 1) * 512]
                            .rearrange("p (j n) -> p j n", n=128),
                            pb[bank].rearrange("p (j n) -> p j n", n=128)))
                        w.wait("tensor", s_ve)
                    w.wait("vector", s_ld, ld_x)
                    V(lambda e, t=t: e.tensor_scalar(
                        x4, x4, mbuf[:, t:t + 1], scalar2=rbuf[:, t:t + 1],
                        op0=ALU.subtract, op1=ALU.mult))
                    ve_x4 = s_ve.n
                    V(lambda e: e.scalar_tensor_tensor(
                        s4, s4, gate_ap, x4, op0=ALU.mult, op1=ALU.add))
                    ve_sf = s_ve.n
                    w.wait("gpsimd", s_ve, ve_sf)
                    w.do("gpsimd", lambda e, t=t: e.dma_start(
                        out=s_stage[t * 128:(t + 1) * 128, :], in_=s4), (s_gp, 16))
                    gp_s = s_gp.n
                w.wait("vector", s_gp, gp_s)
        barrier()

        if PH < 5:
            w.wait("gpsimd", s_gp, s_gp.n)
            w.emit(block)
            return nc
        A.reset(p4m)
        p5m = A.mark()
        wa_sb = A.alloc(8192)
        wg_sb = A.alloc(8192)
        wu_sb = A.alloc(8192)
        xg5 = A.alloc(4096)
        xfm5 = A.alloc(4096)
        act5 = A.alloc(4096)
        gel5 = A.alloc(512)
        o2sb = A.alloc(512)
        part5 = A.alloc(512)
        w_in_v = w_in_d.rearrange("(e c p) n -> e p c n", e=E, p=128)
        w_out_v = w_out_d.rearrange("(e c p) n -> e p c n", e=E, p=128)
        wav = wa_sb.rearrange("p (c n) -> p c n", n=1024)
        wgv = wg_sb.rearrange("p (c n) -> p c n", n=1024)
        wuv = wu_sb.rearrange("p (c n) -> p c n", n=1024)
        xfv5 = xfm5.rearrange("p (c n) -> p c n", n=512)
        xg5v = xg5.rearrange("p (s d) -> p s d", d=D)
        a5 = act5.rearrange("p (c n) -> p c n", n=512)
        pe_prev_phase = 0
        jh0_stores = 0
        for ex in range(E):
            for jh in range(2):
                w.wait("sync", s_pe, pe_prev_phase)
                w.do("sync", lambda e, ex=ex, jh=jh: e.dma_start(
                    out=wav, in_=w_in_v[ex, :, :, jh * 1024:(jh + 1) * 1024]), (s_ld, 16))
                w.do("sync", lambda e, ex=ex, jh=jh: e.dma_start(
                    out=wgv, in_=w_in_v[ex, :, :, 2048 + jh * 1024: 2048 + (jh + 1) * 1024]),
                    (s_ld, 16))
                w.do("sync", lambda e, ex=ex, jh=jh: e.dma_start(
                    out=wuv, in_=w_out_v[ex, :, jh * 8:(jh + 1) * 8, :]), (s_ld, 16))
                ld_w5 = s_ld.n
                if jh == 1:
                    w.wait("sync", s_gp, jh0_stores)
                for st in range(5):
                    r0 = ex * CL + st * 512
                    w.wait("sync", s_pe, pe_prev_phase if st == 0 else pe_tr5)
                    w.do("sync", lambda e, r0=r0: e.dma_start(
                        out=xg5v,
                        in_=expbuf[r0:r0 + 512, :].rearrange("(s p) d -> p s d", p=128)),
                        (s_ld, 16))
                    ld_x5 = s_ld.n
                    w.wait("tensor", s_ld, ld_x5)
                    evh = []
                    for h in range(8):
                        sub, st_ = h // 2, h % 2
                        bank = 6 + (h % 2)
                        if h >= 2:
                            sem, val = evh[h - 2]
                            w.wait("tensor", sem, val)
                        for j in range(4):
                            fc = st_ * 4 + j
                            w.do("tensor", lambda e, bank=bank, j=j, sub=sub, fc=fc:
                                 e.transpose(pb[bank][:, j * 128:(j + 1) * 128],
                                             xg5v[:, sub, fc * 128:(fc + 1) * 128],
                                             ident))
                        w.mark("tensor", s_pe, 1)
                        if h % 2 == 0:
                            w.wait("vector", s_pe)
                            V(lambda e, bank=bank, st_=st_, sub=sub: e.tensor_copy(
                                xfv5[:, st_ * 4:st_ * 4 + 4, sub * 128:(sub + 1) * 128],
                                pb[bank].rearrange("p (j n) -> p j n", n=128)))
                            evh.append((s_ve, s_ve.n))
                        else:
                            w.wait("scalar", s_pe)
                            S(lambda e, bank=bank, st_=st_, sub=sub: e.copy(
                                xfv5[:, st_ * 4:st_ * 4 + 4, sub * 128:(sub + 1) * 128],
                                pb[bank].rearrange("p (j n) -> p j n", n=128)))
                            evh.append((s_ac, s_ac.n))
                    pe_tr5 = s_pe.n
                    for sem, val in evh[-2:]:
                        w.wait("tensor", sem, val)
                    if st == 0:
                        w.wait("tensor", s_ld, ld_w5)
                    ve_l1 = {}
                    for fo in range(8):
                        ba = (fo % 2) * 2
                        if fo >= 2:
                            w.wait("tensor", s_ve, ve_l1[fo - 2])
                        for fc in range(DC):
                            w.do("tensor", lambda e, ba=ba, fc=fc, fo=fo: e.matmul(
                                pb[ba][:, :], wav[:, fc, fo * 128:(fo + 1) * 128],
                                xfv5[:, fc, :], start=(fc == 0), stop=(fc == DC - 1)))
                        for fc in range(DC):
                            w.do("tensor", lambda e, ba=ba, fc=fc, fo=fo: e.matmul(
                                pb[ba + 1][:, :], wgv[:, fc, fo * 128:(fo + 1) * 128],
                                xfv5[:, fc, :], start=(fc == 0), stop=(fc == DC - 1)))
                        w.mark("tensor", s_pe, 1)
                        w.wait("scalar", s_pe)
                        if fo >= 1:
                            w.wait("scalar", s_ve, ve_l1[fo - 1])
                        S(lambda e, ba=ba: e.activation(gel5[:, :], pb[ba][:, :],
                                                        ACT.Gelu))
                        w.wait("vector", s_ac)
                        V(lambda e, ba=ba, fo=fo: e.tensor_tensor(
                            a5[:, fo, :], gel5[:, :], pb[ba + 1][:, :], op=ALU.mult))
                        ve_l1[fo] = s_ve.n
                    w.wait("tensor", s_ve, ve_l1[7])
                    w.wait("tensor", s_ve, ve_l1[6])
                    for hh in range(2):
                        for qs in range(4):
                            for fo in range(8):
                                w.do("tensor", lambda e, qs=qs, fo=fo, hh=hh: e.matmul(
                                    pb[qs][:, :], a5[:, fo, qs * 128:(qs + 1) * 128],
                                    wuv[:, fo, hh * 512:(hh + 1) * 512],
                                    start=(fo == 0), stop=(fo == 7)))
                            w.mark("tensor", s_pe, 1)
                            if jh == 0:
                                w.wait("vector", s_pe)
                                V(lambda e, qs=qs: e.tensor_copy(o2sb[:, :], pb[qs][:, :]))
                                ve_o2 = s_ve.n
                                w.wait("gpsimd", s_ve, ve_o2)
                                w.do("gpsimd", lambda e, r0=r0, qs=qs, hh=hh: e.dma_start(
                                    out=outbuf[r0 + qs * 128: r0 + (qs + 1) * 128,
                                               hh * 512:(hh + 1) * 512],
                                    in_=o2sb), (s_gp, 16))
                                gp_o2 = s_gp.n
                                w.wait("vector", s_gp, gp_o2)
                            else:
                                w.do("sync", lambda e, r0=r0, qs=qs, hh=hh: e.dma_start(
                                    out=part5,
                                    in_=outbuf[r0 + qs * 128: r0 + (qs + 1) * 128,
                                               hh * 512:(hh + 1) * 512]), (s_ld, 16))
                                ld_p = s_ld.n
                                w.wait("vector", s_pe)
                                w.wait("vector", s_ld, ld_p)
                                V(lambda e, qs=qs: e.tensor_tensor(
                                    o2sb[:, :], pb[qs][:, :], part5[:, :], op=ALU.add))
                                ve_o2 = s_ve.n
                                w.wait("gpsimd", s_ve, ve_o2)
                                w.do("gpsimd", lambda e, r0=r0, qs=qs, hh=hh: e.dma_start(
                                    out=outbuf[r0 + qs * 128: r0 + (qs + 1) * 128,
                                               hh * 512:(hh + 1) * 512],
                                    in_=o2sb), (s_gp, 16))
                                gp_o2 = s_gp.n
                                w.wait("vector", s_gp, gp_o2)
                                w.wait("sync", s_ve, ve_o2)
                        w.wait("tensor", s_ve, s_ve.n)
                    pe_prev_phase = s_pe.n
                if jh == 0:
                    jh0_stores = s_gp.n
        barrier()

        if PH < 6:
            w.wait("gpsimd", s_gp, s_gp.n)
            w.emit(block)
            return nc
        A.reset(p5m)
        rowA = A.alloc(1024)
        rowB = A.alloc(1024)
        t2b = A.alloc(1024)
        moeb = A.alloc(1024)
        x6 = A.alloc(1024)
        s6 = A.alloc(1024)
        o6 = A.alloc(1024)
        ve_o6 = 0
        gp_out = 0
        for t in range(NT):
            if t > 0:
                w.wait("gpsimd", s_ve, ve_o6)
            w.do("gpsimd", lambda e, t=t: e.indirect_dma_start(
                out=rowA, out_offset=None, in_=outbuf,
                in_offset=IndirectOffsetOnAxis(ap=physG[:, t:t + 1], axis=0)), (s_gp, 16))
            w.do("gpsimd", lambda e, t=t: e.indirect_dma_start(
                out=rowB, out_offset=None, in_=outbuf,
                in_offset=IndirectOffsetOnAxis(ap=physG[:, 64 + t:64 + t + 1], axis=0)),
                (s_gp, 16))
            gp_g = s_gp.n
            w.wait("sync", s_ve, ve_o6)
            w.do("sync", lambda e, t=t: e.dma_start(
                out=s6, in_=s_stage[t * 128:(t + 1) * 128, :]), (s_ld, 16))
            w.do("sync", lambda e, t=t: e.dma_start(
                out=x6, in_=x_full[t * 128:(t + 1) * 128, :]), (s_ld, 16))
            ld_sx = s_ld.n
            w.wait("scalar", s_gp, gp_g)
            S(lambda e, t=t: e.activation(t2b, rowB, ACT.Copy,
                                          scale=gbuf[:, 64 + t:64 + t + 1]))
            ac_t2 = s_ac.n
            w.wait("vector", s_gp, gp_g)
            w.wait("vector", s_ac, ac_t2)
            if t > 0:
                w.wait("vector", s_gp, gp_out)
            V(lambda e, t=t: e.scalar_tensor_tensor(
                moeb, rowA, gbuf[:, t:t + 1], t2b, op0=ALU.mult, op1=ALU.add))
            ve_m = s_ve.n
            w.wait("scalar", s_ve, ve_m)
            S(lambda e: e.activation(moeb, moeb, ACT.Silu))
            ac_sil = s_ac.n
            w.wait("vector", s_ac, ac_sil)
            w.wait("vector", s_ld, ld_sx)
            V(lambda e: e.tensor_tensor(o6, moeb, s6, op=ALU.mult))
            V(lambda e: e.tensor_tensor(o6, o6, x6, op=ALU.add))
            ve_o6 = s_ve.n
            w.wait("gpsimd", s_ve, ve_o6)
            w.do("gpsimd", lambda e, t=t: e.dma_start(
                out=out_d[t * 128:(t + 1) * 128, :], in_=o6), (s_gp, 16))
            gp_out = s_gp.n
        if dbg_arena is not None:
            w.do("gpsimd", lambda e: e.dma_start(out=dbg_arena, in_=AP_[:]), (s_gp, 16))
            w.do("gpsimd", lambda e: e.dma_start(out=dbg_i, in_=AI_[:]), (s_gp, 16))
        w.wait("gpsimd", s_gp, s_gp.n)

        w.emit(block)
    return nc


_NC_CACHE = None


def kernel(L0_attn_x, L0_attn_norm_w, L0_attn_Wq, L0_attn_Wk, L0_attn_Wv,
           L0_attn_Wo, L0_attn_gate, L0_attn_log_temperature,
           L0_ffn_Wg, L0_ffn_W_in, L0_ffn_W_out):
    global _NC_CACHE
    x = np.ascontiguousarray(np.asarray(L0_attn_x, np.float32).reshape(STOK, D))
    wg = np.asarray(L0_ffn_Wg, np.float32)
    wg_ext = np.ascontiguousarray(
        np.concatenate([wg, np.full((D, 1), 2.0 ** -10, np.float32)], axis=1))
    w_in = np.ascontiguousarray(np.asarray(L0_ffn_W_in, np.float32).reshape(E * D, 4096))
    w_out = np.ascontiguousarray(np.asarray(L0_ffn_W_out, np.float32).reshape(E * 2048, D))
    consts = {
        "norm_w": np.ascontiguousarray(np.asarray(L0_attn_norm_w, np.float32).reshape(1, D)),
        "wq": np.ascontiguousarray(np.asarray(L0_attn_Wq, np.float32)),
        "wk": np.ascontiguousarray(np.asarray(L0_attn_Wk, np.float32)),
        "wv": np.ascontiguousarray(np.asarray(L0_attn_Wv, np.float32)),
        "wo": np.ascontiguousarray(np.asarray(L0_attn_Wo, np.float32)),
        "gate_in": np.asarray(L0_attn_gate, np.float32).reshape(1, 1),
        "ltemp_in": np.asarray(L0_attn_log_temperature, np.float32).reshape(1, 1),
        "wg_ext": wg_ext,
        "w_in": w_in,
        "w_out": w_out,
        "c_ident": np.eye(128, dtype=np.float32),
        "c_tri": np.triu(np.ones((128, 128), np.float32), 1),
        "c_ones": np.ones((128, 128), np.float32),
        "c_iotar": np.ascontiguousarray(
            np.tile((7.0 - np.arange(8, dtype=np.float32)), (128, 4))),
        "c_maskd": np.ascontiguousarray(np.concatenate(
            [(128 * j + np.arange(128)[:, None] <= np.arange(512)[None, :])
             .astype(np.float32) for j in range(4)], axis=1)),
    }
    in_maps = []
    for c in range(NCORES):
        xr = np.ascontiguousarray(np.roll(x, -c * TPC, axis=0))
        gr = (c + np.arange(8)) % 8
        m = dict(consts)
        m["x_full"] = xr
        m["c_coremask"] = np.ascontiguousarray(
            np.repeat((gr < c).astype(np.float32), 16).reshape(1, 128))
        m["c_mymask"] = np.ascontiguousarray(
            np.repeat((np.arange(8) == 0).astype(np.float32), 16).reshape(1, 128))
        in_maps.append(m)
    if _NC_CACHE is None:
        _NC_CACHE = build_nc()
    res = run_bass_kernel_spmd(_NC_CACHE, in_maps, core_ids=list(range(NCORES)))
    outs = [res.results[c]["out"] for c in range(NCORES)]
    return np.concatenate(outs, axis=0).reshape(32, L, D)


# version 07
# speedup vs baseline: 1.0472x; 1.0472x over previous
import os
import numpy as np
from contextlib import ExitStack

import concourse.bass as bass
import concourse.mybir as mybir
from concourse.bass_utils import run_bass_kernel_spmd
from concourse.bass_core_rust import IndirectOffsetOnAxis

F32 = mybir.dt.float32
I32 = mybir.dt.int32
ALU = mybir.AluOpType
AX = mybir.AxisListType
ACT = mybir.ActivationFunctionType

NCORES = 8
D = 1024
DC = 8
L = 2048
NSEQ = 4
TPC = NSEQ * L
NT = TPC // 128
STOK = NCORES * TPC
NGRP = STOK // 512
GPR = NGRP // NCORES
E = 8
CL = 2560
CCAP = 20480
TRASH = E * CL
NHEAD = 4

ENGS = ("sync", "tensor", "vector", "scalar", "gpsimd")


class Sem:
    def __init__(self, handle):
        self.h = handle
        self.n = 0


class Walk:
    def __init__(self):
        self.ops = {k: [] for k in ENGS}
        self.self_sems = {}

    def do(self, eng, fn, inc=None, selfsync=True):
        ss = self.self_sems.get(eng)
        if ss is not None:
            if selfsync:
                self.ops[eng].append([lambda e, h=ss.h, v=ss.n: e.wait_ge(h, v),
                                      None, True])
            if inc is None:
                inc = (ss, 1)
        if inc is not None:
            inc[0].n += inc[1]
        self.ops[eng].append([fn, inc, False])

    def wait(self, eng, sem, val=None):
        v = sem.n if val is None else val
        self.ops[eng].append([lambda e, h=sem.h, v=v: e.wait_ge(h, v), None, True])

    def mark(self, eng, sem, k=1):
        for op in reversed(self.ops[eng]):
            if not op[2]:
                if op[1] is None:
                    op[1] = (sem, k)
                    sem.n += k
                    return True
                if op[1][0] is sem:
                    return False
                raise AssertionError(f"mark({eng}): last op has other-sem inc")
        raise AssertionError(f"mark({eng}): no real op to mark")

    def emit(self, block):
        for eng in ENGS:
            ops = self.ops[eng]

            def mk(ops):
                def f(e):
                    for fn, inc, _ in ops:
                        r = fn(e)
                        if inc is not None:
                            r.then_inc(inc[0].h, inc[1])
                return f
            getattr(block, eng)(mk(ops))


class Arena:
    def __init__(self, t, words):
        self.t = t
        self.words = words
        self.off = 0

    def alloc(self, w_, align=16):
        o = (self.off + align - 1) // align * align
        assert o + w_ <= self.words, f"arena overflow {o}+{w_}>{self.words}"
        self.off = o + w_
        return self.t[:, o:o + w_]

    def mark(self):
        return self.off

    def reset(self, m):
        self.off = m


def build_nc():
    PH = int(os.environ.get("K_PHASES", "9"))
    P1S = int(os.environ.get("K_P1SUB", "9"))
    P4S = int(os.environ.get("K_P4SUB", "9"))
    P4K = int(os.environ.get("K_P4K", "9"))
    KSSQ = int(os.environ.get("K_SSQ", "1"))
    KEVAC = os.environ.get("K_EVAC", "mix")
    nc = bass.Bass("TRN2", target_bir_lowering=False, debug=False,
                   detect_race_conditions=False)

    def di(name, shape, dt=F32, kind="ExternalInput"):
        return nc.dram_tensor(name, list(shape), dt, kind=kind).ap()

    x_full = di("x_full", [STOK, D])
    norm_w = di("norm_w", [1, D])
    wq_d = di("wq", [D, D])
    wk_d = di("wk", [D, 512])
    wv_d = di("wv", [D, 512])
    wo_d = di("wo", [D, D])
    gate_in = di("gate_in", [1, 1])
    ltemp_in = di("ltemp_in", [1, 1])
    wg_ext = di("wg_ext", [D, 9])
    w_in_d = di("w_in", [E * D, 4096])
    w_out_d = di("w_out", [E * 2048, D])
    c_ident = di("c_ident", [128, 128])
    c_tri = di("c_tri", [128, 128])
    c_ones = di("c_ones", [128, 128])
    c_iotar = di("c_iotar", [128, 32])
    c_maskd = di("c_maskd", [128, 2048])
    c_coremask = di("c_coremask", [1, 128])
    c_mymask = di("c_mymask", [1, 128])
    out_d = di("out", [TPC, D], kind="ExternalOutput")
    DBG = "ExternalOutput" if os.environ.get("K_DEBUG_OUT") else "Internal"
    expbuf = di("expbuf", [TRASH + 128, D], kind=DBG)
    outbuf = di("outbuf", [TRASH, D], kind=DBG)
    s_stage = di("s_stage", [TPC, D], kind=DBG)
    dbg_arena = di("dbg_arena", [128, 7424], kind=DBG) if DBG == "ExternalOutput" else None
    dbg_i = di("dbg_i", [128, 512], I32, kind=DBG) if DBG == "ExternalOutput" else None
    h_stage = di("h_stage", [16 * D, 512], kind="Internal")
    bounce = di("bounce", [64, 16], kind="Internal")

    es = ExitStack()
    w = Walk()
    with es:
        AP_ = es.enter_context(nc.sbuf_tensor("arena_p", [128, 7424], F32))
        AF_ = es.enter_context(nc.sbuf_tensor("arena_f", [128, 45000], F32))
        AI_ = es.enter_context(nc.sbuf_tensor("arena_i", [128, 512], I32))
        P = Arena(AP_, 7424)
        A = Arena(AF_, 45000)
        AI = Arena(AI_, 512)
        pb = [es.enter_context(nc.psum_tensor(f"pb{i}", [128, 512], F32))
              for i in range(8)]
        s_ld = Sem(es.enter_context(nc.semaphore("s_ld")))
        s_gp = Sem(es.enter_context(nc.semaphore("s_gp")))
        s_pe = Sem(es.enter_context(nc.semaphore("s_pe")))
        s_ve = Sem(es.enter_context(nc.semaphore("s_ve")))
        s_ac = Sem(es.enter_context(nc.semaphore("s_ac")))
        s_gb = Sem(es.enter_context(nc.semaphore("s_gb")))
        ALLS = (s_ld, s_gp, s_pe, s_ve, s_ac, s_gb)
        w.self_sems = {"vector": s_ve, "scalar": s_ac}
        block = es.enter_context(nc.Block())

        ident = P.alloc(128)
        tri = P.alloc(128)
        ones = P.alloc(128)
        iotar = P.alloc(32)
        maskd = P.alloc(2048)
        wgext = P.alloc(80)
        nwrep = P.alloc(D)
        cwrep = P.alloc(32)
        grep = P.alloc(16)
        cm_row = P.alloc(128)
        mm_row = P.alloc(128)
        cnts8 = P.alloc(128)
        base_row = P.alloc(16)
        k2_row = P.alloc(16)
        negbuf = P.alloc(512)
        ohbuf = P.alloc(1024)
        ecat = P.alloc(128)
        e12 = P.alloc(128)
        mbuf = P.alloc(64)
        ssqbuf = P.alloc(64)
        rbuf = P.alloc(64)
        gbuf = P.alloc(128)
        trow = P.alloc(32)
        physS = AI.alloc(128)
        physG = AI.alloc(128)
        assert P.off <= 7424

        def W8(v):
            return v.rearrange("p (t e) -> p t e", e=8)

        def W16(v):
            return v.rearrange("p (t e) -> p t e", e=16)

        def bc(v, shape):
            return v[:, :, None].to_broadcast(shape)

        def V(fn):
            w.do("vector", fn, (s_ve, 1))

        def VE(fn):
            w.do("vector", fn, (s_ve, 1), selfsync=False)

        def S(fn):
            w.do("scalar", fn, (s_ac, 1))

        def SE(fn):
            w.do("scalar", fn, (s_ac, 1), selfsync=False)

        def barrier():
            w.mark("tensor", s_pe, 1)
            w.mark("vector", s_ve, 1)
            w.mark("scalar", s_ac, 1)
            w.mark("gpsimd", s_gp, 16)
            tgt = [s.n for s in ALLS]
            for eng in ENGS:
                for s, v in zip(ALLS, tgt):
                    w.wait(eng, s, v)

        tr_evac = {"k": 0}

        def transp8(src, dstv, sub, banks, evac_engines=("vector", "scalar")):
            for st_ in range(2):
                bank = banks[st_]
                for j in range(4):
                    fc = st_ * 4 + j
                    w.do("tensor", lambda e, bank=bank, j=j, fc=fc, src=src:
                         e.transpose(pb[bank][:, j * 128:(j + 1) * 128],
                                     src[:, fc * 128:(fc + 1) * 128], ident))
                w.mark("tensor", s_pe, 1)
                eng = evac_engines[st_ % len(evac_engines)]
                sem = s_ve if eng == "vector" else s_ac
                w.wait(eng, s_pe)
                if eng == "vector":
                    w.do(eng, lambda e, bank=bank, st_=st_, sub=sub, dstv=dstv:
                         e.tensor_copy(
                             dstv[:, st_ * 4:st_ * 4 + 4, sub * 128:(sub + 1) * 128],
                             pb[bank].rearrange("p (j n) -> p j n", n=128)), (sem, 1))
                else:
                    w.do(eng, lambda e, bank=bank, st_=st_, sub=sub, dstv=dstv:
                         e.copy(
                             dstv[:, st_ * 4:st_ * 4 + 4, sub * 128:(sub + 1) * 128],
                             pb[bank].rearrange("p (j n) -> p j n", n=128)), (sem, 1))
                yield (eng, (s_ve.n if eng == "vector" else s_ac.n))

        for dst, src in ((ident, c_ident), (tri, c_tri), (ones, c_ones),
                         (iotar, c_iotar), (maskd, c_maskd)):
            w.do("sync", lambda e, d=dst, s=src: e.dma_start(out=d, in_=s), (s_ld, 16))
        w.do("sync", lambda e: e.dma_start(out=cm_row[0:1, :], in_=c_coremask), (s_ld, 16))
        w.do("sync", lambda e: e.dma_start(out=mm_row[0:1, :], in_=c_mymask), (s_ld, 16))
        w.do("sync", lambda e: e.dma_start(
            out=wgext[:, 0:72].rearrange("p (c n) -> p c n", n=9),
            in_=wg_ext.rearrange("(c p) n -> p c n", p=128)), (s_ld, 16))
        w.do("sync", lambda e: e.dma_start(out=grep[0:1, 0:1], in_=gate_in), (s_ld, 16))
        w.do("sync", lambda e: e.dma_start(out=grep[0:1, 1:2], in_=ltemp_in), (s_ld, 16))
        zt = P.alloc(D)
        p0m = A.mark()
        nw_row = A.alloc(D)
        cw_row = A.alloc(32)
        w.do("sync", lambda e: e.dma_start(out=nw_row[0:1, :], in_=norm_w), (s_ld, 16))
        ld0 = s_ld.n
        w.wait("scalar", s_ld, ld0)
        S(lambda e: e.activation(grep[0:1, 1:2], grep[0:1, 1:2], ACT.Exp))
        w.wait("tensor", s_ld, ld0)
        w.wait("tensor", s_ac)
        w.do("tensor", lambda e: e.matmul(pb[0][:, 0:2], ones[0:1, :], grep[0:1, 0:2],
                                          start=True, stop=True))
        w.do("tensor", lambda e: e.matmul(pb[1][:, 0:512], ones[0:1, :],
                                          nw_row[0:1, 0:512], start=True, stop=True))
        w.do("tensor", lambda e: e.matmul(pb[2][:, 0:512], ones[0:1, :],
                                          nw_row[0:1, 512:1024], start=True, stop=True))
        for fc in range(DC):
            w.do("tensor", lambda e, fc=fc: e.matmul(
                pb[3][0:1, 0:8], ones[:, 0:1], wgext[:, fc * 9:fc * 9 + 8],
                start=(fc == 0), stop=(fc == DC - 1)),
                (s_pe, 1) if fc == DC - 1 else None)
        w.wait("vector", s_pe)
        V(lambda e: e.tensor_copy(grep[:, 2:3], pb[0][:, 0:1]))
        V(lambda e: e.tensor_copy(grep[:, 3:4], pb[0][:, 1:2]))
        V(lambda e: e.tensor_copy(nwrep[:, 0:512], pb[1][:, 0:512]))
        V(lambda e: e.tensor_copy(nwrep[:, 512:1024], pb[2][:, 0:512]))
        V(lambda e: e.tensor_copy(
            cw_row[0:1, :].rearrange("p (t n) -> p t n", n=8),
            pb[3][0:1, 0:8].rearrange("p (o n) -> p o n", o=1).to_broadcast([1, 4, 8])))
        V(lambda e: e.memset(zt[:], 0.0))
        ceps = P.alloc(16)
        V(lambda e: e.memset(ceps[:, 0:1], 1e-5))
        V(lambda e: e.memset(ceps[:, 1:2], 1e-6))
        V(lambda e: e.memset(ceps[:, 2:3], 1.0 / 1024))
        nc.const_aps.register(F32, 1e-5, ceps[:, 0:1])
        nc.const_aps.register(F32, 1e-6, ceps[:, 1:2])
        nc.const_aps.register(F32, 1.0 / 1024, ceps[:, 2:3])
        ve_cw = s_ve.n
        w.wait("tensor", s_ve, ve_cw)
        w.do("tensor", lambda e: e.matmul(pb[4][:, 0:32], ones[0:1, :], cw_row[0:1, :],
                                          start=True, stop=True), (s_pe, 1))
        w.wait("vector", s_pe)
        V(lambda e: e.tensor_copy(cwrep[:], pb[4][:, 0:32]))
        ve_consts = s_ve.n
        gate_ap = grep[:, 2:3]
        temp_ap = grep[:, 3:4]
        w.wait("gpsimd", s_ve, ve_consts)
        for i in range((TRASH + 128) // 128):
            w.do("gpsimd", lambda e, i=i: e.dma_start(
                out=expbuf[i * 128:(i + 1) * 128, :], in_=zt[:]), (s_gp, 16))
        zf_done = s_gp.n

        if PH < 1:
            w.wait("gpsimd", s_gp, s_gp.n)
            w.emit(block)
            return nc
        A.reset(p0m)
        p1m = A.mark()
        xg = [A.alloc(4096), A.alloc(4096)]
        xfm = A.alloc(4096)
        rawg = A.alloc(512)
        rawtm = A.alloc(48)
        sqd = A.alloc(1024)
        tA = A.alloc(32)
        tB = A.alloc(32)
        tC = A.alloc(32)
        mg = A.alloc(16)
        fng = A.alloc(32)
        fog = A.alloc(64)
        feg = A.alloc(16)
        CNT = 6
        ld_grp = [None] * NGRP
        grp_free = [None] * NGRP
        evac_hist = []

        def p1_load(g):
            if grp_free[g - 2] is not None if g >= 2 else False:
                sem, val = grp_free[g - 2]
                w.wait("sync", sem, val)
            if g >= 2 and g - 2 < GPR:
                w.wait("sync", s_ac, ac_own[g - 2])
            w.do("sync", lambda e, g=g: e.dma_start(
                out=xg[g % 2].rearrange("p (s d) -> p s d", d=D),
                in_=x_full[g * 512:(g + 1) * 512, :].rearrange("(s p) d -> p s d", p=128)),
                (s_ld, 16))
            ld_grp[g] = s_ld.n

        ac_own = [0] * NGRP
        xfv = xfm.rearrange("p (c n) -> p c n", n=512)
        w.wait("sync", s_pe, s_pe.n)
        w.wait("sync", s_ve, s_ve.n)
        for g in range(NGRP):
            own = g < GPR
            if g < 2:
                p1_load(g)
            xgv = xg[g % 2].rearrange("p (s d) -> p s d", d=D)
            w.wait("tensor", s_ld, ld_grp[g])
            for h in range(8 if P1S >= 2 else 0):
                sub, st_ = h // 2, h % 2
                bank = h % 4
                k = len(evac_hist)
                if k >= 4:
                    sem, val = evac_hist[k - 4]
                    w.wait("tensor", sem, val)
                for j in range(4):
                    fc = st_ * 4 + j
                    w.do("tensor", lambda e, bank=bank, j=j, sub=sub, fc=fc, xgv=xgv:
                         e.transpose(pb[bank][:, j * 128:(j + 1) * 128],
                                     xgv[:, sub, fc * 128:(fc + 1) * 128], ident))
                w.mark("tensor", s_pe, 1)
                if h % 2 == 0 or KEVAC == "v":
                    w.wait("vector", s_pe)
                    V(lambda e, bank=bank, st_=st_, sub=sub:
                      e.tensor_copy(xfv[:, st_ * 4:st_ * 4 + 4, sub * 128:(sub + 1) * 128],
                                    pb[bank].rearrange("p (j n) -> p j n", n=128)))
                    evac_hist.append((s_ve, s_ve.n))
                else:
                    w.wait("scalar", s_pe)
                    S(lambda e, bank=bank, st_=st_, sub=sub:
                      e.copy(xfv[:, st_ * 4:st_ * 4 + 4, sub * 128:(sub + 1) * 128],
                             pb[bank].rearrange("p (j n) -> p j n", n=128)))
                    evac_hist.append((s_ac, s_ac.n))
            pe_tr = s_pe.n
            if own and P1S >= 2 and KSSQ:
                w.wait("scalar", s_ld, ld_grp[g])
                for sub in range(4):
                    t = g * 4 + sub
                    S(lambda e, sub=sub, t=t, xgv=xgv:
                      e.activation(sqd[:], xgv[:, sub, :], ACT.Square,
                                   accum_out=ssqbuf[:, t:t + 1]))
            ac_own[g] = s_ac.n
            grp_free[g] = (s_pe, pe_tr) if P1S >= 2 else None
            if g + 2 < NGRP:
                p1_load(g + 2)
            if P1S < 3:
                continue
            for kk in range(8):
                sem, val = evac_hist[len(evac_hist) - 8 + kk]
                w.wait("tensor", sem, val)
            for fc in range(DC):
                w.do("tensor", lambda e, fc=fc: e.matmul(
                    pb[4][0:9, :], wgext[:, fc * 9:(fc + 1) * 9], xfv[:, fc, :],
                    start=(fc == 0), stop=(fc == DC - 1)))
            w.mark("tensor", s_pe, 1)
            w.wait("vector", s_pe)
            VE(lambda e: e.tensor_copy(rawg[0:9, :], pb[4][0:9, :]))
            if P1S < 4:
                continue
            w.wait("tensor", s_ve)
            for j in range(4):
                w.do("tensor", lambda e, j=j: e.transpose(
                    pb[5][:, j * 9:(j + 1) * 9], rawg[0:9, j * 128:(j + 1) * 128],
                    ident[0:9, 0:9]))
            w.mark("tensor", s_pe, 1)
            w.wait("vector", s_pe)
            VE(lambda e: e.tensor_copy(rawtm[:, 0:36], pb[5][:, 0:36]))
            r9 = rawtm[:, 0:36].rearrange("p (t n) -> p t n", n=9)
            raw8 = r9[:, :, 0:8]
            mcol = r9[:, :, 8:9]
            negs = negbuf[:, g * 32:(g + 1) * 32] if own else fng
            ohg = ohbuf[:, g * 64:(g + 1) * 64] if own else fog
            if own:
                e1d = e12[:, g * 4:(g + 1) * 4]
                e2d = e12[:, 64 + g * 4:64 + (g + 1) * 4]
            else:
                e1d = feg[:, 0:4]
                e2d = feg[:, 4:8]
            if own:
                V(lambda e, g=g, r9=r9: e.tensor_copy(
                    mbuf[:, g * 4:(g + 1) * 4],
                    r9[:, :, 8:9].rearrange("p t o -> p (t o)")))
            V(lambda e, mcol=mcol: e.tensor_tensor(
                W8(tA), W8(cwrep), mcol.to_broadcast([128, 4, 8]), op=ALU.mult))
            V(lambda e, raw8=raw8, negs=negs: e.tensor_tensor(
                W8(negs), W8(tA), raw8, op=ALU.subtract))
            V(lambda e, negs=negs: e.tensor_reduce(
                mg[:, 0:4], W8(negs), axis=AX.X, op=ALU.min))
            V(lambda e, negs=negs: e.tensor_tensor(
                W8(tB), W8(negs), bc(mg[:, 0:4], [128, 4, 8]), op=ALU.is_equal))
            V(lambda e: e.tensor_tensor(W8(tC), W8(tB), W8(iotar), op=ALU.mult))
            V(lambda e, e1d=e1d: e.tensor_reduce(e1d, W8(tC), axis=AX.X, op=ALU.max))
            oh1 = W16(ohg)[:, :, 0:8]
            oh2 = W16(ohg)[:, :, 8:16]
            V(lambda e, oh1=oh1, e1d=e1d: e.tensor_tensor(
                oh1, W8(iotar), e1d[:, :, None].to_broadcast([128, 4, 8]),
                op=ALU.is_equal))
            V(lambda e, oh1=oh1, negs=negs: e.scalar_tensor_tensor(
                W8(tA), oh1, 1e30, W8(negs), op0=ALU.mult, op1=ALU.add))
            V(lambda e: e.tensor_reduce(mg[:, 4:8], W8(tA), axis=AX.X, op=ALU.min))
            V(lambda e: e.tensor_tensor(
                W8(tB), W8(tA), bc(mg[:, 4:8], [128, 4, 8]), op=ALU.is_equal))
            V(lambda e: e.tensor_tensor(W8(tC), W8(tB), W8(iotar), op=ALU.mult))
            V(lambda e, e2d=e2d: e.tensor_reduce(e2d, W8(tC), axis=AX.X, op=ALU.max))
            V(lambda e, oh2=oh2, e2d=e2d: e.tensor_tensor(
                oh2, W8(iotar), e2d[:, :, None].to_broadcast([128, 4, 8]),
                op=ALU.is_equal))
            ve_oh = s_ve.n
            if P1S < 6:
                continue
            r = g // GPR
            w.wait("tensor", s_ve, ve_oh)
            w.do("tensor", lambda e, r=r, g=g, ohg=ohg: e.matmul(
                pb[CNT][0:1, r * 64:(r + 1) * 64], ones[:, 0:1], ohg,
                start=(g % GPR == 0), stop=(g % GPR == GPR - 1)), (s_pe, 1))
            if not own:
                w.wait("vector", s_pe)
        p1_pe = s_pe.n
        w.wait("vector", s_pe, p1_pe)
        if P1S >= 6:
            V(lambda e: e.tensor_reduce(
                cnts8[0:1, :].rearrange("p (r ke) -> p r ke", ke=16),
                pb[CNT][0:1, :].rearrange("p (r t ke) -> p r t ke", t=4, ke=16)
                .transpose([0, 1, 3, 2]),
                axis=AX.X, op=ALU.add))
        barrier()

        if PH < 2:
            if dbg_arena is not None:
                w.do("gpsimd", lambda e: e.dma_start(out=dbg_arena, in_=AP_[:]), (s_gp, 16))
                w.do("gpsimd", lambda e: e.dma_start(out=dbg_i, in_=AI_[:]), (s_gp, 16))
            w.wait("gpsimd", s_gp, s_gp.n)
            w.emit(block)
            return nc
        A.reset(p1m)
        p2m = A.mark()
        lbuf = A.alloc(512)
        pbuf = A.alloc(512)
        qbuf = A.alloc(512)
        s64a = A.alloc(64)
        s64b = A.alloc(64)
        s64c = A.alloc(64)
        locG = A.alloc(1024)
        locB = A.alloc(1024)
        selG = A.alloc(128)
        selB = A.alloc(128)
        kbuf = A.alloc(128)
        kBbuf = A.alloc(128)
        t128a = A.alloc(128)
        t128b = A.alloc(128)
        trash_c = A.alloc(128)
        tot64 = A.alloc(16)
        tprow = A.alloc(1024)

        V(lambda e: e.tensor_tensor(t128a[0:1, :], cnts8[0:1, :], cm_row[0:1, :], op=ALU.mult))
        V(lambda e: e.tensor_reduce(
            base_row[0:1, :].rearrange("p (o ke) -> p o ke", o=1),
            t128a[0:1, :].rearrange("p (r ke) -> p r ke", ke=16).transpose([0, 2, 1]),
            axis=AX.X, op=ALU.add))
        V(lambda e: e.tensor_reduce(
            trow[0:1, 0:16].rearrange("p (o ke) -> p o ke", o=1),
            cnts8[0:1, :].rearrange("p (r ke) -> p r ke", ke=16).transpose([0, 2, 1]),
            axis=AX.X, op=ALU.add))
        V(lambda e: e.tensor_tensor(base_row[0:1, 8:16], base_row[0:1, 8:16],
                                    trow[0:1, 0:8], op=ALU.add))
        V(lambda e: e.tensor_tensor(t128a[0:1, :], cnts8[0:1, :], mm_row[0:1, :], op=ALU.mult))
        V(lambda e: e.tensor_reduce(
            trow[0:1, 16:32].rearrange("p (o ke) -> p o ke", o=1),
            t128a[0:1, :].rearrange("p (r ke) -> p r ke", ke=16).transpose([0, 2, 1]),
            axis=AX.X, op=ALU.add))
        V(lambda e: e.memset(k2_row[0:1, 0:8], 0.0))
        V(lambda e: e.tensor_copy(k2_row[0:1, 8:16], trow[0:1, 16:24]))
        ve_bases = s_ve.n
        w.wait("tensor", s_ve, ve_bases)
        for t in range(NT):
            b = t // 32
            w.do("tensor", lambda e, b=b, t=t: e.matmul(
                pb[b][0:1, (t % 32) * 16:(t % 32) * 16 + 16], ones[:, 0:1],
                ohbuf[:, t * 16:(t + 1) * 16],
                start=(t % 32 == 0), stop=(t % 32 == 31)),
                (s_pe, 1) if t % 32 == 31 else None)
        w.wait("vector", s_pe)
        V(lambda e: e.tensor_copy(tprow[0:1, 0:512], pb[0][0:1, :]))
        V(lambda e: e.tensor_copy(tprow[0:1, 512:1024], pb[1][0:1, :]))
        ve_tot = s_ve.n
        w.wait("gpsimd", s_ve, ve_tot)
        w.do("gpsimd", lambda e: e.dma_start(out=bounce, in_=tprow[0:1, :]), (s_gp, 16))
        w.wait("gpsimd", s_gp)
        w.do("gpsimd", lambda e: e.dma_start(out=tot64[0:64, :], in_=bounce), (s_gp, 16))
        gp_t64 = s_gp.n
        w.wait("tensor", s_gp, gp_t64)
        w.do("tensor", lambda e: e.matmul(pb[2][0:64, 0:16], tri[0:64, 0:64],
                                          tot64[0:64, :], start=True, stop=True), (s_pe, 1))
        w.wait("vector", s_pe)
        V(lambda e: e.tensor_copy(tot64[0:64, :], pb[2][0:64, 0:16]))
        ve_tp = s_ve.n
        w.wait("gpsimd", s_ve, ve_tp)
        w.do("gpsimd", lambda e: e.dma_start(out=bounce, in_=tot64[0:64, :]), (s_gp, 16))
        w.wait("gpsimd", s_gp)
        w.do("gpsimd", lambda e: e.dma_start(out=tprow[0:1, :], in_=bounce), (s_gp, 16))
        gp_tp = s_gp.n
        w.wait("tensor", s_gp, gp_tp)
        ve_loc = {}
        for t in range(NT):
            ba = (t % 2) * 2
            if t >= 2:
                w.wait("tensor", s_ve, ve_loc[t - 2])
            for bx, krow in ((ba, base_row), (ba + 1, k2_row)):
                w.do("tensor", lambda e, bx=bx, t=t: e.matmul(
                    pb[bx][:, 0:16], tri, ohbuf[:, t * 16:(t + 1) * 16],
                    start=True, stop=False))
                w.do("tensor", lambda e, bx=bx, t=t: e.matmul(
                    pb[bx][:, 0:16], ones[0:1, :], tprow[0:1, t * 16:(t + 1) * 16],
                    start=False, stop=False))
                w.do("tensor", lambda e, bx=bx, krow=krow: e.matmul(
                    pb[bx][:, 0:16], ones[0:1, :], krow[0:1, :],
                    start=False, stop=True),
                    (s_pe, 1) if bx == ba + 1 else None)
            w.wait("vector", s_pe)
            VE(lambda e, ba=ba, t=t: e.tensor_copy(locG[:, t * 16:(t + 1) * 16],
                                                   pb[ba][:, 0:16]))
            VE(lambda e, ba=ba, t=t: e.tensor_copy(locB[:, t * 16:(t + 1) * 16],
                                                   pb[ba + 1][:, 0:16]))
            ve_loc[t] = s_ve.n
        V(lambda e: e.tensor_tensor(s64a[:, :], mbuf[:, :], mbuf[:, :], op=ALU.mult))
        V(lambda e: e.tensor_scalar(s64b[:, :], ssqbuf[:, :], 1.0 / 1024, scalar2=None,
                                    op0=ALU.mult))
        V(lambda e: e.tensor_tensor(s64a[:, :], s64b[:, :], s64a[:, :], op=ALU.subtract))
        ve_var = s_ve.n
        w.wait("scalar", s_ve, ve_var)
        S(lambda e: e.activation(s64b[:, :], s64a[:, :], ACT.Sqrt, bias=1e-5))
        w.wait("vector", s_ac)
        V(lambda e: e.reciprocal(rbuf[:, :], s64b[:, :]))
        V(lambda e: e.tensor_scalar(s64c[:, :], rbuf[:, :], -1.0, scalar2=None,
                                    op0=ALU.mult))
        V(lambda e: e.tensor_tensor(W8(lbuf), W8(negbuf), bc(s64c, [128, 64, 8]),
                                    op=ALU.mult))
        V(lambda e: e.tensor_reduce(s64a[:, :], W8(lbuf), axis=AX.X, op=ALU.max))
        V(lambda e: e.tensor_tensor(W8(lbuf), W8(lbuf), bc(s64a, [128, 64, 8]),
                                    op=ALU.subtract))
        ve_l = s_ve.n
        w.wait("scalar", s_ve, ve_l)
        S(lambda e: e.activation(pbuf[:, :], lbuf[:, :], ACT.Exp))
        w.wait("vector", s_ac)
        V(lambda e: e.tensor_reduce(s64a[:, :], W8(pbuf), axis=AX.X, op=ALU.add))
        V(lambda e: e.reciprocal(s64b[:, :], s64a[:, :]))
        V(lambda e: e.tensor_tensor(W8(pbuf), W8(pbuf), bc(s64b, [128, 64, 8]),
                                    op=ALU.mult))
        V(lambda e: e.tensor_tensor(W8(qbuf), W8(pbuf), W16(ohbuf)[:, :, 0:8],
                                    op=ALU.mult))
        V(lambda e: e.tensor_reduce(gbuf[:, 0:64], W8(qbuf), axis=AX.X, op=ALU.add))
        V(lambda e: e.tensor_tensor(W8(qbuf), W8(pbuf), W16(ohbuf)[:, :, 8:16],
                                    op=ALU.mult))
        V(lambda e: e.tensor_reduce(gbuf[:, 64:128], W8(qbuf), axis=AX.X, op=ALU.add))
        V(lambda e: e.tensor_tensor(locG[:, :], locG[:, :], ohbuf[:, :], op=ALU.mult))
        V(lambda e: e.tensor_reduce(selG[:, 0:64], W16(locG)[:, :, 0:8],
                                    axis=AX.X, op=ALU.add))
        V(lambda e: e.tensor_reduce(selG[:, 64:128], W16(locG)[:, :, 8:16],
                                    axis=AX.X, op=ALU.add))
        V(lambda e: e.tensor_tensor(locB[:, :], locB[:, :], ohbuf[:, :], op=ALU.mult))
        V(lambda e: e.tensor_reduce(selB[:, 0:64], W16(locB)[:, :, 0:8],
                                    axis=AX.X, op=ALU.add))
        V(lambda e: e.tensor_reduce(selB[:, 64:128], W16(locB)[:, :, 8:16],
                                    axis=AX.X, op=ALU.add))
        V(lambda e: e.tensor_scalar(kbuf[:, :], selG[:, :], float(CCAP), scalar2=None,
                                    op0=ALU.is_lt))
        V(lambda e: e.tensor_scalar(kBbuf[:, :], selB[:, :], float(CL), scalar2=None,
                                    op0=ALU.is_lt))
        V(lambda e: e.tensor_tensor(gbuf[:, :], gbuf[:, :], kbuf[:, :], op=ALU.mult))
        V(lambda e: e.tensor_tensor(s64a[:, :], gbuf[:, 0:64], gbuf[:, 64:128],
                                    op=ALU.add))
        V(lambda e: e.tensor_scalar(s64a[:, :], s64a[:, :], 1e-9, scalar2=None,
                                    op0=ALU.max))
        V(lambda e: e.reciprocal(s64b[:, :], s64a[:, :]))
        V(lambda e: e.tensor_tensor(gbuf[:, 0:64], gbuf[:, 0:64], s64b[:, :],
                                    op=ALU.mult))
        V(lambda e: e.tensor_tensor(gbuf[:, 64:128], gbuf[:, 64:128], s64b[:, :],
                                    op=ALU.mult))
        V(lambda e: e.tensor_tensor(gbuf[:, :], gbuf[:, :], kBbuf[:, :], op=ALU.mult))
        V(lambda e: e.scalar_tensor_tensor(t128a[:, :], e12[:, :], -2560.0, selB[:, :],
                                           op0=ALU.mult, op1=ALU.add))
        V(lambda e: e.tensor_scalar(t128a[:, :], t128a[:, :], 17920.0, scalar2=None,
                                    op0=ALU.add))
        V(lambda e: e.tensor_tensor(t128a[:, :], t128a[:, :], kBbuf[:, :], op=ALU.mult))
        V(lambda e: e.tensor_copy(physG[:, :], t128a[:, :]))
        V(lambda e: e.tensor_scalar(t128b[:, :], kBbuf[:, :], -1.0, scalar2=1.0,
                                    op0=ALU.mult, op1=ALU.add))
        V(lambda e: e.scalar_tensor_tensor(t128b[:, :], t128b[:, :], float(TRASH),
                                           t128a[:, :], op0=ALU.mult, op1=ALU.add))
        V(lambda e: e.tensor_copy(physS[:, :], t128b[:, :]))
        barrier()

        if PH < 3:
            w.wait("gpsimd", s_gp, s_gp.n)
            w.emit(block)
            return nc
        A.reset(p2m)
        p3m = A.mark()
        x3 = [A.alloc(1024), A.alloc(1024)]
        xn3 = [A.alloc(1024), A.alloc(1024)]
        ld3 = [None] * NT
        ve3 = [None] * NT
        gp3 = [None] * NT

        def p3_load(t):
            if t >= 2:
                w.wait("sync", s_ve, ve3[t - 2])
            w.do("sync", lambda e, t=t: e.dma_start(
                out=x3[t % 2], in_=x_full[t * 128:(t + 1) * 128, :]), (s_ld, 16))
            ld3[t] = s_ld.n

        p3_load(0)
        p3_load(1)
        for t in range(NT):
            w.wait("vector", s_ld, ld3[t])
            if t >= 2:
                w.wait("vector", s_gp, gp3[t - 2])
            V(lambda e, t=t: e.tensor_scalar(
                xn3[t % 2], x3[t % 2], mbuf[:, t:t + 1], scalar2=rbuf[:, t:t + 1],
                op0=ALU.subtract, op1=ALU.mult))
            ve3[t] = s_ve.n
            if t + 2 < NT:
                p3_load(t + 2)
            w.wait("gpsimd", s_ve, ve3[t])
            if t == 0:
                w.wait("gpsimd", s_gp, zf_done)
            for k in range(2):
                w.do("gpsimd", lambda e, t=t, k=k: e.indirect_dma_start(
                    out=expbuf,
                    out_offset=IndirectOffsetOnAxis(
                        ap=physS[:, k * 64 + t:k * 64 + t + 1], axis=0),
                    in_=xn3[t % 2], in_offset=None), (s_gp, 16))
            gp3[t] = s_gp.n
        barrier()

        if PH < 4:
            w.wait("gpsimd", s_gp, s_gp.n)
            w.emit(block)
            return nc
        A.reset(p3m)
        p4m = A.mark()
        wkv_sb = A.alloc(4096)
        wo_sb = A.alloc(4096)
        wq_sb = A.alloc(2048)
        hqb = A.alloc(4096)
        khat = A.alloc(8192)
        vext = A.alloc(8224)
        qhat = A.alloc(1024)
        nsc = A.alloc(512)
        nrm = A.alloc(512)
        attn_tm = A.alloc(4096)
        attnfm = A.alloc(4096)
        x4 = A.alloc(1024)
        s4 = A.alloc(1024)
        sc1 = A.alloc(16)
        vxv = vext.rearrange("p (t n) -> p t n", n=514)
        V(lambda e: e.memset(vxv[:, :, 256:257], 1.0))
        V(lambda e: e.memset(vxv[:, :, 513:514], 1.0))
        ve_vone = s_ve.n
        hv = hqb.rearrange("p (c n) -> p c n", n=512)
        wkv = wkv_sb.rearrange("p (c n) -> p c n", n=512)
        wvv = wkv
        wqv = wq_sb.rearrange("p (c n) -> p c n", n=256)
        wov = wo_sb.rearrange("p (c n) -> p c n", n=512)
        afv = attnfm.rearrange("p (c n) -> p c n", n=512)
        sfv = attn_tm.rearrange("p (c n) -> p c n", n=512)
        ve_x4 = 0
        pe_wq = 0
        pe_h = 0
        pe_wo = 0
        pe_wv = 0
        gp_h = 0
        for sq in range(NSEQ):
            for qb in range(NSEQ):
                tb = sq * 4 + qb
                w.wait("vector", s_gp, gp_h)
                for sub in range(4):
                    t = tb * 4 + sub
                    w.wait("sync", s_ve, ve_x4)
                    w.do("sync", lambda e, t=t: e.dma_start(
                        out=x4, in_=x_full[t * 128:(t + 1) * 128, :]), (s_ld, 16))
                    ldx = s_ld.n
                    w.wait("vector", s_ld, ldx)
                    V(lambda e, t=t: e.tensor_scalar(
                        s4, x4, mbuf[:, t:t + 1], scalar2=rbuf[:, t:t + 1],
                        op0=ALU.subtract, op1=ALU.mult))
                    ve_xn = s_ve.n
                    ve_x4 = ve_xn
                    w.wait("scalar", s_ve, ve_xn)
                    S(lambda e: e.activation(nsc[:, 0:512], s4[:, 0:512], ACT.Square,
                                             accum_out=sc1[:, 0:1]))
                    S(lambda e: e.activation(nsc[:, 0:512], s4[:, 512:1024], ACT.Square,
                                             accum_out=sc1[:, 1:2]))
                    w.wait("vector", s_ac)
                    V(lambda e: e.tensor_tensor(sc1[:, 2:3], sc1[:, 0:1], sc1[:, 1:2],
                                                op=ALU.add))
                    ve_s = s_ve.n
                    w.wait("scalar", s_ve, ve_s)
                    S(lambda e: e.activation(sc1[:, 3:4], sc1[:, 2:3], ACT.Sqrt,
                                             scale=1.0 / 1024, bias=1e-6))
                    w.wait("vector", s_ac)
                    V(lambda e: e.reciprocal(sc1[:, 4:5], sc1[:, 3:4]))
                    V(lambda e: e.tensor_scalar(s4, s4, sc1[:, 4:5], scalar2=None,
                                                op0=ALU.mult))
                    V(lambda e: e.tensor_tensor(s4, s4, nwrep, op=ALU.mult))
                    ve_h2 = s_ve.n
                    w.wait("tensor", s_ve, ve_h2)
                    for st_ in range(2):
                        bank = 6 + st_
                        for j in range(4):
                            fc = st_ * 4 + j
                            w.do("tensor", lambda e, bank=bank, j=j, fc=fc:
                                 e.transpose(pb[bank][:, j * 128:(j + 1) * 128],
                                             s4[:, fc * 128:(fc + 1) * 128], ident))
                        w.mark("tensor", s_pe, 1)
                        w.wait("vector", s_pe)
                        VE(lambda e, bank=bank, st_=st_, sub=sub: e.tensor_copy(
                            hv[:, st_ * 4:st_ * 4 + 4, sub * 128:(sub + 1) * 128],
                            pb[bank].rearrange("p (j n) -> p j n", n=128)))
                        w.wait("tensor", s_ve)
                    pe_h = s_pe.n
                    w.wait("vector", s_pe, pe_h)
                ve_hd = s_ve.n
                w.wait("gpsimd", s_ve, ve_hd)
                w.do("gpsimd", lambda e, tb=tb: e.dma_start(
                    out=h_stage[tb * 1024:(tb + 1) * 1024, :]
                    .rearrange("(c p) n -> p c n", p=128),
                    in_=hv), (s_gp, 16))
                gp_h = s_gp.n
                if P4S < 2:
                    continue
                w.wait("sync", s_pe, pe_wv)
                w.do("sync", lambda e: e.dma_start(
                    out=wkv, in_=wk_d.rearrange("(c p) n -> p c n", p=128)), (s_ld, 16))
                ld_wk = s_ld.n
                w.wait("tensor", s_ld, ld_wk)
                for kvh in range(2):
                    for c in range(2):
                        kd = kvh * 2 + c
                        for fc in range(DC):
                            w.do("tensor", lambda e, kd=kd, fc=fc, c=c: e.matmul(
                                pb[c][:, :], wkv[:, fc, kd * 128:(kd + 1) * 128],
                                hv[:, fc, :], start=(fc == 0), stop=(fc == DC - 1)))
                        w.mark("tensor", s_pe, 1)
                        w.wait("vector", s_pe)
                        VE(lambda e, kd=kd, qb=qb, c=c: e.tensor_copy(
                            khat[:, kd * 2048 + qb * 512: kd * 2048 + (qb + 1) * 512],
                            pb[c][:, :]))
                        if P4K >= 2:
                            w.wait("scalar", s_ve)
                            S(lambda e, kd=kd, qb=qb: e.activation(
                                nsc[:, 0:512],
                                khat[:, kd * 2048 + qb * 512: kd * 2048 + (qb + 1) * 512],
                                ACT.Square))
                            w.wait("tensor", s_ac)
                            w.do("tensor", lambda e, c=c: e.matmul(
                                pb[2][:, :], ones, nsc[:, 0:512],
                                start=(c == 0), stop=(c == 1)),
                                (s_pe, 1) if c == 1 else None)
                    if P4K >= 2:
                        w.wait("scalar", s_pe)
                        S(lambda e: e.activation(nrm[:, 0:512], pb[2][:, :], ACT.Sqrt))
                        w.wait("vector", s_ac)
                        V(lambda e: e.tensor_scalar(nrm[:, 0:512], nrm[:, 0:512], 1e-12,
                                                    scalar2=None, op0=ALU.max))
                        V(lambda e: e.reciprocal(nrm[:, 0:512], nrm[:, 0:512]))
                        for c in range(2):
                            kd = kvh * 2 + c
                            V(lambda e, kd=kd, qb=qb: e.tensor_tensor(
                                khat[:, kd * 2048 + qb * 512: kd * 2048 + (qb + 1) * 512],
                                khat[:, kd * 2048 + qb * 512: kd * 2048 + (qb + 1) * 512],
                                nrm[:, 0:512], op=ALU.mult))
                    ve_k = s_ve.n
                    w.wait("tensor", s_ve, ve_k)
                if P4K < 3:
                    continue
                w.wait("sync", s_pe, s_pe.n)
                w.do("sync", lambda e: e.dma_start(
                    out=wkv, in_=wv_d.rearrange("(c p) n -> p c n", p=128)), (s_ld, 16))
                ld_wv = s_ld.n
                w.wait("tensor", s_ld, ld_wv)
                if sq == 0 and qb == 0:
                    w.wait("tensor", s_ve, ve_vone)
                for sub in range(4):
                    ti = qb * 4 + sub
                    for fc in range(DC):
                        w.do("tensor", lambda e, sub=sub, fc=fc: e.matmul(
                            pb[3][:, :], hv[:, fc, sub * 128:(sub + 1) * 128],
                            wvv[:, fc, :], start=(fc == 0), stop=(fc == DC - 1)))
                    w.mark("tensor", s_pe, 1)
                    w.wait("vector", s_pe)
                    VE(lambda e, ti=ti: e.tensor_copy(
                        vext[:, ti * 514:ti * 514 + 256], pb[3][:, 0:256]))
                    VE(lambda e, ti=ti: e.tensor_copy(
                        vext[:, ti * 514 + 257:ti * 514 + 513], pb[3][:, 256:512]))
                    ve_v = s_ve.n
                    w.wait("tensor", s_ve, ve_v)
                pe_wv = s_pe.n
                w.wait("vector", s_pe, s_pe.n)
            if P4S < 3:
                continue
            for qb in range(NSEQ):
                tb = sq * 4 + qb
                w.wait("sync", s_pe, pe_wq)
                w.wait("sync", s_gp, gp_h)
                w.do("sync", lambda e, tb=tb: e.dma_start(
                    out=hqb.rearrange("p (c n) -> p c n", n=512),
                    in_=h_stage[tb * 1024:(tb + 1) * 1024, :]
                    .rearrange("(c p) n -> p c n", p=128)), (s_ld, 16))
                ld_h = s_ld.n
                for head in range(NHEAD):
                    kvh = head // 2
                    w.wait("sync", s_pe, pe_wq)
                    w.do("sync", lambda e, head=head: e.dma_start(
                        out=wqv,
                        in_=wq_d.rearrange("(c p) n -> p c n", p=128)
                        [:, :, head * 256:(head + 1) * 256]), (s_ld, 16))
                    ld_q = s_ld.n
                    w.wait("tensor", s_ld, ld_q)
                    if head == 0:
                        w.wait("tensor", s_ld, ld_h)
                    for c in range(2):
                        for fc in range(DC):
                            w.do("tensor", lambda e, c=c, fc=fc: e.matmul(
                                pb[6 + c][:, :], wqv[:, fc, c * 128:(c + 1) * 128],
                                hv[:, fc, :], start=(fc == 0), stop=(fc == DC - 1)))
                        w.mark("tensor", s_pe, 1)
                        w.wait("vector", s_pe)
                        VE(lambda e, c=c: e.tensor_copy(qhat[:, c * 512:(c + 1) * 512],
                                                        pb[6 + c][:, :]))
                        w.wait("scalar", s_ve)
                        S(lambda e, c=c: e.activation(nsc[:, 0:512],
                                                      qhat[:, c * 512:(c + 1) * 512],
                                                      ACT.Square))
                        w.wait("tensor", s_ac)
                        w.do("tensor", lambda e, c=c: e.matmul(
                            pb[0][:, :], ones, nsc[:, 0:512],
                            start=(c == 0), stop=(c == 1)),
                            (s_pe, 1) if c == 1 else None)
                    pe_wq = s_pe.n
                    w.wait("scalar", s_pe, pe_wq)
                    S(lambda e: e.activation(nrm[:, 0:512], pb[0][:, :], ACT.Sqrt))
                    w.wait("vector", s_ac)
                    V(lambda e: e.tensor_scalar(nrm[:, 0:512], nrm[:, 0:512], 1e-12,
                                                scalar2=None, op0=ALU.max))
                    V(lambda e: e.reciprocal(nrm[:, 0:512], nrm[:, 0:512]))
                    for c in range(2):
                        V(lambda e, c=c: e.tensor_tensor(
                            qhat[:, c * 512:(c + 1) * 512],
                            qhat[:, c * 512:(c + 1) * 512],
                            nrm[:, 0:512], op=ALU.mult))
                    ve_q = s_ve.n
                    w.wait("tensor", s_ve, ve_q)
                    if P4S < 4:
                        continue
                    nkt = 4 * qb + 4
                    ac_exp = [0] * nkt
                    for kt in range(nkt):
                        sb_ = kt % 2
                        if kt >= 2:
                            w.wait("tensor", s_ac, ac_exp[kt - 2])
                        for c in range(2):
                            kd = kvh * 2 + c
                            w.do("tensor", lambda e, sb_=sb_, kd=kd, kt=kt, c=c: e.matmul(
                                pb[sb_][:, :],
                                khat[:, kd * 2048 + kt * 128: kd * 2048 + (kt + 1) * 128],
                                qhat[:, c * 512:(c + 1) * 512],
                                start=(c == 0), stop=(c == 1)),
                                (s_pe, 1) if c == 1 else None)
                        w.wait("scalar", s_pe)
                        S(lambda e, sb_=sb_: e.activation(
                            nrm[:, 0:512], pb[sb_][:, :], ACT.Exp, scale=temp_ap))
                        ac_exp[kt] = s_ac.n
                        if kt >= 4 * qb:
                            j = kt - 4 * qb
                            w.wait("vector", s_ac, ac_exp[kt])
                            V(lambda e, j=j: e.tensor_tensor(
                                nrm[:, 0:512], nrm[:, 0:512],
                                maskd[:, j * 512:(j + 1) * 512], op=ALU.mult))
                            w.wait("tensor", s_ve)
                        else:
                            w.wait("tensor", s_ac, ac_exp[kt])
                        for qs in range(4):
                            w.do("tensor", lambda e, qs=qs, kt=kt, kvh=kvh, nkt=nkt:
                                 e.matmul(
                                     pb[2 + qs][:, 0:257],
                                     nrm[:, qs * 128:(qs + 1) * 128],
                                     vxv[:, kt, kvh * 257:(kvh + 1) * 257],
                                     start=(kt == 0), stop=(kt == nkt - 1)),
                                 (s_pe, 1) if (kt == nkt - 1 and qs == 3) else None)
                        w.wait("scalar", s_pe, s_pe.n)
                    w.wait("vector", s_pe)
                    for qs in range(4):
                        VE(lambda e, qs=qs: e.reciprocal(sc1[:, 5:6],
                                                         pb[2 + qs][:, 256:257]))
                        V(lambda e, qs=qs, head=head: e.tensor_scalar(
                            attn_tm[:, qs * 1024 + head * 256:
                                    qs * 1024 + (head + 1) * 256],
                            pb[2 + qs][:, 0:256], sc1[:, 5:6], scalar2=None,
                            op0=ALU.mult))
                    ve_pv = s_ve.n
                    w.wait("tensor", s_ve, ve_pv)
                if P4S < 5:
                    continue
                for qs in range(4):
                    for st_ in range(2):
                        bank = 6 + st_
                        for j in range(4):
                            adc = st_ * 4 + j
                            w.do("tensor", lambda e, bank=bank, j=j, qs=qs, adc=adc:
                                 e.transpose(pb[bank][:, j * 128:(j + 1) * 128],
                                             attn_tm[:, qs * 1024 + adc * 128:
                                                     qs * 1024 + (adc + 1) * 128],
                                             ident))
                        w.mark("tensor", s_pe, 1)
                        w.wait("vector", s_pe)
                        VE(lambda e, bank=bank, st_=st_, qs=qs: e.tensor_copy(
                            afv[:, st_ * 4:st_ * 4 + 4, qs * 128:(qs + 1) * 128],
                            pb[bank].rearrange("p (j n) -> p j n", n=128)))
                        w.wait("tensor", s_ve)
                ve_wo = {}
                for half in range(2):
                    w.wait("sync", s_pe, pe_wo)
                    w.do("sync", lambda e, half=half: e.dma_start(
                        out=wov,
                        in_=wo_d.rearrange("(c p) n -> p c n", p=128)
                        [:, :, half * 512:(half + 1) * 512]), (s_ld, 16))
                    ld_wo = s_ld.n
                    w.wait("tensor", s_ld, ld_wo)
                    for oc in range(4):
                        b2 = 2 + (oc % 2)
                        if oc >= 2:
                            w.wait("tensor", s_ve, ve_wo[oc - 2])
                        for adc in range(DC):
                            w.do("tensor", lambda e, b2=b2, adc=adc, oc=oc: e.matmul(
                                pb[b2][:, :], wov[:, adc, oc * 128:(oc + 1) * 128],
                                afv[:, adc, :], start=(adc == 0), stop=(adc == DC - 1)))
                        w.mark("tensor", s_pe, 1)
                        w.wait("vector", s_pe)
                        VE(lambda e, b2=b2, half=half, oc=oc: e.tensor_copy(
                            sfv[:, half * 4 + oc, :], pb[b2][:, :]))
                        ve_wo[oc] = s_ve.n
                    pe_wo = s_pe.n
                w.wait("tensor", s_ve, s_ve.n)
                gp_s = 0
                for sub in range(4):
                    t = tb * 4 + sub
                    w.wait("sync", s_ve, ve_x4)
                    w.do("sync", lambda e, t=t: e.dma_start(
                        out=x4, in_=x_full[t * 128:(t + 1) * 128, :]), (s_ld, 16))
                    ld_x = s_ld.n
                    if sub > 0:
                        w.wait("vector", s_gp, gp_s)
                    for st_ in range(2):
                        bank = 6 + st_
                        for j in range(4):
                            oc2 = st_ * 4 + j
                            w.do("tensor", lambda e, bank=bank, j=j, oc2=oc2, sub=sub:
                                 e.transpose(pb[bank][:, j * 128:(j + 1) * 128],
                                             sfv[:, oc2, sub * 128:(sub + 1) * 128],
                                             ident))
                        w.mark("tensor", s_pe, 1)
                        w.wait("vector", s_pe)
                        VE(lambda e, bank=bank, st_=st_: e.tensor_copy(
                            s4[:, st_ * 512:(st_ + 1) * 512]
                            .rearrange("p (j n) -> p j n", n=128),
                            pb[bank].rearrange("p (j n) -> p j n", n=128)))
                        w.wait("tensor", s_ve)
                    w.wait("vector", s_ld, ld_x)
                    V(lambda e, t=t: e.tensor_scalar(
                        x4, x4, mbuf[:, t:t + 1], scalar2=rbuf[:, t:t + 1],
                        op0=ALU.subtract, op1=ALU.mult))
                    ve_x4 = s_ve.n
                    V(lambda e: e.scalar_tensor_tensor(
                        s4, s4, gate_ap, x4, op0=ALU.mult, op1=ALU.add))
                    ve_sf = s_ve.n
                    w.wait("gpsimd", s_ve, ve_sf)
                    w.do("gpsimd", lambda e, t=t: e.dma_start(
                        out=s_stage[t * 128:(t + 1) * 128, :], in_=s4), (s_gp, 16))
                    gp_s = s_gp.n
                w.wait("vector", s_gp, gp_s)
        barrier()

        if PH < 5:
            w.wait("gpsimd", s_gp, s_gp.n)
            w.emit(block)
            return nc
        A.reset(p4m)
        p5m = A.mark()
        wa_sb = A.alloc(8192)
        wg_sb = A.alloc(8192)
        wu_sb = A.alloc(8192)
        xg5 = A.alloc(4096)
        xfm5 = A.alloc(4096)
        act5 = A.alloc(4096)
        gel5 = A.alloc(512)
        o2sb = [A.alloc(512), A.alloc(512)]
        part5 = [A.alloc(512), A.alloc(512)]
        w_in_v = w_in_d.rearrange("(e c p) n -> e p c n", e=E, p=128)
        w_out_v = w_out_d.rearrange("(e c p) n -> e p c n", e=E, p=128)
        wav = wa_sb.rearrange("p (c n) -> p c n", n=1024)
        wgv = wg_sb.rearrange("p (c n) -> p c n", n=1024)
        wuv = wu_sb.rearrange("p (c n) -> p c n", n=1024)
        xfv5 = xfm5.rearrange("p (c n) -> p c n", n=512)
        xg5v = xg5.rearrange("p (s d) -> p s d", d=D)
        a5 = act5.rearrange("p (c n) -> p c n", n=512)
        pe_prev_phase = 0
        jh0_stores = 0
        o2_guard = [0, 0]
        p5_guard = [0, 0]
        for ex in range(E):
            for jh in range(2):
                w.wait("sync", s_pe, pe_prev_phase)
                w.do("sync", lambda e, ex=ex, jh=jh: e.dma_start(
                    out=wav, in_=w_in_v[ex, :, :, jh * 1024:(jh + 1) * 1024]), (s_ld, 16))
                w.do("sync", lambda e, ex=ex, jh=jh: e.dma_start(
                    out=wgv, in_=w_in_v[ex, :, :, 2048 + jh * 1024: 2048 + (jh + 1) * 1024]),
                    (s_ld, 16))
                ld_w5_l1 = s_ld.n
                w.do("sync", lambda e, ex=ex, jh=jh: e.dma_start(
                    out=wuv, in_=w_out_v[ex, :, jh * 8:(jh + 1) * 8, :]), (s_ld, 16))
                ld_w5 = s_ld.n
                if jh == 1:
                    w.wait("sync", s_gp, jh0_stores)
                for st in range(5):
                    r0 = ex * CL + st * 512
                    w.wait("sync", s_pe, pe_prev_phase if st == 0 else pe_tr5)
                    w.do("sync", lambda e, r0=r0: e.dma_start(
                        out=xg5v,
                        in_=expbuf[r0:r0 + 512, :].rearrange("(s p) d -> p s d", p=128)),
                        (s_ld, 16))
                    ld_x5 = s_ld.n
                    w.wait("tensor", s_ld, ld_x5)
                    evh = []
                    for h in range(8):
                        sub, st_ = h // 2, h % 2
                        bank = 6 + (h % 2)
                        if h >= 2:
                            sem, val = evh[h - 2]
                            w.wait("tensor", sem, val)
                        for j in range(4):
                            fc = st_ * 4 + j
                            w.do("tensor", lambda e, bank=bank, j=j, sub=sub, fc=fc:
                                 e.transpose(pb[bank][:, j * 128:(j + 1) * 128],
                                             xg5v[:, sub, fc * 128:(fc + 1) * 128],
                                             ident))
                        w.mark("tensor", s_pe, 1)
                        if h % 2 == 0:
                            w.wait("vector", s_pe)
                            VE(lambda e, bank=bank, st_=st_, sub=sub: e.tensor_copy(
                                xfv5[:, st_ * 4:st_ * 4 + 4, sub * 128:(sub + 1) * 128],
                                pb[bank].rearrange("p (j n) -> p j n", n=128)))
                            evh.append((s_ve, s_ve.n))
                        else:
                            w.wait("scalar", s_pe)
                            SE(lambda e, bank=bank, st_=st_, sub=sub: e.copy(
                                xfv5[:, st_ * 4:st_ * 4 + 4, sub * 128:(sub + 1) * 128],
                                pb[bank].rearrange("p (j n) -> p j n", n=128)))
                            evh.append((s_ac, s_ac.n))
                    pe_tr5 = s_pe.n
                    for sem, val in evh[-2:]:
                        w.wait("tensor", sem, val)
                    if st == 0:
                        w.wait("tensor", s_ld, ld_w5_l1)
                    ve_l1 = {}
                    for fo in range(8):
                        ba = (fo % 2) * 2
                        if fo >= 2:
                            w.wait("tensor", s_ve, ve_l1[fo - 2])
                        for fc in range(DC):
                            w.do("tensor", lambda e, ba=ba, fc=fc, fo=fo: e.matmul(
                                pb[ba][:, :], wav[:, fc, fo * 128:(fo + 1) * 128],
                                xfv5[:, fc, :], start=(fc == 0), stop=(fc == DC - 1)))
                        for fc in range(DC):
                            w.do("tensor", lambda e, ba=ba, fc=fc, fo=fo: e.matmul(
                                pb[ba + 1][:, :], wgv[:, fc, fo * 128:(fo + 1) * 128],
                                xfv5[:, fc, :], start=(fc == 0), stop=(fc == DC - 1)))
                        w.mark("tensor", s_pe, 1)
                        w.wait("scalar", s_pe)
                        if fo >= 1:
                            w.wait("scalar", s_ve, ve_l1[fo - 1])
                        SE(lambda e, ba=ba: e.activation(gel5[:, :], pb[ba][:, :],
                                                         ACT.Gelu))
                        w.wait("vector", s_ac)
                        VE(lambda e, ba=ba, fo=fo: e.tensor_tensor(
                            a5[:, fo, :], gel5[:, :], pb[ba + 1][:, :], op=ALU.mult))
                        ve_l1[fo] = s_ve.n
                    w.wait("tensor", s_ve, ve_l1[7])
                    w.wait("tensor", s_ve, ve_l1[6])
                    if st == 0:
                        w.wait("tensor", s_ld, ld_w5)
                    for hh in range(2):
                        for qs in range(4):
                            u = qs % 2
                            for fo in range(8):
                                w.do("tensor", lambda e, qs=qs, fo=fo, hh=hh: e.matmul(
                                    pb[qs][:, :], a5[:, fo, qs * 128:(qs + 1) * 128],
                                    wuv[:, fo, hh * 512:(hh + 1) * 512],
                                    start=(fo == 0), stop=(fo == 7)))
                            w.mark("tensor", s_pe, 1)
                            if jh == 1:
                                w.wait("sync", s_ve, p5_guard[u])
                                w.do("sync", lambda e, r0=r0, qs=qs, hh=hh, u=u:
                                     e.dma_start(
                                         out=part5[u],
                                         in_=outbuf[r0 + qs * 128: r0 + (qs + 1) * 128,
                                                    hh * 512:(hh + 1) * 512]), (s_ld, 16))
                                ld_p = s_ld.n
                            w.wait("vector", s_pe)
                            w.wait("vector", s_gp, o2_guard[u])
                            if jh == 0:
                                VE(lambda e, qs=qs, u=u: e.tensor_copy(o2sb[u][:, :],
                                                                       pb[qs][:, :]))
                            else:
                                w.wait("vector", s_ld, ld_p)
                                VE(lambda e, qs=qs, u=u: e.tensor_tensor(
                                    o2sb[u][:, :], pb[qs][:, :], part5[u][:, :],
                                    op=ALU.add))
                                p5_guard[u] = s_ve.n
                            ve_o2 = s_ve.n
                            w.wait("gpsimd", s_ve, ve_o2)
                            w.do("gpsimd", lambda e, r0=r0, qs=qs, hh=hh, u=u:
                                 e.dma_start(
                                     out=outbuf[r0 + qs * 128: r0 + (qs + 1) * 128,
                                                hh * 512:(hh + 1) * 512],
                                     in_=o2sb[u]), (s_gp, 16))
                            o2_guard[u] = s_gp.n
                        w.wait("tensor", s_ve, s_ve.n)
                    pe_prev_phase = s_pe.n
                if jh == 0:
                    jh0_stores = s_gp.n
        barrier()

        if PH < 6:
            w.wait("gpsimd", s_gp, s_gp.n)
            w.emit(block)
            return nc
        A.reset(p5m)
        rowA = A.alloc(1024)
        rowB = A.alloc(1024)
        t2b = A.alloc(1024)
        moeb = A.alloc(1024)
        x6 = A.alloc(1024)
        s6 = A.alloc(1024)
        o6 = A.alloc(1024)
        ve_o6 = 0
        gp_out = 0
        for t in range(NT):
            if t > 0:
                w.wait("gpsimd", s_ve, ve_o6)
            w.do("gpsimd", lambda e, t=t: e.indirect_dma_start(
                out=rowA, out_offset=None, in_=outbuf,
                in_offset=IndirectOffsetOnAxis(ap=physG[:, t:t + 1], axis=0)), (s_gp, 16))
            w.do("gpsimd", lambda e, t=t: e.indirect_dma_start(
                out=rowB, out_offset=None, in_=outbuf,
                in_offset=IndirectOffsetOnAxis(ap=physG[:, 64 + t:64 + t + 1], axis=0)),
                (s_gp, 16))
            gp_g = s_gp.n
            w.wait("sync", s_ve, ve_o6)
            w.do("sync", lambda e, t=t: e.dma_start(
                out=s6, in_=s_stage[t * 128:(t + 1) * 128, :]), (s_ld, 16))
            w.do("sync", lambda e, t=t: e.dma_start(
                out=x6, in_=x_full[t * 128:(t + 1) * 128, :]), (s_ld, 16))
            ld_sx = s_ld.n
            w.wait("scalar", s_gp, gp_g)
            S(lambda e, t=t: e.activation(t2b, rowB, ACT.Copy,
                                          scale=gbuf[:, 64 + t:64 + t + 1]))
            ac_t2 = s_ac.n
            w.wait("vector", s_gp, gp_g)
            w.wait("vector", s_ac, ac_t2)
            if t > 0:
                w.wait("vector", s_gp, gp_out)
            V(lambda e, t=t: e.scalar_tensor_tensor(
                moeb, rowA, gbuf[:, t:t + 1], t2b, op0=ALU.mult, op1=ALU.add))
            ve_m = s_ve.n
            w.wait("scalar", s_ve, ve_m)
            S(lambda e: e.activation(moeb, moeb, ACT.Silu))
            ac_sil = s_ac.n
            w.wait("vector", s_ac, ac_sil)
            w.wait("vector", s_ld, ld_sx)
            V(lambda e: e.tensor_tensor(o6, moeb, s6, op=ALU.mult))
            V(lambda e: e.tensor_tensor(o6, o6, x6, op=ALU.add))
            ve_o6 = s_ve.n
            w.wait("gpsimd", s_ve, ve_o6)
            w.do("gpsimd", lambda e, t=t: e.dma_start(
                out=out_d[t * 128:(t + 1) * 128, :], in_=o6), (s_gp, 16))
            gp_out = s_gp.n
        if dbg_arena is not None:
            w.do("gpsimd", lambda e: e.dma_start(out=dbg_arena, in_=AP_[:]), (s_gp, 16))
            w.do("gpsimd", lambda e: e.dma_start(out=dbg_i, in_=AI_[:]), (s_gp, 16))
        w.wait("gpsimd", s_gp, s_gp.n)

        w.emit(block)
    return nc


_NC_CACHE = None


def kernel(L0_attn_x, L0_attn_norm_w, L0_attn_Wq, L0_attn_Wk, L0_attn_Wv,
           L0_attn_Wo, L0_attn_gate, L0_attn_log_temperature,
           L0_ffn_Wg, L0_ffn_W_in, L0_ffn_W_out):
    global _NC_CACHE
    x = np.ascontiguousarray(np.asarray(L0_attn_x, np.float32).reshape(STOK, D))
    wg = np.asarray(L0_ffn_Wg, np.float32)
    wg_ext = np.ascontiguousarray(
        np.concatenate([wg, np.full((D, 1), 2.0 ** -10, np.float32)], axis=1))
    w_in = np.ascontiguousarray(np.asarray(L0_ffn_W_in, np.float32).reshape(E * D, 4096))
    w_out = np.ascontiguousarray(np.asarray(L0_ffn_W_out, np.float32).reshape(E * 2048, D))
    consts = {
        "norm_w": np.ascontiguousarray(np.asarray(L0_attn_norm_w, np.float32).reshape(1, D)),
        "wq": np.ascontiguousarray(np.asarray(L0_attn_Wq, np.float32)),
        "wk": np.ascontiguousarray(np.asarray(L0_attn_Wk, np.float32)),
        "wv": np.ascontiguousarray(np.asarray(L0_attn_Wv, np.float32)),
        "wo": np.ascontiguousarray(np.asarray(L0_attn_Wo, np.float32)),
        "gate_in": np.asarray(L0_attn_gate, np.float32).reshape(1, 1),
        "ltemp_in": np.asarray(L0_attn_log_temperature, np.float32).reshape(1, 1),
        "wg_ext": wg_ext,
        "w_in": w_in,
        "w_out": w_out,
        "c_ident": np.eye(128, dtype=np.float32),
        "c_tri": np.triu(np.ones((128, 128), np.float32), 1),
        "c_ones": np.ones((128, 128), np.float32),
        "c_iotar": np.ascontiguousarray(
            np.tile((7.0 - np.arange(8, dtype=np.float32)), (128, 4))),
        "c_maskd": np.ascontiguousarray(np.concatenate(
            [(128 * j + np.arange(128)[:, None] <= np.arange(512)[None, :])
             .astype(np.float32) for j in range(4)], axis=1)),
    }
    in_maps = []
    for c in range(NCORES):
        xr = np.ascontiguousarray(np.roll(x, -c * TPC, axis=0))
        gr = (c + np.arange(8)) % 8
        m = dict(consts)
        m["x_full"] = xr
        m["c_coremask"] = np.ascontiguousarray(
            np.repeat((gr < c).astype(np.float32), 16).reshape(1, 128))
        m["c_mymask"] = np.ascontiguousarray(
            np.repeat((np.arange(8) == 0).astype(np.float32), 16).reshape(1, 128))
        in_maps.append(m)
    if _NC_CACHE is None:
        _NC_CACHE = build_nc()
    res = run_bass_kernel_spmd(_NC_CACHE, in_maps, core_ids=list(range(NCORES)))
    outs = [res.results[c]["out"] for c in range(NCORES)]
    return np.concatenate(outs, axis=0).reshape(32, L, D)


# version 08 (final)
# speedup vs baseline: 1.0486x; 1.0013x over previous
import os
import numpy as np
from contextlib import ExitStack

import concourse.bass as bass
import concourse.mybir as mybir
from concourse.bass_utils import run_bass_kernel_spmd
from concourse.bass_core_rust import IndirectOffsetOnAxis

F32 = mybir.dt.float32
I32 = mybir.dt.int32
ALU = mybir.AluOpType
AX = mybir.AxisListType
ACT = mybir.ActivationFunctionType

NCORES = 8
D = 1024
DC = 8
L = 2048
NSEQ = 4
TPC = NSEQ * L
NT = TPC // 128
STOK = NCORES * TPC
NGRP = STOK // 512
GPR = NGRP // NCORES
E = 8
CL = 2560
CCAP = 20480
TRASH = E * CL
NHEAD = 4

ENGS = ("sync", "tensor", "vector", "scalar", "gpsimd")


class Sem:
    def __init__(self, handle):
        self.h = handle
        self.n = 0


class Walk:
    def __init__(self):
        self.ops = {k: [] for k in ENGS}
        self.self_sems = {}

    def do(self, eng, fn, inc=None):
        ss = self.self_sems.get(eng)
        if ss is not None:
            self.ops[eng].append([lambda e, h=ss.h, v=ss.n: e.wait_ge(h, v),
                                  None, True])
            if inc is None:
                inc = (ss, 1)
        if inc is not None:
            inc[0].n += inc[1]
        self.ops[eng].append([fn, inc, False])

    def wait(self, eng, sem, val=None):
        v = sem.n if val is None else val
        self.ops[eng].append([lambda e, h=sem.h, v=v: e.wait_ge(h, v), None, True])

    def mark(self, eng, sem, k=1):
        for op in reversed(self.ops[eng]):
            if not op[2]:
                if op[1] is None:
                    op[1] = (sem, k)
                    sem.n += k
                    return True
                if op[1][0] is sem:
                    return False
                raise AssertionError(f"mark({eng}): last op has other-sem inc")
        raise AssertionError(f"mark({eng}): no real op to mark")

    def emit(self, block):
        for eng in ENGS:
            ops = self.ops[eng]

            def mk(ops):
                def f(e):
                    for fn, inc, _ in ops:
                        r = fn(e)
                        if inc is not None:
                            r.then_inc(inc[0].h, inc[1])
                return f
            getattr(block, eng)(mk(ops))


class Arena:
    def __init__(self, t, words):
        self.t = t
        self.words = words
        self.off = 0

    def alloc(self, w_, align=16):
        o = (self.off + align - 1) // align * align
        assert o + w_ <= self.words, f"arena overflow {o}+{w_}>{self.words}"
        self.off = o + w_
        return self.t[:, o:o + w_]

    def mark(self):
        return self.off

    def reset(self, m):
        self.off = m


def build_nc():
    PH = int(os.environ.get("K_PHASES", "9"))
    P1S = int(os.environ.get("K_P1SUB", "9"))
    P4S = int(os.environ.get("K_P4SUB", "9"))
    P4K = int(os.environ.get("K_P4K", "9"))
    KSSQ = int(os.environ.get("K_SSQ", "1"))
    KEVAC = os.environ.get("K_EVAC", "mix")
    nc = bass.Bass("TRN2", target_bir_lowering=False, debug=False,
                   detect_race_conditions=False)

    def di(name, shape, dt=F32, kind="ExternalInput"):
        return nc.dram_tensor(name, list(shape), dt, kind=kind).ap()

    x_full = di("x_full", [STOK, D])
    norm_w = di("norm_w", [1, D])
    wq_d = di("wq", [D, D])
    wk_d = di("wk", [D, 512])
    wv_d = di("wv", [D, 512])
    wo_d = di("wo", [D, D])
    gate_in = di("gate_in", [1, 1])
    ltemp_in = di("ltemp_in", [1, 1])
    wg_ext = di("wg_ext", [D, 9])
    w_in_d = di("w_in", [E * D, 4096])
    w_out_d = di("w_out", [E * 2048, D])
    c_ident = di("c_ident", [128, 128])
    c_tri = di("c_tri", [128, 128])
    c_ones = di("c_ones", [128, 128])
    c_iotar = di("c_iotar", [128, 32])
    c_maskd = di("c_maskd", [128, 2048])
    c_coremask = di("c_coremask", [1, 128])
    c_mymask = di("c_mymask", [1, 128])
    out_d = di("out", [TPC, D], kind="ExternalOutput")
    DBG = "ExternalOutput" if os.environ.get("K_DEBUG_OUT") else "Internal"
    expbuf = di("expbuf", [TRASH + 128, D], kind=DBG)
    outbuf = di("outbuf", [TRASH, D], kind=DBG)
    s_stage = di("s_stage", [TPC, D], kind=DBG)
    dbg_arena = di("dbg_arena", [128, 7424], kind=DBG) if DBG == "ExternalOutput" else None
    dbg_i = di("dbg_i", [128, 512], I32, kind=DBG) if DBG == "ExternalOutput" else None
    h_stage = di("h_stage", [16 * D, 512], kind="Internal")
    bounce = di("bounce", [64, 16], kind="Internal")

    es = ExitStack()
    w = Walk()
    with es:
        AP_ = es.enter_context(nc.sbuf_tensor("arena_p", [128, 7424], F32))
        AF_ = es.enter_context(nc.sbuf_tensor("arena_f", [128, 45000], F32))
        AI_ = es.enter_context(nc.sbuf_tensor("arena_i", [128, 512], I32))
        P = Arena(AP_, 7424)
        A = Arena(AF_, 45000)
        AI = Arena(AI_, 512)
        pb = [es.enter_context(nc.psum_tensor(f"pb{i}", [128, 512], F32))
              for i in range(8)]
        s_ld = Sem(es.enter_context(nc.semaphore("s_ld")))
        s_gp = Sem(es.enter_context(nc.semaphore("s_gp")))
        s_pe = Sem(es.enter_context(nc.semaphore("s_pe")))
        s_ve = Sem(es.enter_context(nc.semaphore("s_ve")))
        s_ac = Sem(es.enter_context(nc.semaphore("s_ac")))
        s_gb = Sem(es.enter_context(nc.semaphore("s_gb")))
        ALLS = (s_ld, s_gp, s_pe, s_ve, s_ac, s_gb)
        w.self_sems = {"vector": s_ve, "scalar": s_ac}
        block = es.enter_context(nc.Block())

        ident = P.alloc(128)
        tri = P.alloc(128)
        ones = P.alloc(128)
        iotar = P.alloc(32)
        maskd = P.alloc(2048)
        wgext = P.alloc(80)
        nwrep = P.alloc(D)
        cwrep = P.alloc(32)
        grep = P.alloc(16)
        cm_row = P.alloc(128)
        mm_row = P.alloc(128)
        cnts8 = P.alloc(128)
        base_row = P.alloc(16)
        k2_row = P.alloc(16)
        negbuf = P.alloc(512)
        ohbuf = P.alloc(1024)
        ecat = P.alloc(128)
        e12 = P.alloc(128)
        mbuf = P.alloc(64)
        ssqbuf = P.alloc(64)
        rbuf = P.alloc(64)
        gbuf = P.alloc(128)
        trow = P.alloc(32)
        physS = AI.alloc(128)
        physG = AI.alloc(128)
        assert P.off <= 7424

        def W8(v):
            return v.rearrange("p (t e) -> p t e", e=8)

        def W16(v):
            return v.rearrange("p (t e) -> p t e", e=16)

        def bc(v, shape):
            return v[:, :, None].to_broadcast(shape)

        def V(fn):
            w.do("vector", fn, (s_ve, 1))

        def S(fn):
            w.do("scalar", fn, (s_ac, 1))

        def barrier():
            w.mark("tensor", s_pe, 1)
            w.mark("vector", s_ve, 1)
            w.mark("scalar", s_ac, 1)
            w.mark("gpsimd", s_gp, 16)
            tgt = [s.n for s in ALLS]
            for eng in ENGS:
                for s, v in zip(ALLS, tgt):
                    w.wait(eng, s, v)

        tr_evac = {"k": 0}

        def transp8(src, dstv, sub, banks, evac_engines=("vector", "scalar")):
            for st_ in range(2):
                bank = banks[st_]
                for j in range(4):
                    fc = st_ * 4 + j
                    w.do("tensor", lambda e, bank=bank, j=j, fc=fc, src=src:
                         e.transpose(pb[bank][:, j * 128:(j + 1) * 128],
                                     src[:, fc * 128:(fc + 1) * 128], ident))
                w.mark("tensor", s_pe, 1)
                eng = evac_engines[st_ % len(evac_engines)]
                sem = s_ve if eng == "vector" else s_ac
                w.wait(eng, s_pe)
                if eng == "vector":
                    w.do(eng, lambda e, bank=bank, st_=st_, sub=sub, dstv=dstv:
                         e.tensor_copy(
                             dstv[:, st_ * 4:st_ * 4 + 4, sub * 128:(sub + 1) * 128],
                             pb[bank].rearrange("p (j n) -> p j n", n=128)), (sem, 1))
                else:
                    w.do(eng, lambda e, bank=bank, st_=st_, sub=sub, dstv=dstv:
                         e.copy(
                             dstv[:, st_ * 4:st_ * 4 + 4, sub * 128:(sub + 1) * 128],
                             pb[bank].rearrange("p (j n) -> p j n", n=128)), (sem, 1))
                yield (eng, (s_ve.n if eng == "vector" else s_ac.n))

        for dst, src in ((ident, c_ident), (tri, c_tri), (ones, c_ones),
                         (iotar, c_iotar), (maskd, c_maskd)):
            w.do("sync", lambda e, d=dst, s=src: e.dma_start(out=d, in_=s), (s_ld, 16))
        w.do("sync", lambda e: e.dma_start(out=cm_row[0:1, :], in_=c_coremask), (s_ld, 16))
        w.do("sync", lambda e: e.dma_start(out=mm_row[0:1, :], in_=c_mymask), (s_ld, 16))
        w.do("sync", lambda e: e.dma_start(
            out=wgext[:, 0:72].rearrange("p (c n) -> p c n", n=9),
            in_=wg_ext.rearrange("(c p) n -> p c n", p=128)), (s_ld, 16))
        w.do("sync", lambda e: e.dma_start(out=grep[0:1, 0:1], in_=gate_in), (s_ld, 16))
        w.do("sync", lambda e: e.dma_start(out=grep[0:1, 1:2], in_=ltemp_in), (s_ld, 16))
        zt = P.alloc(D)
        p0m = A.mark()
        nw_row = A.alloc(D)
        cw_row = A.alloc(32)
        w.do("sync", lambda e: e.dma_start(out=nw_row[0:1, :], in_=norm_w), (s_ld, 16))
        ld0 = s_ld.n
        w.wait("scalar", s_ld, ld0)
        S(lambda e: e.activation(grep[0:1, 1:2], grep[0:1, 1:2], ACT.Exp))
        w.wait("tensor", s_ld, ld0)
        w.wait("tensor", s_ac)
        w.do("tensor", lambda e: e.matmul(pb[0][:, 0:2], ones[0:1, :], grep[0:1, 0:2],
                                          start=True, stop=True))
        w.do("tensor", lambda e: e.matmul(pb[1][:, 0:512], ones[0:1, :],
                                          nw_row[0:1, 0:512], start=True, stop=True))
        w.do("tensor", lambda e: e.matmul(pb[2][:, 0:512], ones[0:1, :],
                                          nw_row[0:1, 512:1024], start=True, stop=True))
        for fc in range(DC):
            w.do("tensor", lambda e, fc=fc: e.matmul(
                pb[3][0:1, 0:8], ones[:, 0:1], wgext[:, fc * 9:fc * 9 + 8],
                start=(fc == 0), stop=(fc == DC - 1)),
                (s_pe, 1) if fc == DC - 1 else None)
        w.wait("vector", s_pe)
        V(lambda e: e.tensor_copy(grep[:, 2:3], pb[0][:, 0:1]))
        V(lambda e: e.tensor_copy(grep[:, 3:4], pb[0][:, 1:2]))
        V(lambda e: e.tensor_copy(nwrep[:, 0:512], pb[1][:, 0:512]))
        V(lambda e: e.tensor_copy(nwrep[:, 512:1024], pb[2][:, 0:512]))
        V(lambda e: e.tensor_copy(
            cw_row[0:1, :].rearrange("p (t n) -> p t n", n=8),
            pb[3][0:1, 0:8].rearrange("p (o n) -> p o n", o=1).to_broadcast([1, 4, 8])))
        V(lambda e: e.memset(zt[:], 0.0))
        ceps = P.alloc(16)
        V(lambda e: e.memset(ceps[:, 0:1], 1e-5))
        V(lambda e: e.memset(ceps[:, 1:2], 1e-6))
        V(lambda e: e.memset(ceps[:, 2:3], 1.0 / 1024))
        nc.const_aps.register(F32, 1e-5, ceps[:, 0:1])
        nc.const_aps.register(F32, 1e-6, ceps[:, 1:2])
        nc.const_aps.register(F32, 1.0 / 1024, ceps[:, 2:3])
        ve_cw = s_ve.n
        w.wait("tensor", s_ve, ve_cw)
        w.do("tensor", lambda e: e.matmul(pb[4][:, 0:32], ones[0:1, :], cw_row[0:1, :],
                                          start=True, stop=True), (s_pe, 1))
        w.wait("vector", s_pe)
        V(lambda e: e.tensor_copy(cwrep[:], pb[4][:, 0:32]))
        ve_consts = s_ve.n
        gate_ap = grep[:, 2:3]
        temp_ap = grep[:, 3:4]
        w.wait("gpsimd", s_ve, ve_consts)
        for i in range((TRASH + 128) // 128):
            w.do("gpsimd", lambda e, i=i: e.dma_start(
                out=expbuf[i * 128:(i + 1) * 128, :], in_=zt[:]), (s_gp, 16))
        zf_done = s_gp.n

        if PH < 1:
            w.wait("gpsimd", s_gp, s_gp.n)
            w.emit(block)
            return nc
        A.reset(p0m)
        p1m = A.mark()
        xg = [A.alloc(4096), A.alloc(4096)]
        xfm = A.alloc(4096)
        rawg = A.alloc(512)
        rawtm = A.alloc(48)
        sqd = A.alloc(1024)
        tA = A.alloc(32)
        tB = A.alloc(32)
        tC = A.alloc(32)
        mg = A.alloc(16)
        fng = A.alloc(32)
        fog = A.alloc(64)
        feg = A.alloc(16)
        CNT = 6
        ld_grp = [None] * NGRP
        grp_free = [None] * NGRP
        evac_hist = []

        def p1_load(g):
            if grp_free[g - 2] is not None if g >= 2 else False:
                sem, val = grp_free[g - 2]
                w.wait("sync", sem, val)
            if g >= 2 and g - 2 < GPR:
                w.wait("sync", s_ac, ac_own[g - 2])
            w.do("sync", lambda e, g=g: e.dma_start(
                out=xg[g % 2].rearrange("p (s d) -> p s d", d=D),
                in_=x_full[g * 512:(g + 1) * 512, :].rearrange("(s p) d -> p s d", p=128)),
                (s_ld, 16))
            ld_grp[g] = s_ld.n

        ac_own = [0] * NGRP
        xfv = xfm.rearrange("p (c n) -> p c n", n=512)
        w.wait("sync", s_pe, s_pe.n)
        w.wait("sync", s_ve, s_ve.n)
        for g in range(NGRP):
            own = g < GPR
            if g < 2:
                p1_load(g)
            xgv = xg[g % 2].rearrange("p (s d) -> p s d", d=D)
            w.wait("tensor", s_ld, ld_grp[g])
            for h in range(8 if P1S >= 2 else 0):
                sub, st_ = h // 2, h % 2
                bank = h % 4
                k = len(evac_hist)
                if k >= 4:
                    sem, val = evac_hist[k - 4]
                    w.wait("tensor", sem, val)
                for j in range(4):
                    fc = st_ * 4 + j
                    w.do("tensor", lambda e, bank=bank, j=j, sub=sub, fc=fc, xgv=xgv:
                         e.transpose(pb[bank][:, j * 128:(j + 1) * 128],
                                     xgv[:, sub, fc * 128:(fc + 1) * 128], ident))
                w.mark("tensor", s_pe, 1)
                if h % 2 == 0 or KEVAC == "v":
                    w.wait("vector", s_pe)
                    V(lambda e, bank=bank, st_=st_, sub=sub:
                      e.tensor_copy(xfv[:, st_ * 4:st_ * 4 + 4, sub * 128:(sub + 1) * 128],
                                    pb[bank].rearrange("p (j n) -> p j n", n=128)))
                    evac_hist.append((s_ve, s_ve.n))
                else:
                    w.wait("scalar", s_pe)
                    S(lambda e, bank=bank, st_=st_, sub=sub:
                      e.copy(xfv[:, st_ * 4:st_ * 4 + 4, sub * 128:(sub + 1) * 128],
                             pb[bank].rearrange("p (j n) -> p j n", n=128)))
                    evac_hist.append((s_ac, s_ac.n))
            pe_tr = s_pe.n
            if own and P1S >= 2 and KSSQ:
                w.wait("scalar", s_ld, ld_grp[g])
                for sub in range(4):
                    t = g * 4 + sub
                    S(lambda e, sub=sub, t=t, xgv=xgv:
                      e.activation(sqd[:], xgv[:, sub, :], ACT.Square,
                                   accum_out=ssqbuf[:, t:t + 1]))
            ac_own[g] = s_ac.n
            grp_free[g] = (s_pe, pe_tr) if P1S >= 2 else None
            if g + 2 < NGRP:
                p1_load(g + 2)
            if P1S < 3:
                continue
            for kk in range(8):
                sem, val = evac_hist[len(evac_hist) - 8 + kk]
                w.wait("tensor", sem, val)
            for fc in range(DC):
                w.do("tensor", lambda e, fc=fc: e.matmul(
                    pb[4][0:9, :], wgext[:, fc * 9:(fc + 1) * 9], xfv[:, fc, :],
                    start=(fc == 0), stop=(fc == DC - 1)))
            w.mark("tensor", s_pe, 1)
            w.wait("vector", s_pe)
            V(lambda e: e.tensor_copy(rawg[0:9, :], pb[4][0:9, :]))
            if P1S < 4:
                continue
            w.wait("tensor", s_ve)
            for j in range(4):
                w.do("tensor", lambda e, j=j: e.transpose(
                    pb[5][:, j * 9:(j + 1) * 9], rawg[0:9, j * 128:(j + 1) * 128],
                    ident[0:9, 0:9]))
            w.mark("tensor", s_pe, 1)
            w.wait("vector", s_pe)
            V(lambda e: e.tensor_copy(rawtm[:, 0:36], pb[5][:, 0:36]))
            r9 = rawtm[:, 0:36].rearrange("p (t n) -> p t n", n=9)
            raw8 = r9[:, :, 0:8]
            mcol = r9[:, :, 8:9]
            negs = negbuf[:, g * 32:(g + 1) * 32] if own else fng
            ohg = ohbuf[:, g * 64:(g + 1) * 64] if own else fog
            if own:
                e1d = e12[:, g * 4:(g + 1) * 4]
                e2d = e12[:, 64 + g * 4:64 + (g + 1) * 4]
            else:
                e1d = feg[:, 0:4]
                e2d = feg[:, 4:8]
            if own:
                V(lambda e, g=g, r9=r9: e.tensor_copy(
                    mbuf[:, g * 4:(g + 1) * 4],
                    r9[:, :, 8:9].rearrange("p t o -> p (t o)")))
            V(lambda e, mcol=mcol: e.tensor_tensor(
                W8(tA), W8(cwrep), mcol.to_broadcast([128, 4, 8]), op=ALU.mult))
            V(lambda e, raw8=raw8, negs=negs: e.tensor_tensor(
                W8(negs), W8(tA), raw8, op=ALU.subtract))
            V(lambda e, negs=negs: e.tensor_reduce(
                mg[:, 0:4], W8(negs), axis=AX.X, op=ALU.min))
            V(lambda e, negs=negs: e.tensor_tensor(
                W8(tB), W8(negs), bc(mg[:, 0:4], [128, 4, 8]), op=ALU.is_equal))
            V(lambda e: e.tensor_tensor(W8(tC), W8(tB), W8(iotar), op=ALU.mult))
            V(lambda e, e1d=e1d: e.tensor_reduce(e1d, W8(tC), axis=AX.X, op=ALU.max))
            oh1 = W16(ohg)[:, :, 0:8]
            oh2 = W16(ohg)[:, :, 8:16]
            V(lambda e, oh1=oh1, e1d=e1d: e.tensor_tensor(
                oh1, W8(iotar), e1d[:, :, None].to_broadcast([128, 4, 8]),
                op=ALU.is_equal))
            V(lambda e, oh1=oh1, negs=negs: e.scalar_tensor_tensor(
                W8(tA), oh1, 1e30, W8(negs), op0=ALU.mult, op1=ALU.add))
            V(lambda e: e.tensor_reduce(mg[:, 4:8], W8(tA), axis=AX.X, op=ALU.min))
            V(lambda e: e.tensor_tensor(
                W8(tB), W8(tA), bc(mg[:, 4:8], [128, 4, 8]), op=ALU.is_equal))
            V(lambda e: e.tensor_tensor(W8(tC), W8(tB), W8(iotar), op=ALU.mult))
            V(lambda e, e2d=e2d: e.tensor_reduce(e2d, W8(tC), axis=AX.X, op=ALU.max))
            V(lambda e, oh2=oh2, e2d=e2d: e.tensor_tensor(
                oh2, W8(iotar), e2d[:, :, None].to_broadcast([128, 4, 8]),
                op=ALU.is_equal))
            ve_oh = s_ve.n
            if P1S < 6:
                continue
            r = g // GPR
            w.wait("tensor", s_ve, ve_oh)
            w.do("tensor", lambda e, r=r, g=g, ohg=ohg: e.matmul(
                pb[CNT][0:1, r * 64:(r + 1) * 64], ones[:, 0:1], ohg,
                start=(g % GPR == 0), stop=(g % GPR == GPR - 1)), (s_pe, 1))
            if not own:
                w.wait("vector", s_pe)
        p1_pe = s_pe.n
        w.wait("vector", s_pe, p1_pe)
        if P1S >= 6:
            V(lambda e: e.tensor_reduce(
                cnts8[0:1, :].rearrange("p (r ke) -> p r ke", ke=16),
                pb[CNT][0:1, :].rearrange("p (r t ke) -> p r t ke", t=4, ke=16)
                .transpose([0, 1, 3, 2]),
                axis=AX.X, op=ALU.add))
        barrier()

        if PH < 2:
            if dbg_arena is not None:
                w.do("gpsimd", lambda e: e.dma_start(out=dbg_arena, in_=AP_[:]), (s_gp, 16))
                w.do("gpsimd", lambda e: e.dma_start(out=dbg_i, in_=AI_[:]), (s_gp, 16))
            w.wait("gpsimd", s_gp, s_gp.n)
            w.emit(block)
            return nc
        A.reset(p1m)
        p2m = A.mark()
        lbuf = A.alloc(512)
        pbuf = A.alloc(512)
        qbuf = A.alloc(512)
        s64a = A.alloc(64)
        s64b = A.alloc(64)
        s64c = A.alloc(64)
        locG = A.alloc(1024)
        locB = A.alloc(1024)
        selG = A.alloc(128)
        selB = A.alloc(128)
        kbuf = A.alloc(128)
        kBbuf = A.alloc(128)
        t128a = A.alloc(128)
        t128b = A.alloc(128)
        trash_c = A.alloc(128)
        tot64 = A.alloc(16)
        tprow = A.alloc(1024)

        V(lambda e: e.tensor_tensor(t128a[0:1, :], cnts8[0:1, :], cm_row[0:1, :], op=ALU.mult))
        V(lambda e: e.tensor_reduce(
            base_row[0:1, :].rearrange("p (o ke) -> p o ke", o=1),
            t128a[0:1, :].rearrange("p (r ke) -> p r ke", ke=16).transpose([0, 2, 1]),
            axis=AX.X, op=ALU.add))
        V(lambda e: e.tensor_reduce(
            trow[0:1, 0:16].rearrange("p (o ke) -> p o ke", o=1),
            cnts8[0:1, :].rearrange("p (r ke) -> p r ke", ke=16).transpose([0, 2, 1]),
            axis=AX.X, op=ALU.add))
        V(lambda e: e.tensor_tensor(base_row[0:1, 8:16], base_row[0:1, 8:16],
                                    trow[0:1, 0:8], op=ALU.add))
        V(lambda e: e.tensor_tensor(t128a[0:1, :], cnts8[0:1, :], mm_row[0:1, :], op=ALU.mult))
        V(lambda e: e.tensor_reduce(
            trow[0:1, 16:32].rearrange("p (o ke) -> p o ke", o=1),
            t128a[0:1, :].rearrange("p (r ke) -> p r ke", ke=16).transpose([0, 2, 1]),
            axis=AX.X, op=ALU.add))
        V(lambda e: e.memset(k2_row[0:1, 0:8], 0.0))
        V(lambda e: e.tensor_copy(k2_row[0:1, 8:16], trow[0:1, 16:24]))
        ve_bases = s_ve.n
        w.wait("tensor", s_ve, ve_bases)
        for t in range(NT):
            b = t // 32
            w.do("tensor", lambda e, b=b, t=t: e.matmul(
                pb[b][0:1, (t % 32) * 16:(t % 32) * 16 + 16], ones[:, 0:1],
                ohbuf[:, t * 16:(t + 1) * 16],
                start=(t % 32 == 0), stop=(t % 32 == 31)),
                (s_pe, 1) if t % 32 == 31 else None)
        w.wait("vector", s_pe)
        V(lambda e: e.tensor_copy(tprow[0:1, 0:512], pb[0][0:1, :]))
        V(lambda e: e.tensor_copy(tprow[0:1, 512:1024], pb[1][0:1, :]))
        ve_tot = s_ve.n
        w.wait("gpsimd", s_ve, ve_tot)
        w.do("gpsimd", lambda e: e.dma_start(out=bounce, in_=tprow[0:1, :]), (s_gp, 16))
        w.wait("gpsimd", s_gp)
        w.do("gpsimd", lambda e: e.dma_start(out=tot64[0:64, :], in_=bounce), (s_gp, 16))
        gp_t64 = s_gp.n
        w.wait("tensor", s_gp, gp_t64)
        w.do("tensor", lambda e: e.matmul(pb[2][0:64, 0:16], tri[0:64, 0:64],
                                          tot64[0:64, :], start=True, stop=True), (s_pe, 1))
        w.wait("vector", s_pe)
        V(lambda e: e.tensor_copy(tot64[0:64, :], pb[2][0:64, 0:16]))
        ve_tp = s_ve.n
        w.wait("gpsimd", s_ve, ve_tp)
        w.do("gpsimd", lambda e: e.dma_start(out=bounce, in_=tot64[0:64, :]), (s_gp, 16))
        w.wait("gpsimd", s_gp)
        w.do("gpsimd", lambda e: e.dma_start(out=tprow[0:1, :], in_=bounce), (s_gp, 16))
        gp_tp = s_gp.n
        w.wait("tensor", s_gp, gp_tp)
        ve_loc = {}
        for t in range(NT):
            ba = (t % 2) * 2
            if t >= 2:
                w.wait("tensor", s_ve, ve_loc[t - 2])
            for bx, krow in ((ba, base_row), (ba + 1, k2_row)):
                w.do("tensor", lambda e, bx=bx, t=t: e.matmul(
                    pb[bx][:, 0:16], tri, ohbuf[:, t * 16:(t + 1) * 16],
                    start=True, stop=False))
                w.do("tensor", lambda e, bx=bx, t=t: e.matmul(
                    pb[bx][:, 0:16], ones[0:1, :], tprow[0:1, t * 16:(t + 1) * 16],
                    start=False, stop=False))
                w.do("tensor", lambda e, bx=bx, krow=krow: e.matmul(
                    pb[bx][:, 0:16], ones[0:1, :], krow[0:1, :],
                    start=False, stop=True),
                    (s_pe, 1) if bx == ba + 1 else None)
            w.wait("vector", s_pe)
            V(lambda e, ba=ba, t=t: e.tensor_copy(locG[:, t * 16:(t + 1) * 16],
                                                  pb[ba][:, 0:16]))
            V(lambda e, ba=ba, t=t: e.tensor_copy(locB[:, t * 16:(t + 1) * 16],
                                                  pb[ba + 1][:, 0:16]))
            ve_loc[t] = s_ve.n
        V(lambda e: e.tensor_tensor(s64a[:, :], mbuf[:, :], mbuf[:, :], op=ALU.mult))
        V(lambda e: e.tensor_scalar(s64b[:, :], ssqbuf[:, :], 1.0 / 1024, scalar2=None,
                                    op0=ALU.mult))
        V(lambda e: e.tensor_tensor(s64a[:, :], s64b[:, :], s64a[:, :], op=ALU.subtract))
        ve_var = s_ve.n
        w.wait("scalar", s_ve, ve_var)
        S(lambda e: e.activation(s64b[:, :], s64a[:, :], ACT.Sqrt, bias=1e-5))
        w.wait("vector", s_ac)
        V(lambda e: e.reciprocal(rbuf[:, :], s64b[:, :]))
        V(lambda e: e.tensor_scalar(s64c[:, :], rbuf[:, :], -1.0, scalar2=None,
                                    op0=ALU.mult))
        V(lambda e: e.tensor_tensor(W8(lbuf), W8(negbuf), bc(s64c, [128, 64, 8]),
                                    op=ALU.mult))
        V(lambda e: e.tensor_reduce(s64a[:, :], W8(lbuf), axis=AX.X, op=ALU.max))
        V(lambda e: e.tensor_tensor(W8(lbuf), W8(lbuf), bc(s64a, [128, 64, 8]),
                                    op=ALU.subtract))
        ve_l = s_ve.n
        w.wait("scalar", s_ve, ve_l)
        S(lambda e: e.activation(pbuf[:, :], lbuf[:, :], ACT.Exp))
        w.wait("vector", s_ac)
        V(lambda e: e.tensor_reduce(s64a[:, :], W8(pbuf), axis=AX.X, op=ALU.add))
        V(lambda e: e.reciprocal(s64b[:, :], s64a[:, :]))
        V(lambda e: e.tensor_tensor(W8(pbuf), W8(pbuf), bc(s64b, [128, 64, 8]),
                                    op=ALU.mult))
        V(lambda e: e.tensor_tensor(W8(qbuf), W8(pbuf), W16(ohbuf)[:, :, 0:8],
                                    op=ALU.mult))
        V(lambda e: e.tensor_reduce(gbuf[:, 0:64], W8(qbuf), axis=AX.X, op=ALU.add))
        V(lambda e: e.tensor_tensor(W8(qbuf), W8(pbuf), W16(ohbuf)[:, :, 8:16],
                                    op=ALU.mult))
        V(lambda e: e.tensor_reduce(gbuf[:, 64:128], W8(qbuf), axis=AX.X, op=ALU.add))
        V(lambda e: e.tensor_tensor(locG[:, :], locG[:, :], ohbuf[:, :], op=ALU.mult))
        V(lambda e: e.tensor_reduce(selG[:, 0:64], W16(locG)[:, :, 0:8],
                                    axis=AX.X, op=ALU.add))
        V(lambda e: e.tensor_reduce(selG[:, 64:128], W16(locG)[:, :, 8:16],
                                    axis=AX.X, op=ALU.add))
        V(lambda e: e.tensor_tensor(locB[:, :], locB[:, :], ohbuf[:, :], op=ALU.mult))
        V(lambda e: e.tensor_reduce(selB[:, 0:64], W16(locB)[:, :, 0:8],
                                    axis=AX.X, op=ALU.add))
        V(lambda e: e.tensor_reduce(selB[:, 64:128], W16(locB)[:, :, 8:16],
                                    axis=AX.X, op=ALU.add))
        V(lambda e: e.tensor_scalar(kbuf[:, :], selG[:, :], float(CCAP), scalar2=None,
                                    op0=ALU.is_lt))
        V(lambda e: e.tensor_scalar(kBbuf[:, :], selB[:, :], float(CL), scalar2=None,
                                    op0=ALU.is_lt))
        V(lambda e: e.tensor_tensor(gbuf[:, :], gbuf[:, :], kbuf[:, :], op=ALU.mult))
        V(lambda e: e.tensor_tensor(s64a[:, :], gbuf[:, 0:64], gbuf[:, 64:128],
                                    op=ALU.add))
        V(lambda e: e.tensor_scalar(s64a[:, :], s64a[:, :], 1e-9, scalar2=None,
                                    op0=ALU.max))
        V(lambda e: e.reciprocal(s64b[:, :], s64a[:, :]))
        V(lambda e: e.tensor_tensor(gbuf[:, 0:64], gbuf[:, 0:64], s64b[:, :],
                                    op=ALU.mult))
        V(lambda e: e.tensor_tensor(gbuf[:, 64:128], gbuf[:, 64:128], s64b[:, :],
                                    op=ALU.mult))
        V(lambda e: e.tensor_tensor(gbuf[:, :], gbuf[:, :], kBbuf[:, :], op=ALU.mult))
        V(lambda e: e.scalar_tensor_tensor(t128a[:, :], e12[:, :], -2560.0, selB[:, :],
                                           op0=ALU.mult, op1=ALU.add))
        V(lambda e: e.tensor_scalar(t128a[:, :], t128a[:, :], 17920.0, scalar2=None,
                                    op0=ALU.add))
        V(lambda e: e.tensor_tensor(t128a[:, :], t128a[:, :], kBbuf[:, :], op=ALU.mult))
        V(lambda e: e.tensor_copy(physG[:, :], t128a[:, :]))
        V(lambda e: e.tensor_scalar(t128b[:, :], kBbuf[:, :], -1.0, scalar2=1.0,
                                    op0=ALU.mult, op1=ALU.add))
        V(lambda e: e.scalar_tensor_tensor(t128b[:, :], t128b[:, :], float(TRASH),
                                           t128a[:, :], op0=ALU.mult, op1=ALU.add))
        V(lambda e: e.tensor_copy(physS[:, :], t128b[:, :]))
        barrier()

        if PH < 3:
            w.wait("gpsimd", s_gp, s_gp.n)
            w.emit(block)
            return nc
        A.reset(p2m)
        p3m = A.mark()
        x3 = [A.alloc(1024), A.alloc(1024)]
        xn3 = [A.alloc(1024), A.alloc(1024)]
        ld3 = [None] * NT
        ve3 = [None] * NT
        gp3 = [None] * NT

        def p3_load(t):
            if t >= 2:
                w.wait("sync", s_ve, ve3[t - 2])
            w.do("sync", lambda e, t=t: e.dma_start(
                out=x3[t % 2], in_=x_full[t * 128:(t + 1) * 128, :]), (s_ld, 16))
            ld3[t] = s_ld.n

        p3_load(0)
        p3_load(1)
        for t in range(NT):
            w.wait("vector", s_ld, ld3[t])
            if t >= 2:
                w.wait("vector", s_gp, gp3[t - 2])
            V(lambda e, t=t: e.tensor_scalar(
                xn3[t % 2], x3[t % 2], mbuf[:, t:t + 1], scalar2=rbuf[:, t:t + 1],
                op0=ALU.subtract, op1=ALU.mult))
            ve3[t] = s_ve.n
            if t + 2 < NT:
                p3_load(t + 2)
            w.wait("gpsimd", s_ve, ve3[t])
            if t == 0:
                w.wait("gpsimd", s_gp, zf_done)
            for k in range(2):
                w.do("gpsimd", lambda e, t=t, k=k: e.indirect_dma_start(
                    out=expbuf,
                    out_offset=IndirectOffsetOnAxis(
                        ap=physS[:, k * 64 + t:k * 64 + t + 1], axis=0),
                    in_=xn3[t % 2], in_offset=None), (s_gp, 16))
            gp3[t] = s_gp.n
        barrier()

        if PH < 4:
            w.wait("gpsimd", s_gp, s_gp.n)
            w.emit(block)
            return nc
        A.reset(p3m)
        p4m = A.mark()
        wkv_sb = A.alloc(4096)
        wo_sb = A.alloc(4096)
        wq_sb = A.alloc(2048)
        hqb = A.alloc(4096)
        khat = A.alloc(8192)
        vext = A.alloc(8224)
        qhat = A.alloc(1024)
        nsc = A.alloc(512)
        nrm = A.alloc(512)
        attn_tm = A.alloc(4096)
        attnfm = A.alloc(4096)
        x4 = A.alloc(1024)
        s4 = A.alloc(1024)
        sc1 = A.alloc(16)
        vxv = vext.rearrange("p (t n) -> p t n", n=514)
        V(lambda e: e.memset(vxv[:, :, 256:257], 1.0))
        V(lambda e: e.memset(vxv[:, :, 513:514], 1.0))
        ve_vone = s_ve.n
        hv = hqb.rearrange("p (c n) -> p c n", n=512)
        wkv = wkv_sb.rearrange("p (c n) -> p c n", n=512)
        wvv = wkv
        wqv = wq_sb.rearrange("p (c n) -> p c n", n=256)
        wov = wo_sb.rearrange("p (c n) -> p c n", n=512)
        afv = attnfm.rearrange("p (c n) -> p c n", n=512)
        sfv = attn_tm.rearrange("p (c n) -> p c n", n=512)
        ve_x4 = 0
        pe_wq = 0
        pe_h = 0
        pe_wo = 0
        pe_wv = 0
        gp_h = 0
        for sq in range(NSEQ):
            for qb in range(NSEQ):
                tb = sq * 4 + qb
                w.wait("vector", s_gp, gp_h)
                for sub in range(4):
                    t = tb * 4 + sub
                    w.wait("sync", s_ve, ve_x4)
                    w.do("sync", lambda e, t=t: e.dma_start(
                        out=x4, in_=x_full[t * 128:(t + 1) * 128, :]), (s_ld, 16))
                    ldx = s_ld.n
                    w.wait("vector", s_ld, ldx)
                    V(lambda e, t=t: e.tensor_scalar(
                        s4, x4, mbuf[:, t:t + 1], scalar2=rbuf[:, t:t + 1],
                        op0=ALU.subtract, op1=ALU.mult))
                    ve_xn = s_ve.n
                    ve_x4 = ve_xn
                    w.wait("scalar", s_ve, ve_xn)
                    S(lambda e: e.activation(nsc[:, 0:512], s4[:, 0:512], ACT.Square,
                                             accum_out=sc1[:, 0:1]))
                    S(lambda e: e.activation(nsc[:, 0:512], s4[:, 512:1024], ACT.Square,
                                             accum_out=sc1[:, 1:2]))
                    w.wait("vector", s_ac)
                    V(lambda e: e.tensor_tensor(sc1[:, 2:3], sc1[:, 0:1], sc1[:, 1:2],
                                                op=ALU.add))
                    ve_s = s_ve.n
                    w.wait("scalar", s_ve, ve_s)
                    S(lambda e: e.activation(sc1[:, 3:4], sc1[:, 2:3], ACT.Sqrt,
                                             scale=1.0 / 1024, bias=1e-6))
                    w.wait("vector", s_ac)
                    V(lambda e: e.reciprocal(sc1[:, 4:5], sc1[:, 3:4]))
                    V(lambda e: e.tensor_scalar(s4, s4, sc1[:, 4:5], scalar2=None,
                                                op0=ALU.mult))
                    V(lambda e: e.tensor_tensor(s4, s4, nwrep, op=ALU.mult))
                    ve_h2 = s_ve.n
                    w.wait("tensor", s_ve, ve_h2)
                    for st_ in range(2):
                        bank = 6 + st_
                        for j in range(4):
                            fc = st_ * 4 + j
                            w.do("tensor", lambda e, bank=bank, j=j, fc=fc:
                                 e.transpose(pb[bank][:, j * 128:(j + 1) * 128],
                                             s4[:, fc * 128:(fc + 1) * 128], ident))
                        w.mark("tensor", s_pe, 1)
                        w.wait("vector", s_pe)
                        V(lambda e, bank=bank, st_=st_, sub=sub: e.tensor_copy(
                            hv[:, st_ * 4:st_ * 4 + 4, sub * 128:(sub + 1) * 128],
                            pb[bank].rearrange("p (j n) -> p j n", n=128)))
                        w.wait("tensor", s_ve)
                    pe_h = s_pe.n
                    w.wait("vector", s_pe, pe_h)
                ve_hd = s_ve.n
                w.wait("gpsimd", s_ve, ve_hd)
                w.do("gpsimd", lambda e, tb=tb: e.dma_start(
                    out=h_stage[tb * 1024:(tb + 1) * 1024, :]
                    .rearrange("(c p) n -> p c n", p=128),
                    in_=hv), (s_gp, 16))
                gp_h = s_gp.n
                if P4S < 2:
                    continue
                w.wait("sync", s_pe, pe_wv)
                w.do("sync", lambda e: e.dma_start(
                    out=wkv, in_=wk_d.rearrange("(c p) n -> p c n", p=128)), (s_ld, 16))
                ld_wk = s_ld.n
                w.wait("tensor", s_ld, ld_wk)
                for kvh in range(2):
                    for c in range(2):
                        kd = kvh * 2 + c
                        for fc in range(DC):
                            w.do("tensor", lambda e, kd=kd, fc=fc, c=c: e.matmul(
                                pb[c][:, :], wkv[:, fc, kd * 128:(kd + 1) * 128],
                                hv[:, fc, :], start=(fc == 0), stop=(fc == DC - 1)))
                        w.mark("tensor", s_pe, 1)
                        w.wait("vector", s_pe)
                        V(lambda e, kd=kd, qb=qb, c=c: e.tensor_copy(
                            khat[:, kd * 2048 + qb * 512: kd * 2048 + (qb + 1) * 512],
                            pb[c][:, :]))
                        if P4K >= 2:
                            w.wait("scalar", s_ve)
                            S(lambda e, kd=kd, qb=qb: e.activation(
                                nsc[:, 0:512],
                                khat[:, kd * 2048 + qb * 512: kd * 2048 + (qb + 1) * 512],
                                ACT.Square))
                            w.wait("tensor", s_ac)
                            w.do("tensor", lambda e, c=c: e.matmul(
                                pb[2][:, :], ones, nsc[:, 0:512],
                                start=(c == 0), stop=(c == 1)),
                                (s_pe, 1) if c == 1 else None)
                    if P4K >= 2:
                        w.wait("scalar", s_pe)
                        S(lambda e: e.activation(nrm[:, 0:512], pb[2][:, :], ACT.Sqrt))
                        w.wait("vector", s_ac)
                        V(lambda e: e.tensor_scalar(nrm[:, 0:512], nrm[:, 0:512], 1e-12,
                                                    scalar2=None, op0=ALU.max))
                        V(lambda e: e.reciprocal(nrm[:, 0:512], nrm[:, 0:512]))
                        for c in range(2):
                            kd = kvh * 2 + c
                            V(lambda e, kd=kd, qb=qb: e.tensor_tensor(
                                khat[:, kd * 2048 + qb * 512: kd * 2048 + (qb + 1) * 512],
                                khat[:, kd * 2048 + qb * 512: kd * 2048 + (qb + 1) * 512],
                                nrm[:, 0:512], op=ALU.mult))
                    ve_k = s_ve.n
                    w.wait("tensor", s_ve, ve_k)
                if P4K < 3:
                    continue
                w.wait("sync", s_pe, s_pe.n)
                w.do("sync", lambda e: e.dma_start(
                    out=wkv, in_=wv_d.rearrange("(c p) n -> p c n", p=128)), (s_ld, 16))
                ld_wv = s_ld.n
                w.wait("tensor", s_ld, ld_wv)
                if sq == 0 and qb == 0:
                    w.wait("tensor", s_ve, ve_vone)
                for sub in range(4):
                    ti = qb * 4 + sub
                    for fc in range(DC):
                        w.do("tensor", lambda e, sub=sub, fc=fc: e.matmul(
                            pb[3][:, :], hv[:, fc, sub * 128:(sub + 1) * 128],
                            wvv[:, fc, :], start=(fc == 0), stop=(fc == DC - 1)))
                    w.mark("tensor", s_pe, 1)
                    w.wait("vector", s_pe)
                    V(lambda e, ti=ti: e.tensor_copy(
                        vext[:, ti * 514:ti * 514 + 256], pb[3][:, 0:256]))
                    V(lambda e, ti=ti: e.tensor_copy(
                        vext[:, ti * 514 + 257:ti * 514 + 513], pb[3][:, 256:512]))
                    ve_v = s_ve.n
                    w.wait("tensor", s_ve, ve_v)
                pe_wv = s_pe.n
                w.wait("vector", s_pe, s_pe.n)
            if P4S < 3:
                continue
            for qb in range(NSEQ):
                tb = sq * 4 + qb
                w.wait("sync", s_pe, pe_wq)
                w.wait("sync", s_gp, gp_h)
                w.do("sync", lambda e, tb=tb: e.dma_start(
                    out=hqb.rearrange("p (c n) -> p c n", n=512),
                    in_=h_stage[tb * 1024:(tb + 1) * 1024, :]
                    .rearrange("(c p) n -> p c n", p=128)), (s_ld, 16))
                ld_h = s_ld.n
                for head in range(NHEAD):
                    kvh = head // 2
                    w.wait("sync", s_pe, pe_wq)
                    w.do("sync", lambda e, head=head: e.dma_start(
                        out=wqv,
                        in_=wq_d.rearrange("(c p) n -> p c n", p=128)
                        [:, :, head * 256:(head + 1) * 256]), (s_ld, 16))
                    ld_q = s_ld.n
                    w.wait("tensor", s_ld, ld_q)
                    if head == 0:
                        w.wait("tensor", s_ld, ld_h)
                    for c in range(2):
                        for fc in range(DC):
                            w.do("tensor", lambda e, c=c, fc=fc: e.matmul(
                                pb[6 + c][:, :], wqv[:, fc, c * 128:(c + 1) * 128],
                                hv[:, fc, :], start=(fc == 0), stop=(fc == DC - 1)))
                        w.mark("tensor", s_pe, 1)
                        w.wait("vector", s_pe)
                        V(lambda e, c=c: e.tensor_copy(qhat[:, c * 512:(c + 1) * 512],
                                                       pb[6 + c][:, :]))
                        w.wait("scalar", s_ve)
                        S(lambda e, c=c: e.activation(nsc[:, 0:512],
                                                      qhat[:, c * 512:(c + 1) * 512],
                                                      ACT.Square))
                        w.wait("tensor", s_ac)
                        w.do("tensor", lambda e, c=c: e.matmul(
                            pb[0][:, :], ones, nsc[:, 0:512],
                            start=(c == 0), stop=(c == 1)),
                            (s_pe, 1) if c == 1 else None)
                    pe_wq = s_pe.n
                    w.wait("scalar", s_pe, pe_wq)
                    S(lambda e: e.activation(nrm[:, 0:512], pb[0][:, :], ACT.Sqrt))
                    w.wait("vector", s_ac)
                    V(lambda e: e.tensor_scalar(nrm[:, 0:512], nrm[:, 0:512], 1e-12,
                                                scalar2=None, op0=ALU.max))
                    V(lambda e: e.reciprocal(nrm[:, 0:512], nrm[:, 0:512]))
                    for c in range(2):
                        V(lambda e, c=c: e.tensor_tensor(
                            qhat[:, c * 512:(c + 1) * 512],
                            qhat[:, c * 512:(c + 1) * 512],
                            nrm[:, 0:512], op=ALU.mult))
                    ve_q = s_ve.n
                    w.wait("tensor", s_ve, ve_q)
                    if P4S < 4:
                        continue
                    nkt = 4 * qb + 4
                    ac_exp = [0] * nkt
                    for kt in range(nkt):
                        sb_ = kt % 2
                        if kt >= 2:
                            w.wait("tensor", s_ac, ac_exp[kt - 2])
                        for c in range(2):
                            kd = kvh * 2 + c
                            w.do("tensor", lambda e, sb_=sb_, kd=kd, kt=kt, c=c: e.matmul(
                                pb[sb_][:, :],
                                khat[:, kd * 2048 + kt * 128: kd * 2048 + (kt + 1) * 128],
                                qhat[:, c * 512:(c + 1) * 512],
                                start=(c == 0), stop=(c == 1)),
                                (s_pe, 1) if c == 1 else None)
                        w.wait("scalar", s_pe)
                        S(lambda e, sb_=sb_: e.activation(
                            nrm[:, 0:512], pb[sb_][:, :], ACT.Exp, scale=temp_ap))
                        ac_exp[kt] = s_ac.n
                        if kt >= 4 * qb:
                            j = kt - 4 * qb
                            w.wait("vector", s_ac, ac_exp[kt])
                            V(lambda e, j=j: e.tensor_tensor(
                                nrm[:, 0:512], nrm[:, 0:512],
                                maskd[:, j * 512:(j + 1) * 512], op=ALU.mult))
                            w.wait("tensor", s_ve)
                        else:
                            w.wait("tensor", s_ac, ac_exp[kt])
                        for qs in range(4):
                            w.do("tensor", lambda e, qs=qs, kt=kt, kvh=kvh, nkt=nkt:
                                 e.matmul(
                                     pb[2 + qs][:, 0:257],
                                     nrm[:, qs * 128:(qs + 1) * 128],
                                     vxv[:, kt, kvh * 257:(kvh + 1) * 257],
                                     start=(kt == 0), stop=(kt == nkt - 1)),
                                 (s_pe, 1) if (kt == nkt - 1 and qs == 3) else None)
                        w.mark("tensor", s_pe, 1)
                        w.wait("scalar", s_pe, s_pe.n)
                    w.wait("vector", s_pe)
                    for qs in range(4):
                        V(lambda e, qs=qs: e.reciprocal(sc1[:, 5:6],
                                                        pb[2 + qs][:, 256:257]))
                        V(lambda e, qs=qs, head=head: e.tensor_scalar(
                            attn_tm[:, qs * 1024 + head * 256:
                                    qs * 1024 + (head + 1) * 256],
                            pb[2 + qs][:, 0:256], sc1[:, 5:6], scalar2=None,
                            op0=ALU.mult))
                    ve_pv = s_ve.n
                    w.wait("tensor", s_ve, ve_pv)
                if P4S < 5:
                    continue
                for qs in range(4):
                    for st_ in range(2):
                        bank = 6 + st_
                        for j in range(4):
                            adc = st_ * 4 + j
                            w.do("tensor", lambda e, bank=bank, j=j, qs=qs, adc=adc:
                                 e.transpose(pb[bank][:, j * 128:(j + 1) * 128],
                                             attn_tm[:, qs * 1024 + adc * 128:
                                                     qs * 1024 + (adc + 1) * 128],
                                             ident))
                        w.mark("tensor", s_pe, 1)
                        w.wait("vector", s_pe)
                        V(lambda e, bank=bank, st_=st_, qs=qs: e.tensor_copy(
                            afv[:, st_ * 4:st_ * 4 + 4, qs * 128:(qs + 1) * 128],
                            pb[bank].rearrange("p (j n) -> p j n", n=128)))
                        w.wait("tensor", s_ve)
                ve_wo = {}
                for half in range(2):
                    w.wait("sync", s_pe, pe_wo)
                    w.do("sync", lambda e, half=half: e.dma_start(
                        out=wov,
                        in_=wo_d.rearrange("(c p) n -> p c n", p=128)
                        [:, :, half * 512:(half + 1) * 512]), (s_ld, 16))
                    ld_wo = s_ld.n
                    w.wait("tensor", s_ld, ld_wo)
                    for oc in range(4):
                        b2 = 2 + (oc % 2)
                        if oc >= 2:
                            w.wait("tensor", s_ve, ve_wo[oc - 2])
                        for adc in range(DC):
                            w.do("tensor", lambda e, b2=b2, adc=adc, oc=oc: e.matmul(
                                pb[b2][:, :], wov[:, adc, oc * 128:(oc + 1) * 128],
                                afv[:, adc, :], start=(adc == 0), stop=(adc == DC - 1)))
                        w.mark("tensor", s_pe, 1)
                        w.wait("vector", s_pe)
                        V(lambda e, b2=b2, half=half, oc=oc: e.tensor_copy(
                            sfv[:, half * 4 + oc, :], pb[b2][:, :]))
                        ve_wo[oc] = s_ve.n
                    pe_wo = s_pe.n
                w.wait("tensor", s_ve, s_ve.n)
                gp_s = 0
                for sub in range(4):
                    t = tb * 4 + sub
                    w.wait("sync", s_ve, ve_x4)
                    w.do("sync", lambda e, t=t: e.dma_start(
                        out=x4, in_=x_full[t * 128:(t + 1) * 128, :]), (s_ld, 16))
                    ld_x = s_ld.n
                    if sub > 0:
                        w.wait("vector", s_gp, gp_s)
                    for st_ in range(2):
                        bank = 6 + st_
                        for j in range(4):
                            oc2 = st_ * 4 + j
                            w.do("tensor", lambda e, bank=bank, j=j, oc2=oc2, sub=sub:
                                 e.transpose(pb[bank][:, j * 128:(j + 1) * 128],
                                             sfv[:, oc2, sub * 128:(sub + 1) * 128],
                                             ident))
                        w.mark("tensor", s_pe, 1)
                        w.wait("vector", s_pe)
                        V(lambda e, bank=bank, st_=st_: e.tensor_copy(
                            s4[:, st_ * 512:(st_ + 1) * 512]
                            .rearrange("p (j n) -> p j n", n=128),
                            pb[bank].rearrange("p (j n) -> p j n", n=128)))
                        w.wait("tensor", s_ve)
                    w.wait("vector", s_ld, ld_x)
                    V(lambda e, t=t: e.tensor_scalar(
                        x4, x4, mbuf[:, t:t + 1], scalar2=rbuf[:, t:t + 1],
                        op0=ALU.subtract, op1=ALU.mult))
                    ve_x4 = s_ve.n
                    V(lambda e: e.scalar_tensor_tensor(
                        s4, s4, gate_ap, x4, op0=ALU.mult, op1=ALU.add))
                    ve_sf = s_ve.n
                    w.wait("gpsimd", s_ve, ve_sf)
                    w.do("gpsimd", lambda e, t=t: e.dma_start(
                        out=s_stage[t * 128:(t + 1) * 128, :], in_=s4), (s_gp, 16))
                    gp_s = s_gp.n
                w.wait("vector", s_gp, gp_s)
        barrier()

        if PH < 5:
            w.wait("gpsimd", s_gp, s_gp.n)
            w.emit(block)
            return nc
        A.reset(p4m)
        p5m = A.mark()
        wa_sb = A.alloc(8192)
        wg_sb = A.alloc(8192)
        wu_sb = A.alloc(8192)
        xg5 = A.alloc(4096)
        xfm5 = A.alloc(4096)
        act5 = A.alloc(4096)
        gel5 = A.alloc(512)
        o2sb = A.alloc(512)
        part5 = A.alloc(512)
        w_in_v = w_in_d.rearrange("(e c p) n -> e p c n", e=E, p=128)
        w_out_v = w_out_d.rearrange("(e c p) n -> e p c n", e=E, p=128)
        wav = wa_sb.rearrange("p (c n) -> p c n", n=1024)
        wgv = wg_sb.rearrange("p (c n) -> p c n", n=1024)
        wuv = wu_sb.rearrange("p (c n) -> p c n", n=1024)
        xfv5 = xfm5.rearrange("p (c n) -> p c n", n=512)
        xg5v = xg5.rearrange("p (s d) -> p s d", d=D)
        a5 = act5.rearrange("p (c n) -> p c n", n=512)
        pe_prev_phase = 0
        jh0_stores = 0
        for ex in range(E):
            for jh in range(2):
                w.wait("sync", s_pe, pe_prev_phase)
                w.do("sync", lambda e, ex=ex, jh=jh: e.dma_start(
                    out=wav, in_=w_in_v[ex, :, :, jh * 1024:(jh + 1) * 1024]), (s_ld, 16))
                w.do("sync", lambda e, ex=ex, jh=jh: e.dma_start(
                    out=wgv, in_=w_in_v[ex, :, :, 2048 + jh * 1024: 2048 + (jh + 1) * 1024]),
                    (s_ld, 16))
                w.do("sync", lambda e, ex=ex, jh=jh: e.dma_start(
                    out=wuv, in_=w_out_v[ex, :, jh * 8:(jh + 1) * 8, :]), (s_ld, 16))
                ld_w5 = s_ld.n
                if jh == 1:
                    w.wait("sync", s_gp, jh0_stores)
                for st in range(5):
                    r0 = ex * CL + st * 512
                    w.wait("sync", s_pe, pe_prev_phase if st == 0 else pe_tr5)
                    w.do("sync", lambda e, r0=r0: e.dma_start(
                        out=xg5v,
                        in_=expbuf[r0:r0 + 512, :].rearrange("(s p) d -> p s d", p=128)),
                        (s_ld, 16))
                    ld_x5 = s_ld.n
                    w.wait("tensor", s_ld, ld_x5)
                    evh = []
                    for h in range(8):
                        sub, st_ = h // 2, h % 2
                        bank = 6 + (h % 2)
                        if h >= 2:
                            sem, val = evh[h - 2]
                            w.wait("tensor", sem, val)
                        for j in range(4):
                            fc = st_ * 4 + j
                            w.do("tensor", lambda e, bank=bank, j=j, sub=sub, fc=fc:
                                 e.transpose(pb[bank][:, j * 128:(j + 1) * 128],
                                             xg5v[:, sub, fc * 128:(fc + 1) * 128],
                                             ident))
                        w.mark("tensor", s_pe, 1)
                        if h % 2 == 0:
                            w.wait("vector", s_pe)
                            V(lambda e, bank=bank, st_=st_, sub=sub: e.tensor_copy(
                                xfv5[:, st_ * 4:st_ * 4 + 4, sub * 128:(sub + 1) * 128],
                                pb[bank].rearrange("p (j n) -> p j n", n=128)))
                            evh.append((s_ve, s_ve.n))
                        else:
                            w.wait("scalar", s_pe)
                            S(lambda e, bank=bank, st_=st_, sub=sub: e.copy(
                                xfv5[:, st_ * 4:st_ * 4 + 4, sub * 128:(sub + 1) * 128],
                                pb[bank].rearrange("p (j n) -> p j n", n=128)))
                            evh.append((s_ac, s_ac.n))
                    pe_tr5 = s_pe.n
                    for sem, val in evh[-2:]:
                        w.wait("tensor", sem, val)
                    if st == 0:
                        w.wait("tensor", s_ld, ld_w5)
                    ve_l1 = {}
                    for fo in range(8):
                        ba = (fo % 2) * 2
                        if fo >= 2:
                            w.wait("tensor", s_ve, ve_l1[fo - 2])
                        for fc in range(DC):
                            w.do("tensor", lambda e, ba=ba, fc=fc, fo=fo: e.matmul(
                                pb[ba][:, :], wav[:, fc, fo * 128:(fo + 1) * 128],
                                xfv5[:, fc, :], start=(fc == 0), stop=(fc == DC - 1)))
                        for fc in range(DC):
                            w.do("tensor", lambda e, ba=ba, fc=fc, fo=fo: e.matmul(
                                pb[ba + 1][:, :], wgv[:, fc, fo * 128:(fo + 1) * 128],
                                xfv5[:, fc, :], start=(fc == 0), stop=(fc == DC - 1)))
                        w.mark("tensor", s_pe, 1)
                        w.wait("scalar", s_pe)
                        if fo >= 1:
                            w.wait("scalar", s_ve, ve_l1[fo - 1])
                        S(lambda e, ba=ba: e.activation(gel5[:, :], pb[ba][:, :],
                                                        ACT.Gelu))
                        w.wait("vector", s_ac)
                        V(lambda e, ba=ba, fo=fo: e.tensor_tensor(
                            a5[:, fo, :], gel5[:, :], pb[ba + 1][:, :], op=ALU.mult))
                        ve_l1[fo] = s_ve.n
                    w.wait("tensor", s_ve, ve_l1[7])
                    w.wait("tensor", s_ve, ve_l1[6])
                    for hh in range(2):
                        for qs in range(4):
                            for fo in range(8):
                                w.do("tensor", lambda e, qs=qs, fo=fo, hh=hh: e.matmul(
                                    pb[qs][:, :], a5[:, fo, qs * 128:(qs + 1) * 128],
                                    wuv[:, fo, hh * 512:(hh + 1) * 512],
                                    start=(fo == 0), stop=(fo == 7)))
                            w.mark("tensor", s_pe, 1)
                            if jh == 0:
                                w.wait("vector", s_pe)
                                V(lambda e, qs=qs: e.tensor_copy(o2sb[:, :], pb[qs][:, :]))
                                ve_o2 = s_ve.n
                                w.wait("gpsimd", s_ve, ve_o2)
                                w.do("gpsimd", lambda e, r0=r0, qs=qs, hh=hh: e.dma_start(
                                    out=outbuf[r0 + qs * 128: r0 + (qs + 1) * 128,
                                               hh * 512:(hh + 1) * 512],
                                    in_=o2sb), (s_gp, 16))
                                gp_o2 = s_gp.n
                                w.wait("vector", s_gp, gp_o2)
                            else:
                                w.do("sync", lambda e, r0=r0, qs=qs, hh=hh: e.dma_start(
                                    out=part5,
                                    in_=outbuf[r0 + qs * 128: r0 + (qs + 1) * 128,
                                               hh * 512:(hh + 1) * 512]), (s_ld, 16))
                                ld_p = s_ld.n
                                w.wait("vector", s_pe)
                                w.wait("vector", s_ld, ld_p)
                                V(lambda e, qs=qs: e.tensor_tensor(
                                    o2sb[:, :], pb[qs][:, :], part5[:, :], op=ALU.add))
                                ve_o2 = s_ve.n
                                w.wait("gpsimd", s_ve, ve_o2)
                                w.do("gpsimd", lambda e, r0=r0, qs=qs, hh=hh: e.dma_start(
                                    out=outbuf[r0 + qs * 128: r0 + (qs + 1) * 128,
                                               hh * 512:(hh + 1) * 512],
                                    in_=o2sb), (s_gp, 16))
                                gp_o2 = s_gp.n
                                w.wait("vector", s_gp, gp_o2)
                                w.wait("sync", s_ve, ve_o2)
                        w.wait("tensor", s_ve, s_ve.n)
                    pe_prev_phase = s_pe.n
                if jh == 0:
                    jh0_stores = s_gp.n
        barrier()

        if PH < 6:
            w.wait("gpsimd", s_gp, s_gp.n)
            w.emit(block)
            return nc
        A.reset(p5m)
        rowA = A.alloc(1024)
        rowB = A.alloc(1024)
        t2b = A.alloc(1024)
        moeb = A.alloc(1024)
        x6 = A.alloc(1024)
        s6 = A.alloc(1024)
        o6 = A.alloc(1024)
        ve_o6 = 0
        gp_out = 0
        for t in range(NT):
            if t > 0:
                w.wait("gpsimd", s_ve, ve_o6)
            w.do("gpsimd", lambda e, t=t: e.indirect_dma_start(
                out=rowA, out_offset=None, in_=outbuf,
                in_offset=IndirectOffsetOnAxis(ap=physG[:, t:t + 1], axis=0)), (s_gp, 16))
            w.do("gpsimd", lambda e, t=t: e.indirect_dma_start(
                out=rowB, out_offset=None, in_=outbuf,
                in_offset=IndirectOffsetOnAxis(ap=physG[:, 64 + t:64 + t + 1], axis=0)),
                (s_gp, 16))
            gp_g = s_gp.n
            w.wait("sync", s_ve, ve_o6)
            w.do("sync", lambda e, t=t: e.dma_start(
                out=s6, in_=s_stage[t * 128:(t + 1) * 128, :]), (s_ld, 16))
            w.do("sync", lambda e, t=t: e.dma_start(
                out=x6, in_=x_full[t * 128:(t + 1) * 128, :]), (s_ld, 16))
            ld_sx = s_ld.n
            w.wait("scalar", s_gp, gp_g)
            S(lambda e, t=t: e.activation(t2b, rowB, ACT.Copy,
                                          scale=gbuf[:, 64 + t:64 + t + 1]))
            ac_t2 = s_ac.n
            w.wait("vector", s_gp, gp_g)
            w.wait("vector", s_ac, ac_t2)
            if t > 0:
                w.wait("vector", s_gp, gp_out)
            V(lambda e, t=t: e.scalar_tensor_tensor(
                moeb, rowA, gbuf[:, t:t + 1], t2b, op0=ALU.mult, op1=ALU.add))
            ve_m = s_ve.n
            w.wait("scalar", s_ve, ve_m)
            S(lambda e: e.activation(moeb, moeb, ACT.Silu))
            ac_sil = s_ac.n
            w.wait("vector", s_ac, ac_sil)
            w.wait("vector", s_ld, ld_sx)
            V(lambda e: e.tensor_tensor(o6, moeb, s6, op=ALU.mult))
            V(lambda e: e.tensor_tensor(o6, o6, x6, op=ALU.add))
            ve_o6 = s_ve.n
            w.wait("gpsimd", s_ve, ve_o6)
            w.do("gpsimd", lambda e, t=t: e.dma_start(
                out=out_d[t * 128:(t + 1) * 128, :], in_=o6), (s_gp, 16))
            gp_out = s_gp.n
        if dbg_arena is not None:
            w.do("gpsimd", lambda e: e.dma_start(out=dbg_arena, in_=AP_[:]), (s_gp, 16))
            w.do("gpsimd", lambda e: e.dma_start(out=dbg_i, in_=AI_[:]), (s_gp, 16))
        w.wait("gpsimd", s_gp, s_gp.n)

        w.emit(block)
    return nc


_NC_CACHE = None


def kernel(L0_attn_x, L0_attn_norm_w, L0_attn_Wq, L0_attn_Wk, L0_attn_Wv,
           L0_attn_Wo, L0_attn_gate, L0_attn_log_temperature,
           L0_ffn_Wg, L0_ffn_W_in, L0_ffn_W_out):
    global _NC_CACHE
    x = np.ascontiguousarray(np.asarray(L0_attn_x, np.float32).reshape(STOK, D))
    wg = np.asarray(L0_ffn_Wg, np.float32)
    wg_ext = np.ascontiguousarray(
        np.concatenate([wg, np.full((D, 1), 2.0 ** -10, np.float32)], axis=1))
    w_in = np.ascontiguousarray(np.asarray(L0_ffn_W_in, np.float32).reshape(E * D, 4096))
    w_out = np.ascontiguousarray(np.asarray(L0_ffn_W_out, np.float32).reshape(E * 2048, D))
    consts = {
        "norm_w": np.ascontiguousarray(np.asarray(L0_attn_norm_w, np.float32).reshape(1, D)),
        "wq": np.ascontiguousarray(np.asarray(L0_attn_Wq, np.float32)),
        "wk": np.ascontiguousarray(np.asarray(L0_attn_Wk, np.float32)),
        "wv": np.ascontiguousarray(np.asarray(L0_attn_Wv, np.float32)),
        "wo": np.ascontiguousarray(np.asarray(L0_attn_Wo, np.float32)),
        "gate_in": np.asarray(L0_attn_gate, np.float32).reshape(1, 1),
        "ltemp_in": np.asarray(L0_attn_log_temperature, np.float32).reshape(1, 1),
        "wg_ext": wg_ext,
        "w_in": w_in,
        "w_out": w_out,
        "c_ident": np.eye(128, dtype=np.float32),
        "c_tri": np.triu(np.ones((128, 128), np.float32), 1),
        "c_ones": np.ones((128, 128), np.float32),
        "c_iotar": np.ascontiguousarray(
            np.tile((7.0 - np.arange(8, dtype=np.float32)), (128, 4))),
        "c_maskd": np.ascontiguousarray(np.concatenate(
            [(128 * j + np.arange(128)[:, None] <= np.arange(512)[None, :])
             .astype(np.float32) for j in range(4)], axis=1)),
    }
    in_maps = []
    for c in range(NCORES):
        xr = np.ascontiguousarray(np.roll(x, -c * TPC, axis=0))
        gr = (c + np.arange(8)) % 8
        m = dict(consts)
        m["x_full"] = xr
        m["c_coremask"] = np.ascontiguousarray(
            np.repeat((gr < c).astype(np.float32), 16).reshape(1, 128))
        m["c_mymask"] = np.ascontiguousarray(
            np.repeat((np.arange(8) == 0).astype(np.float32), 16).reshape(1, 128))
        in_maps.append(m)
    if _NC_CACHE is None:
        _NC_CACHE = build_nc()
    res = run_bass_kernel_spmd(_NC_CACHE, in_maps, core_ids=list(range(NCORES)))
    outs = [res.results[c]["out"] for c in range(NCORES)]
    return np.concatenate(outs, axis=0).reshape(32, L, D)
